# version 11
# baseline (speedup 1.0000x reference)
import contextlib
import os
import numpy as np
import concourse.bass as bass
import concourse.mybir as mybir
from concourse.bass_utils import run_bass_kernel_spmd

F32 = mybir.dt.float32
BF16 = mybir.dt.bfloat16
Alu = mybir.AluOpType
Act = mybir.ActivationFunctionType

PI = float(np.pi)
TWO_PI = float(2 * np.pi)
HALF_PI = float(np.pi / 2)

D = 1024
T = 2048
NCT = 8
FD = 1024
FF = 4096
WPAD = 2048
PAR = 8
EPS_TOP = 1e-5
EPS_BLK = 1e-6
NG = float(8192 * 32)

NSLOTS_DMA = 8
_DMA_Q = ("qd", "sd", "gd")
SEMS = ("t", "v", "s", "g", "q", "cc") + tuple(
    f"{q}{i}" for q in _DMA_Q for i in range(NSLOTS_DMA))
ENG_OF = {"t": "t", "v": "v", "s": "s", "g": "g", "q": "q", "cc": "g"}
for _i in range(NSLOTS_DMA):
    ENG_OF[f"qd{_i}"] = "q"
    ENG_OF[f"sd{_i}"] = "s"
    ENG_OF[f"gd{_i}"] = "g"
DMA_SEMS = frozenset(f"{q}{i}" for q in _DMA_Q for i in range(NSLOTS_DMA))


class Buf:
    __slots__ = ("ap", "wtok", "rtoks", "name", "excl")

    def __init__(self, ap, name="", excl=False):
        self.ap = ap
        self.wtok = None
        self.rtoks = []
        self.name = name
        self.excl = excl


class Emitter:
    def __init__(self):
        self.prog = {e: [] for e in "tvsgq"}
        self.seq = {s: 0 for s in SEMS}
        self.needed = set()
        self.dma_rr = {q: 0 for q in _DMA_Q}
        self.dma_slot_buf = {f"{q}{i}": Buf(None, f"slot_{q}{i}")
                             for q in _DMA_Q for i in range(NSLOTS_DMA)}

    def op(self, sem, fn, reads=(), writes=(), wtok_update=True):
        eng = ENG_OF[sem]
        deps = set()
        for b in reads:
            if b.wtok is not None:
                deps.add(b.wtok)
            if b.excl:
                for t in b.rtoks:
                    if ENG_OF[t[0]] != eng:
                        deps.add(t)
        for b in writes:
            if b.wtok is not None:
                deps.add(b.wtok)
            deps.update(b.rtoks)
        tok = (sem, self.seq[sem])
        self.seq[sem] += 1
        deps.discard(tok)
        self.needed.update(deps)
        if sem in DMA_SEMS or sem == "cc":
            self.needed.add(tok)
        self.prog[eng].append((deps, fn, sem, tok))
        for b in writes:
            if wtok_update:
                b.wtok = tok
                b.rtoks = []
        for b in reads:
            if tok not in b.rtoks:
                b.rtoks.append(tok)
        return tok

    def wait_on(self, sem, toks):
        self.needed.update(toks)
        self.prog[ENG_OF[sem]].append((set(toks), None, sem, None))

    def play(self, sems):
        val_after = {}
        cnt = {s: 0 for s in SEMS}
        for e in "tvsgq":
            for deps, fn, sem, tok in self.prog[e]:
                if tok is not None and tok in self.needed and fn is not None:
                    cnt[sem] += 16 if sem in DMA_SEMS else 1
                if tok is not None:
                    val_after[tok] = cnt[sem]

        def make_runner(e):
            def run(engobj):
                watermark = {}
                for deps, fn, sem, tok in self.prog[e]:
                    for dd in sorted(deps):
                        ds, v = dd[0], val_after[dd]
                        if v > watermark.get(ds, 0):
                            engobj.wait_ge(sems[ds], v)
                            watermark[ds] = v
                    if fn is None:
                        continue
                    ins = fn(engobj)
                    if isinstance(ins, list):
                        ins = ins[-1]
                    if tok in self.needed:
                        ins.then_inc(sems[sem], 16 if sem in DMA_SEMS else 1)
            return run

        return {e: make_runner(e) for e in "tvsgq"}


def build_program(debug_outputs=False):
    nc = bass.Bass("TRN2", target_bir_lowering=False, num_devices=8)
    E = Emitter()
    ctx = contextlib.ExitStack()

    def din(name, shape, dt=F32):
        return Buf(nc.dram_tensor(name, list(shape), dt, kind="ExternalInput").ap(), name)

    d_xT = din("xT", [D, T])
    d_Wq = din("Wq_p", [8, 128, 1024])
    d_Wk = din("Wk_p", [8, 128, 1024])
    d_Wv = din("Wv_p", [8, 128, 1024])
    d_Wg = din("Wg_p", [8, 128, 1024])
    d_Wa = din("Wa_p", [16, 128, 1024])
    d_W1 = din("W1_p", [32, 128, 1024])
    d_Wo = din("Wo_r", [D, D])
    d_W2 = din("W2_r", [FF, D])
    d_WinT = din("WinT_r", [D, WPAD])
    d_WoutT = din("WoutT_r", [WPAD, D])
    d_gns = din("gn_s", [128, 8])
    d_gnb = din("gn_b", [128, 8])
    d_l1s = din("ln1_s", [128, 8])
    d_l1b = din("ln1_b", [128, 8])
    d_l2s = din("ln2_s", [128, 8])
    d_l2b = din("ln2_b", [128, 8])
    d_b1 = din("b1_pp", [128, 32])
    d_b2 = din("b2_pp", [128, 8])
    d_binrow = din("binrow", [1, WPAD])

    d_y = Buf(nc.dram_tensor("y", [D, T], F32, kind="ExternalOutput").ap(), "y")
    d_xn = Buf(nc.dram_tensor("xn_sc", [D, T], F32, kind="Internal").ap(), "xn_sc")
    d_xat = Buf(nc.dram_tensor("xat_sc", [D, T], F32, kind="Internal").ap(), "xat_sc")
    d_ffn = Buf(nc.dram_tensor("ffn_sc", [D, T], F32, kind="Internal").ap(), "ffn_sc")
    d_sg = Buf(nc.dram_tensor("sg_sc", [D, T], BF16, kind="Internal").ap(), "sg_sc")
    d_yr = Buf(nc.dram_tensor("yr_sc", [D, T], F32, kind="Internal").ap(), "yr_sc")
    d_o2 = Buf(nc.dram_tensor("o2_sc", [D, T], F32, kind="Internal").ap(), "o2_sc")
    d_ccin = Buf(nc.dram_tensor("cc_in", [128, 24], F32, kind="Internal").ap(), "cc_in")
    d_ccout = Buf(nc.dram_tensor("cc_out", [128, 24], F32, kind="Internal").ap(), "cc_out")
    dbg = {}
    if debug_outputs:
        for nm in ("dbg_xn", "dbg_yr", "dbg_z", "dbg_mix", "dbg_xat", "dbg_ffn", "dbg_o2"):
            dbg[nm] = Buf(nc.dram_tensor(nm, [D, T], F32, kind="ExternalOutput").ap(), nm)
        dbg["dbg_st"] = Buf(
            nc.dram_tensor("dbg_st", [128, 24], F32, kind="ExternalOutput").ap(), "dbg_st")

    def sbt(name, shape, dt=F32):
        h = ctx.enter_context(nc.sbuf_tensor(name, list(shape), dt))
        return Buf(h[:], name)

    big64_t = ctx.enter_context(nc.sbuf_tensor("big64", [128, NCT * T], F32))
    big32_t = ctx.enter_context(nc.sbuf_tensor("big32", [128, NCT * T], BF16))
    big64ap = big64_t[:]
    big32ap = big32_t[:]
    b64 = [Buf(big64ap[:, k * T:(k + 1) * T], f"b64_{k}") for k in range(NCT)]
    b32 = [Buf(big32ap[:, k * T:(k + 1) * T], f"b32_{k}") for k in range(NCT)]
    NSLOT = 14
    sl = [sbt(f"sl{i}", [128, FD]) for i in range(NSLOT)]
    wbuf_t = ctx.enter_context(nc.sbuf_tensor("wbuf", [128, 16384], BF16))
    wreg = [Buf(wbuf_t[:][:, i * 1024:(i + 1) * 1024], f"wreg{i}") for i in range(16)]
    r_mu = sbt("r_mu", [1, T])
    r_iv = sbt("r_iv", [1, T])
    onescol = sbt("onescol", [128, 1], BF16)
    onesrow = sbt("onesrow", [1, 128], BF16)
    onesrow_f = sbt("onesrow_f", [1, 128])
    binbf = sbt("binbf", [1, WPAD], BF16)
    cst = sbt("cst", [128, 6])
    v_gns = sbt("v_gns", [128, 8])
    v_gnb = sbt("v_gnb", [128, 8])
    v_l1s = sbt("v_l1s", [128, 8])
    v_l1b = sbt("v_l1b", [128, 8])
    v_l2s = sbt("v_l2s", [128, 8])
    v_l2b = sbt("v_l2b", [128, 8])
    v_b1 = sbt("v_b1", [128, 32])
    v_b2 = sbt("v_b2", [128, 8])
    st_yr = sbt("st_yr", [128, 16])
    st_yi = sbt("st_yi", [128, 16])
    st_q2 = sbt("st_q2", [128, 32])
    st_yr_c = [Buf(st_yr.ap[:, i:i + 1], f"styr{i}") for i in range(16)]
    st_yi_c = [Buf(st_yi.ap[:, i:i + 1], f"styi{i}") for i in range(16)]
    st_q2_c = [Buf(st_q2.ap[:, i:i + 1], f"stq2{i}") for i in range(32)]
    st_g = [sbt(f"st_g{i}", [128, 24]) for i in range(14)]
    gnA = sbt("gnA", [128, 8])
    gnB = sbt("gnB", [128, 8])
    carry = [sbt(f"carry{i}", [128, 1]) for i in range(3)]
    carryO = [sbt(f"carryO{i}", [128, 1]) for i in range(3)]
    slB = [Buf(big64ap[:, i * FD:(i + 1) * FD], f"slB{i}") for i in range(14)]
    pb = [Buf(ctx.enter_context(nc.psum_tensor(f"pb{i}", [128, 512], F32))[:], f"pb{i}",
              excl=True) for i in range(8)]
    idn = sbt("idn", [128, 128], BF16)

    def bc(i):
        return sl[i].ap.bitcast(BF16)

    def row_ap(buf, c):
        return buf.ap[0:1, c * 512:(c + 1) * 512]

    def vop(fn, reads=(), writes=()):
        return E.op("v", fn, reads, writes)

    def sop(fn, reads=(), writes=()):
        return E.op("s", fn, reads, writes)

    def top(fn, reads=(), writes=()):
        return E.op("t", fn, reads, writes)

    def pop(fn, reads=(), writes=()):
        return E.op("g", fn, reads, writes)

    def dma(q, fn, reads=(), writes=()):
        slot = E.dma_rr[q] % NSLOTS_DMA
        E.dma_rr[q] += 1
        sem = f"{q}{slot}"
        sbuf = E.dma_slot_buf[sem]
        return E.op(sem, fn, list(reads), list(writes) + [sbuf])

    def mm_acc(chunks, reads, writes):
        n = len(chunks)
        for i, f in enumerate(chunks):
            if i == 0:
                E.op("t", f, reads=reads, writes=writes, wtok_update=(n == 1))
            elif i == n - 1:
                E.op("t", f, reads=reads, writes=writes)
            else:
                E.op("t", f, reads=(), writes=())

    vop(lambda v: v.memset(onescol.ap, 1.0), writes=[onescol])
    vop(lambda v: v.memset(cst.ap[:, 0:1], EPS_TOP), writes=[cst])
    vop(lambda v: v.memset(cst.ap[:, 1:2], EPS_BLK), writes=[cst])
    vop(lambda v: v.memset(cst.ap[:, 2:3], 1e-38), writes=[cst])
    vop(lambda v: v.memset(cst.ap[:, 3:4], 0.5), writes=[cst])
    vop(lambda v: v.memset(cst.ap[:, 4:5], HALF_PI), writes=[cst])
    vop(lambda v: v.memset(cst.ap[:, 5:6], -0.5), writes=[cst])
    pop(lambda g: g.iota(sl[0].ap[:, 0:128], pattern=[[-1, 128]], base=0,
                         channel_multiplier=1,
                         allow_small_or_imprecise_dtypes=True), writes=[sl[0]])
    vop(lambda v: v.tensor_scalar(idn.ap, sl[0].ap[:, 0:128], 0.0, None,
                                  Alu.is_equal),
        reads=[sl[0]], writes=[idn])
    vop(lambda v: v.memset(onesrow.ap, 1.0), writes=[onesrow])
    vop(lambda v: v.memset(onesrow_f.ap, 1.0), writes=[onesrow_f])
    for buf, src in [(v_gns, d_gns), (v_gnb, d_gnb), (v_l1s, d_l1s), (v_l1b, d_l1b),
                     (v_l2s, d_l2s), (v_l2b, d_l2b), (v_b1, d_b1), (v_b2, d_b2)]:
        dma("qd", lambda q, b=buf, s=src: q.dma_start(out=b.ap, in_=s.ap),
            reads=[src], writes=[buf])
    dma("gd", lambda g: g.dma_start(out=binbf.ap, in_=d_binrow.ap),
        reads=[d_binrow], writes=[binbf])

    def ln_rows(src, eps_col):
        for k in range(NCT):
            cbuf, qbuf = sl[2], sl[3]
            sop(lambda s, k=k: s.copy(bc(2), src[k].ap),
                reads=[src[k]], writes=[cbuf])
            sop(lambda s, k=k: s.activation(bc(3), src[k].ap, Act.Square),
                reads=[src[k]], writes=[qbuf])
            for c in range(4):
                top(lambda t, k=k, c=c: t.matmul(
                    pb[c].ap[0:1, :], onescol.ap, bc(2)[:, c * 512:(c + 1) * 512],
                    start=(k == 0), stop=(k == NCT - 1)),
                    reads=[cbuf, onescol],
                    writes=([pb[c]] if k in (0, NCT - 1) else []))
                top(lambda t, k=k, c=c: t.matmul(
                    pb[4 + c].ap[0:1, :], onescol.ap, bc(3)[:, c * 512:(c + 1) * 512],
                    start=(k == 0), stop=(k == NCT - 1)),
                    reads=[qbuf, onescol],
                    writes=([pb[4 + c]] if k in (0, NCT - 1) else []))
        for c in range(4):
            vop(lambda v, c=c: v.tensor_scalar(row_ap(r_mu, c), pb[c].ap[0:1, :],
                                               1.0 / D, None, Alu.mult),
                reads=[pb[c]], writes=[r_mu])
            s2d = Buf(sl[2].ap[0:1, 0:512], "s2d")
            msq = Buf(sl[3].ap[0:1, 0:512], "msq")
            vrr = Buf(sl[4].ap[0:1, 0:512], "vrr")
            sdr = Buf(sl[2].ap[0:1, 512:1024], "sdr")
            vop(lambda v, c=c, s2d=s2d: v.tensor_scalar(s2d.ap, pb[4 + c].ap[0:1, :],
                                                        1.0 / D, None, Alu.mult),
                reads=[pb[4 + c]], writes=[s2d, sl[2]])
            vop(lambda v, c=c, msq=msq: v.tensor_tensor(msq.ap, row_ap(r_mu, c),
                                                        row_ap(r_mu, c), Alu.mult),
                reads=[r_mu], writes=[msq, sl[3]])
            vop(lambda v, s2d=s2d, msq=msq, vrr=vrr: v.tensor_tensor(
                vrr.ap, s2d.ap, msq.ap, Alu.subtract),
                reads=[s2d, msq, sl[2], sl[3]], writes=[vrr, sl[4]])
            sop(lambda s, c=c, vrr=vrr: s.activation(
                row_ap(r_iv, c), vrr.ap, Act.Sqrt,
                bias=cst.ap[0:1, eps_col:eps_col + 1]),
                reads=[vrr, sl[4], cst], writes=[r_iv])
        for c in range(4):
            top(lambda t, c=c: t.matmul(pb[c].ap, onesrow_f.ap, row_ap(r_mu, c),
                                        start=True, stop=True),
                reads=[r_mu, onesrow_f], writes=[pb[c]])
            top(lambda t, c=c: t.matmul(pb[4 + c].ap, onesrow_f.ap, row_ap(r_iv, c),
                                        start=True, stop=True),
                reads=[r_iv, onesrow_f], writes=[pb[4 + c]])
        for c in range(4):
            dst_m = sl[5 + c // 2]
            dst_i = sl[7 + c // 2]
            vop(lambda v, c=c, dst_m=dst_m: v.tensor_copy(
                dst_m.ap[:, (c % 2) * 512:(c % 2) * 512 + 512], pb[c].ap),
                reads=[pb[c]], writes=[dst_m])
            vop(lambda v, c=c, dst_i=dst_i: v.reciprocal(
                dst_i.ap[:, (c % 2) * 512:(c % 2) * 512 + 512], pb[4 + c].ap),
                reads=[pb[4 + c]], writes=[dst_i])
        return (sl[5], sl[6]), (sl[7], sl[8])

    for k in range(NCT):
        dma("qd", lambda q, k=k: q.dma_start(
            out=b64[k].ap, in_=d_xT.ap[k * 128:(k + 1) * 128, :]),
            reads=[d_xT], writes=[b64[k]])
    mu_h, iv_h = ln_rows(b64, 0)
    for k in range(NCT):
        for h in range(2):
            t0, xnf = sl[0], sl[1]
            vop(lambda v, k=k, h=h, t0=t0: v.tensor_tensor(
                t0.ap, b64[k].ap[:, h * FD:(h + 1) * FD],
                mu_h[h].ap, Alu.subtract), reads=[b64[k], mu_h[h]], writes=[t0])
            vop(lambda v, h=h, t0=t0, xnf=xnf: v.tensor_tensor(
                xnf.ap, t0.ap, iv_h[h].ap, Alu.mult),
                reads=[t0, iv_h[h]], writes=[xnf])
            sop(lambda s, k=k, h=h, xnf=xnf: s.copy(
                b32[k].ap[:, h * FD:(h + 1) * FD], xnf.ap),
                reads=[xnf], writes=[b32[k]])
            dma("qd", lambda q, k=k, h=h, xnf=xnf: q.dma_start(
                out=d_xn.ap[k * 128:(k + 1) * 128, h * FD:(h + 1) * FD], in_=xnf.ap),
                reads=[xnf], writes=[d_xn])
    if debug_outputs:
        for k in range(NCT):
            dma("qd", lambda q, k=k: q.dma_start(
                out=dbg["dbg_xn"].ap[k * 128:(k + 1) * 128, :],
                in_=d_xn.ap[k * 128:(k + 1) * 128, :]),
                reads=[d_xn], writes=[dbg["dbg_xn"]])

    PANELS = [("q", d_Wq, None), ("k", d_Wk, None), ("v", d_Wv, None),
              ("g", d_Wg, None), ("ar", d_Wa, 0), ("ai", d_Wa, 8)]

    def load_panels(ct):
        base = 0 if ct % 2 == 0 else 6
        regs = []
        for j, (_, dten, off) in enumerate(PANELS):
            idx = ct if off is None else off + ct
            r = wreg[base + j]
            dma("gd", lambda g, idx=idx, dten=dten, r=r: g.dma_start(
                out=r.ap, in_=dten.ap[idx]), reads=[dten], writes=[r])
            regs.append(r)
        return regs

    def b_chain(ct, regs):
        SLt = sl if ct % 2 == 0 else slB
        car = carry if ct % 2 == 0 else carryO
        grp = pb[0:4] if ct % 2 == 0 else pb[4:8]

        def bcx(i):
            return SLt[i].ap.bitcast(BF16)

        TH, GR, GI = SLt[10], SLt[11], SLt[12]
        u_s, q_s = SLt[0], SLt[1]
        for h in range(2):
            def rhs(k, c, h=h):
                return big32ap[:, k * T + h * FD + c * 512:
                               k * T + h * FD + (c + 1) * 512]

            def mixer_mm(pj, b0, b1, h=h):
                for c, bank in ((0, b0), (1, b1)):
                    chunks = []
                    for k in range(8):
                        lap = regs[pj].ap[:, k * 128:(k + 1) * 128]
                        rap = rhs(k, c)
                        chunks.append(lambda t, k=k, bank=bank, lap=lap,
                                      rap=rap: t.matmul(
                            bank.ap, lap, rap, start=(k == 0), stop=(k == 7)))
                    mm_acc(chunks, reads=[regs[pj]] + b32, writes=[bank])

            mixer_mm(1, grp[0], grp[1])
            mixer_mm(2, grp[2], grp[3])
            yield
            for c in range(2):
                sop(lambda s, c=c, va=grp[2 + c].ap: s.copy(
                    SLt[2].ap[:, c * 512:(c + 1) * 512], va),
                    reads=[grp[2 + c]], writes=[SLt[2]])
                yield
            for c in range(2):
                vop(lambda v, c=c, u_s=u_s, ka=grp[c].ap: v.tensor_tensor(
                    u_s.ap[:, c * 512:(c + 1) * 512], ka,
                    SLt[2].ap[:, c * 512:(c + 1) * 512], Alu.mult),
                    reads=[grp[c], SLt[2]], writes=[u_s])
                yield
            mixer_mm(4, grp[0], grp[1])
            mixer_mm(5, grp[2], grp[3])
            yield
            for c in range(2):
                sop(lambda s, c=c, aa=grp[c].ap: s.activation(
                    SLt[3].ap[:, c * 512:(c + 1) * 512], aa, Act.Square),
                    reads=[grp[c]], writes=[SLt[3]])
                yield
                sop(lambda s, c=c, aa=grp[2 + c].ap: s.activation(
                    SLt[4].ap[:, c * 512:(c + 1) * 512], aa, Act.Square),
                    reads=[grp[2 + c]], writes=[SLt[4]])
                yield
                sop(lambda s, c=c, aa=grp[c].ap: s.sign(
                    bcx(5)[:, c * 512:(c + 1) * 512], aa),
                    reads=[grp[c]], writes=[SLt[5]])
                yield
                sop(lambda s, c=c, aa=grp[2 + c].ap: s.sign(
                    bcx(5)[:, 1024 + c * 512:1024 + (c + 1) * 512], aa),
                    reads=[grp[2 + c]], writes=[SLt[5]])
                yield
            mixer_mm(0, grp[0], grp[1])
            mixer_mm(3, grp[2], grp[3])
            yield
            for c in range(2):
                sop(lambda s, c=c, aa=grp[2 + c].ap: s.activation(
                    bcx(13)[:, c * 512:(c + 1) * 512], aa, Act.Silu),
                    reads=[grp[2 + c]], writes=[SLt[13]])
                yield
            dma("sd", lambda s_, ct=ct, h=h: s_.dma_start(
                out=d_sg.ap[ct * 128:(ct + 1) * 128, h * FD:(h + 1) * FD],
                in_=bcx(13)[:, 0:FD]), reads=[SLt[13]], writes=[d_sg])
            yield

            vop(lambda v: v.tensor_tensor(SLt[2].ap, SLt[3].ap, SLt[4].ap,
                                          Alu.add),
                reads=[SLt[3], SLt[4]], writes=[SLt[2]])
            yield
            vop(lambda v: v.tensor_tensor(bcx(6)[:, 0:1024], SLt[4].ap,
                                          SLt[3].ap, Alu.is_gt),
                reads=[SLt[3], SLt[4]], writes=[SLt[6]])
            yield
            sop(lambda s: s.activation(SLt[1].ap, SLt[3].ap, Act.Ln,
                                       bias=cst.ap[:, 2:3]),
                reads=[SLt[3], cst], writes=[SLt[1]])
            yield
            sop(lambda s: s.activation(SLt[3].ap, SLt[4].ap, Act.Ln,
                                       bias=cst.ap[:, 2:3]),
                reads=[SLt[4], cst], writes=[SLt[3]])
            yield
            vop(lambda v: v.tensor_tensor(SLt[4].ap, SLt[1].ap, SLt[3].ap,
                                          Alu.subtract),
                reads=[SLt[1], SLt[3]], writes=[SLt[4]])
            yield
            sop(lambda s: s.activation(SLt[1].ap, SLt[4].ap, Act.Abs),
                reads=[SLt[4]], writes=[SLt[1]])
            yield
            sop(lambda s: s.activation(SLt[3].ap, SLt[1].ap, Act.Exp,
                                       scale=cst.ap[:, 5:6]),
                reads=[SLt[1], cst], writes=[SLt[3]])
            yield
            sop(lambda s: s.activation(SLt[4].ap, SLt[2].ap, Act.Ln,
                                       bias=cst.ap[:, 2:3]),
                reads=[SLt[2], cst], writes=[SLt[4]])
            yield
            sop(lambda s: s.activation(SLt[2].ap, SLt[4].ap, Act.Exp,
                                       scale=cst.ap[:, 3:4]),
                reads=[SLt[4], cst], writes=[SLt[2]])
            yield
            sop(lambda s: s.activation(SLt[7].ap, SLt[2].ap, Act.Sigmoid),
                reads=[SLt[2]], writes=[SLt[7]])
            yield
            sop(lambda s: s.activation(SLt[1].ap, SLt[3].ap, Act.Arctan),
                reads=[SLt[3]], writes=[SLt[1]])
            yield
            pop(lambda g: g.tensor_scalar(SLt[2].ap, SLt[1].ap, 1.0, -HALF_PI,
                                          Alu.mult, Alu.add),
                reads=[SLt[1]], writes=[SLt[2]])
            yield
            pop(lambda g: g.tensor_scalar(SLt[3].ap, SLt[1].ap, -2.0, HALF_PI,
                                          Alu.mult, Alu.add),
                reads=[SLt[1]], writes=[SLt[3]])
            yield
            pop(lambda g: g.tensor_tensor(bcx(6)[:, 1024:2048],
                                          bcx(5)[:, 0:1024],
                                          bcx(5)[:, 1024:2048], Alu.mult),
                reads=[SLt[5]], writes=[SLt[6]])
            yield
            vop(lambda v: v.tensor_tensor(SLt[4].ap, bcx(6)[:, 0:1024],
                                          SLt[3].ap, Alu.mult),
                reads=[SLt[6], SLt[3]], writes=[SLt[4]])
            yield
            vop(lambda v: v.tensor_tensor(SLt[3].ap, SLt[4].ap, SLt[2].ap,
                                          Alu.add),
                reads=[SLt[4], SLt[2]], writes=[SLt[3]])
            yield
            vop(lambda v: v.tensor_tensor(SLt[2].ap, bcx(6)[:, 1024:2048],
                                          SLt[3].ap, Alu.mult),
                reads=[SLt[6], SLt[3]], writes=[SLt[2]])
            yield
            vop(lambda v: v.scalar_tensor_tensor(SLt[4].ap,
                                                 bcx(5)[:, 1024:2048],
                                                 HALF_PI, SLt[2].ap,
                                                 Alu.mult, Alu.add),
                reads=[SLt[5], SLt[2]], writes=[SLt[4]])
            yield

            if h == 0:
                vop(lambda v: v.tensor_tensor_scan(TH.ap, SLt[4].ap, SLt[4].ap,
                                                   0.0, Alu.add, Alu.bypass),
                    reads=[SLt[4]], writes=[TH])
                yield
                vop(lambda v: v.tensor_copy(car[0].ap, TH.ap[:, FD - 1:FD]),
                    reads=[TH], writes=[car[0]])
                yield
            else:
                vop(lambda v: v.tensor_tensor_scan(TH.ap, SLt[4].ap, SLt[4].ap,
                                                   car[0].ap[:, 0:1], Alu.add,
                                                   Alu.bypass),
                    reads=[SLt[4], car[0]], writes=[TH])
                yield

            MAGIC = 12582912.0
            pop(lambda g: g.tensor_scalar(SLt[3].ap, TH.ap, 1.0 / TWO_PI, MAGIC,
                                          Alu.mult, Alu.add),
                reads=[TH], writes=[SLt[3]])
            yield
            pop(lambda g: g.tensor_scalar(SLt[2].ap, TH.ap, 1.0 / TWO_PI, 0.25,
                                          Alu.mult, Alu.add),
                reads=[TH], writes=[SLt[2]])
            yield
            pop(lambda g: g.tensor_scalar(SLt[4].ap, SLt[2].ap, 1.0, MAGIC,
                                          Alu.mult, Alu.add),
                reads=[SLt[2]], writes=[SLt[4]])
            yield
            pop(lambda g: g.tensor_scalar(SLt[2].ap, SLt[3].ap, 1.0, -MAGIC,
                                          Alu.mult, Alu.add),
                reads=[SLt[3]], writes=[SLt[2]])
            yield
            pop(lambda g: g.tensor_scalar(SLt[3].ap, SLt[4].ap, 1.0, -MAGIC,
                                          Alu.mult, Alu.add),
                reads=[SLt[4]], writes=[SLt[3]])
            yield
            vop(lambda v: v.scalar_tensor_tensor(SLt[4].ap, SLt[2].ap, -TWO_PI,
                                                 TH.ap, Alu.mult, Alu.add),
                reads=[SLt[2], TH], writes=[SLt[4]])
            yield
            vop(lambda v: v.tensor_scalar(SLt[2].ap, SLt[4].ap, -PI, PI,
                                          Alu.max, Alu.min),
                reads=[SLt[4]], writes=[SLt[2]])
            yield
            vop(lambda v: v.scalar_tensor_tensor(SLt[4].ap, SLt[3].ap, -TWO_PI,
                                                 TH.ap, Alu.mult, Alu.add),
                reads=[SLt[3], TH], writes=[SLt[4]])
            yield
            vop(lambda v: v.tensor_scalar(SLt[3].ap, SLt[4].ap,
                                          -3.0 * HALF_PI, HALF_PI,
                                          Alu.max, Alu.min),
                reads=[SLt[4]], writes=[SLt[3]])
            yield
            sop(lambda s: s.activation(SLt[8].ap, SLt[2].ap, Act.Sin),
                reads=[SLt[2]], writes=[SLt[8]])
            yield
            sop(lambda s: s.activation(SLt[9].ap, SLt[3].ap, Act.Sin,
                                       bias=cst.ap[:, 4:5]),
                reads=[SLt[3], cst], writes=[SLt[9]])
            yield

            vop(lambda v, u_s=u_s: v.tensor_tensor(SLt[2].ap, u_s.ap,
                                                   SLt[9].ap, Alu.mult),
                reads=[u_s, SLt[9]], writes=[SLt[2]])
            yield
            vop(lambda v, u_s=u_s: v.scalar_tensor_tensor(
                SLt[3].ap, u_s.ap, -1.0, SLt[8].ap, Alu.mult, Alu.mult),
                reads=[u_s, SLt[8]], writes=[SLt[3]])
            yield
            if h == 0:
                vop(lambda v: v.tensor_tensor_scan(GR.ap, SLt[7].ap, SLt[2].ap,
                                                   0.0, Alu.mult, Alu.add),
                    reads=[SLt[7], SLt[2]], writes=[GR])
                yield
                vop(lambda v: v.tensor_tensor_scan(GI.ap, SLt[7].ap, SLt[3].ap,
                                                   0.0, Alu.mult, Alu.add),
                    reads=[SLt[7], SLt[3]], writes=[GI])
                yield
                vop(lambda v: v.tensor_copy(car[1].ap, GR.ap[:, FD - 1:FD]),
                    reads=[GR], writes=[car[1]])
                yield
                vop(lambda v: v.tensor_copy(car[2].ap, GI.ap[:, FD - 1:FD]),
                    reads=[GI], writes=[car[2]])
                yield
            else:
                vop(lambda v: v.tensor_tensor_scan(GR.ap, SLt[7].ap, SLt[2].ap,
                                                   car[1].ap[:, 0:1], Alu.mult,
                                                   Alu.add),
                    reads=[SLt[7], SLt[2], car[1]], writes=[GR])
                yield
                vop(lambda v: v.tensor_tensor_scan(GI.ap, SLt[7].ap, SLt[3].ap,
                                                   car[2].ap[:, 0:1], Alu.mult,
                                                   Alu.add),
                    reads=[SLt[7], SLt[3], car[2]], writes=[GI])
                yield

            for c in range(2):
                sop(lambda s, c=c, q_s=q_s, aa=grp[c].ap: s.copy(
                    q_s.ap[:, c * 512:(c + 1) * 512], aa),
                    reads=[grp[c]], writes=[q_s])
                yield

            ui = 2 * ct + h
            vop(lambda v, q_s=q_s: v.tensor_tensor(SLt[2].ap, q_s.ap,
                                                   SLt[9].ap, Alu.mult),
                reads=[q_s, SLt[9]], writes=[SLt[2]])
            yield
            vop(lambda v, q_s=q_s: v.scalar_tensor_tensor(
                SLt[3].ap, q_s.ap, -1.0, SLt[8].ap, Alu.mult, Alu.mult),
                reads=[q_s, SLt[8]], writes=[SLt[3]])
            yield
            vop(lambda v: v.tensor_tensor(SLt[4].ap, SLt[2].ap, GR.ap,
                                          Alu.mult),
                reads=[SLt[2], GR], writes=[SLt[4]])
            yield
            vop(lambda v: v.tensor_tensor(SLt[5].ap, SLt[3].ap, GI.ap,
                                          Alu.mult),
                reads=[SLt[3], GI], writes=[SLt[5]])
            yield
            vop(lambda v, ui=ui: v.scalar_tensor_tensor(
                SLt[0].ap, SLt[4].ap, 1.0, SLt[5].ap, Alu.mult, Alu.add,
                accum_out=st_yr.ap[:, ui:ui + 1]),
                reads=[SLt[4], SLt[5]], writes=[SLt[0], st_yr_c[ui]])
            yield
            dma("qd", lambda q, ct=ct, h=h, u_s=u_s: q.dma_start(
                out=d_yr.ap[ct * 128:(ct + 1) * 128, h * FD:(h + 1) * FD],
                in_=u_s.ap), reads=[u_s], writes=[d_yr])
            yield
            vop(lambda v: v.tensor_tensor(SLt[4].ap, SLt[2].ap, GI.ap,
                                          Alu.mult),
                reads=[SLt[2], GI], writes=[SLt[4]])
            yield
            vop(lambda v: v.tensor_tensor(SLt[5].ap, SLt[3].ap, GR.ap,
                                          Alu.mult),
                reads=[SLt[3], GR], writes=[SLt[5]])
            yield
            vop(lambda v, ui=ui: v.scalar_tensor_tensor(
                SLt[6].ap, SLt[5].ap, -1.0, SLt[4].ap, Alu.mult, Alu.add,
                accum_out=st_yi.ap[:, ui:ui + 1]),
                reads=[SLt[5], SLt[4]], writes=[SLt[6], st_yi_c[ui]])
            yield
            sop(lambda s, ui=ui, u_s=u_s: s.activation(
                SLt[2].ap, u_s.ap, Act.Square,
                accum_out=st_q2.ap[:, 2 * ui:2 * ui + 1]),
                reads=[u_s], writes=[SLt[2], st_q2_c[2 * ui]])
            yield
            sop(lambda s, ui=ui: s.activation(
                SLt[3].ap, SLt[6].ap, Act.Square,
                accum_out=st_q2.ap[:, 2 * ui + 1:2 * ui + 2]),
                reads=[SLt[6]], writes=[SLt[3], st_q2_c[2 * ui + 1]])
            yield

    regsA = load_panels(0)
    regsB = load_panels(1)
    for p in range(4):
        gens = [b_chain(2 * p, regsA), b_chain(2 * p + 1, regsB)]
        alive = [True, True]
        while any(alive):
            for gi_ in range(2):
                if alive[gi_]:
                    try:
                        next(gens[gi_])
                    except StopIteration:
                        alive[gi_] = False
        if p + 1 < 4:
            regsA = load_panels(2 * p + 2)
            regsB = load_panels(2 * p + 3)

    if debug_outputs:
        for k in range(NCT):
            tb = sl[0]
            dma("qd", lambda q, k=k, tb=tb: q.dma_start(
                out=tb.ap, in_=d_yr.ap[k * 128:(k + 1) * 128, 0:FD]),
                reads=[d_yr], writes=[tb])
            dma("qd", lambda q, k=k, tb=tb: q.dma_start(
                out=dbg["dbg_yr"].ap[k * 128:(k + 1) * 128, 0:FD], in_=tb.ap),
                reads=[tb], writes=[dbg["dbg_yr"]])
            tb2 = sl[1]
            dma("qd", lambda q, k=k, tb2=tb2: q.dma_start(
                out=tb2.ap, in_=d_yr.ap[k * 128:(k + 1) * 128, FD:T]),
                reads=[d_yr], writes=[tb2])
            dma("qd", lambda q, k=k, tb2=tb2: q.dma_start(
                out=dbg["dbg_yr"].ap[k * 128:(k + 1) * 128, FD:T], in_=tb2.ap),
                reads=[tb2], writes=[dbg["dbg_yr"]])

    for k in range(8):
        for half in range(2):
            r = wreg[2 * k + half]
            dma("gd", lambda g, k=k, half=half, r=r: g.dma_start(
                out=r.ap, in_=d_WinT.ap[k * 128:(k + 1) * 128,
                                        half * 1024:(half + 1) * 1024]),
                reads=[d_WinT], writes=[r])
    woutb = [sl[i] for i in range(8)]
    for wt in range(16):
        s_i, off = wt // 2, (wt % 2) * 1024
        dma("gd", lambda g, wt=wt, s_i=s_i, off=off: g.dma_start(
            out=bc(s_i)[:, off:off + 1024],
            in_=d_WoutT.ap[wt * 128:(wt + 1) * 128, :]),
            reads=[d_WoutT], writes=[sl[s_i]])

    stA = st_g[0]
    for ct in range(NCT):
        vop(lambda v, ct=ct: v.tensor_tensor(
            stA.ap[:, ct:ct + 1], st_yr.ap[:, 2 * ct:2 * ct + 1],
            st_yr.ap[:, 2 * ct + 1:2 * ct + 2], Alu.add),
            reads=[st_yr_c[2 * ct], st_yr_c[2 * ct + 1]], writes=[stA])
        vop(lambda v, ct=ct: v.tensor_tensor(
            stA.ap[:, 8 + ct:9 + ct], st_yi.ap[:, 2 * ct:2 * ct + 1],
            st_yi.ap[:, 2 * ct + 1:2 * ct + 2], Alu.add),
            reads=[st_yi_c[2 * ct], st_yi_c[2 * ct + 1]], writes=[stA])
        vop(lambda v, ct=ct: v.tensor_tensor(
            st_g[1].ap[:, ct:ct + 1], st_q2.ap[:, 4 * ct:4 * ct + 1],
            st_q2.ap[:, 4 * ct + 1:4 * ct + 2], Alu.add),
            reads=[st_q2_c[4 * ct], st_q2_c[4 * ct + 1]], writes=[st_g[1]])
        vop(lambda v, ct=ct: v.tensor_tensor(
            st_g[2].ap[:, ct:ct + 1], st_q2.ap[:, 4 * ct + 2:4 * ct + 3],
            st_q2.ap[:, 4 * ct + 3:4 * ct + 4], Alu.add),
            reads=[st_q2_c[4 * ct + 2], st_q2_c[4 * ct + 3]], writes=[st_g[2]])
        vop(lambda v, ct=ct: v.tensor_tensor(
            stA.ap[:, 16 + ct:17 + ct], st_g[1].ap[:, ct:ct + 1],
            st_g[2].ap[:, ct:ct + 1], Alu.add), reads=[st_g[1], st_g[2]], writes=[stA])
    cur = stA
    nxt = 3
    for step in (16, 8, 4, 2, 1):
        mask = [(i + step) % 32 for i in range(32)]
        shb, sm = st_g[nxt], st_g[nxt + 1]
        vop(lambda v, cur=cur, shb=shb, mask=mask: v.stream_shuffle(shb.ap, cur.ap, mask),
            reads=[cur], writes=[shb])
        vop(lambda v, cur=cur, shb=shb, sm=sm: v.tensor_tensor(
            sm.ap, cur.ap, shb.ap, Alu.add), reads=[cur, shb], writes=[sm])
        cur = sm
        nxt += 2
    dma("qd", lambda q, cur=cur: q.dma_start(out=d_ccin.ap, in_=cur.ap),
        reads=[cur], writes=[d_ccin])
    E.op("cc", lambda g: g.collective_compute(
        "AllReduce", Alu.add, replica_groups=[[0, 1, 2, 3], [4, 5, 6, 7]],
        ins=[d_ccin.ap], outs=[d_ccout.ap]), reads=[d_ccin], writes=[d_ccout])

    def g1(tt):
        for c in range(4):
            chunks = []
            for k in range(8):
                wr_i, woff = 2 * k + c // 2, (c % 2) * 512
                chunks.append(lambda t, k=k, tt=tt, ba=pb[c].ap, wr_i=wr_i,
                              woff=woff: t.matmul(
                    ba, big32ap[:, k * T + tt * 128:k * T + (tt + 1) * 128],
                    wreg[wr_i].ap[:, woff:woff + 512], start=(k == 0), stop=False))
            chunks.append(lambda t, c=c, ba=pb[c].ap: t.matmul(
                ba, onesrow.ap, binbf.ap[0:1, c * 512:(c + 1) * 512],
                start=False, stop=True))
            mm_acc(chunks, reads=b32 + [onesrow, binbf] + wreg, writes=[pb[c]])
        for c in range(4):
            vop(lambda v, c=c, aa=pb[c].ap: v.tensor_scalar(
                bc(8)[:, c * 512:(c + 1) * 512], aa, 0.0, None, Alu.is_gt),
                reads=[pb[c]], writes=[sl[8]])
            sop(lambda s, c=c, aa=pb[c].ap: s.activation(
                bc(11)[:, c * 512:(c + 1) * 512], aa, Act.Silu),
                reads=[pb[c]], writes=[sl[11]])

    def g_masks(tt):
        vop(lambda v: v.memset(bc(10), 0.0), writes=[sl[10]])
        dmr = bc(10).rearrange("p (r n) -> p r n", r=PAR)
        der = bc(8).rearrange("p (r n) -> p r n", r=PAR)
        vop(lambda v, dmr=dmr: v.memset(dmr[:, :, 0:1], 1.0), writes=[sl[10]])
        for lv in range(7):
            bp, bchld, npar = 2 ** lv - 1, 2 ** (lv + 1) - 1, 2 ** lv
            vop(lambda v, dmr=dmr, der=der, bp=bp, bchld=bchld, npar=npar:
                v.tensor_mul(
                dmr[:, :, bchld + 1:bchld + 2 * npar:2], dmr[:, :, bp:bp + npar],
                der[:, :, bp:bp + npar]), reads=[sl[10], sl[8]], writes=[sl[10]])
            vop(lambda v, dmr=dmr, bp=bp, bchld=bchld, npar=npar:
                v.tensor_tensor(
                dmr[:, :, bchld:bchld + 2 * npar:2], dmr[:, :, bp:bp + npar],
                dmr[:, :, bchld + 1:bchld + 2 * npar:2], Alu.subtract),
                reads=[sl[10]], writes=[sl[10]])
        vop(lambda v: v.tensor_mul(bc(8), bc(11), bc(10)),
            reads=[sl[11], sl[10], sl[8]], writes=[sl[8]])
        for grp4 in range(4):
            tb_ = pb[6 + grp4 % 2]
            chunks = []
            for j4 in range(4):
                wt = grp4 * 4 + j4
                chunks.append(lambda t, j4=j4, wt=wt, tb_=tb_: t.matmul(
                    tb_.ap[:, j4 * 128:(j4 + 1) * 128],
                    bc(8)[:, wt * 128:(wt + 1) * 128], idn.ap,
                    start=(j4 == 0), stop=(j4 == 3)))
            mm_acc(chunks, reads=[sl[8], idn], writes=[tb_])
            for j4 in range(4):
                wt = grp4 * 4 + j4
                s_i = 12 + wt // 8
                off = (wt % 8) * 256 + (tt % 2) * 128
                sop(lambda s, j4=j4, s_i=s_i, off=off, tb_=tb_: s.copy(
                    bc(s_i)[:, off:off + 128],
                    tb_.ap[:, j4 * 128:(j4 + 1) * 128]),
                    reads=[tb_], writes=[sl[s_i]])

    def mtT(wt):
        s_i = 12 + wt // 8
        off = (wt % 8) * 256
        return bc(s_i)[:, off:off + 256]

    def g2(tt0):
        for sweep in range(2):
            for dt4 in range(4):
                dt = sweep * 4 + dt4
                bank = pb[4 + dt4 // 2]
                coff = (dt4 % 2) * 256
                chunks = []
                for wt in range(16):
                    s_i, woff = wt // 2, (wt % 2) * 1024
                    st_f = (wt == 0 and dt4 % 2 == 0)
                    sp_f = (wt == 15 and dt4 % 2 == 1)
                    chunks.append(lambda t, wt=wt, dt=dt, bank=bank, coff=coff,
                                  s_i=s_i, woff=woff, st_f=st_f, sp_f=sp_f:
                                  t.matmul(
                        bank.ap[:, coff:coff + 256],
                        bc(s_i)[:, woff + dt * 128:woff + (dt + 1) * 128],
                        mtT(wt), start=st_f, stop=sp_f))
                mm_acc(chunks, reads=[sl[12], sl[13]] + woutb, writes=[bank])
            for b_i in range(2):
                vop(lambda v, b_i=b_i: v.tensor_copy(
                    sl[9].ap[:, b_i * 512:(b_i + 1) * 512], pb[4 + b_i].ap),
                    reads=[pb[4 + b_i]], writes=[sl[9]])
            dma("qd", lambda q, tt0=tt0, sweep=sweep: q.dma_start(
                out=d_ffn.ap.rearrange("(a p) t -> p a t", p=128)[
                    :, sweep * 4:(sweep + 1) * 4, tt0 * 128:(tt0 + 2) * 128],
                in_=sl[9].ap.rearrange("p (a t) -> p a t", a=4)),
                reads=[sl[9]], writes=[d_ffn])

    g1(0)
    for tt in range(16):
        g_masks(tt)
        if tt + 1 < 16:
            g1(tt + 1)
        if tt % 2 == 1:
            g2(tt - 1)
    if debug_outputs:
        for k in range(NCT):
            for h in range(2):
                tb = sl[9]
                dma("qd", lambda q, k=k, h=h, tb=tb: q.dma_start(
                    out=tb.ap, in_=d_ffn.ap[k * 128:(k + 1) * 128, h * FD:(h + 1) * FD]),
                    reads=[d_ffn], writes=[tb])
                dma("qd", lambda q, k=k, h=h, tb=tb: q.dma_start(
                    out=dbg["dbg_ffn"].ap[k * 128:(k + 1) * 128, h * FD:(h + 1) * FD],
                    in_=tb.ap), reads=[tb], writes=[dbg["dbg_ffn"]])

    stG = st_g[1]
    dma("qd", lambda q: q.dma_start(out=stG.ap, in_=d_ccout.ap),
        reads=[d_ccout], writes=[stG])
    if debug_outputs:
        dma("qd", lambda q: q.dma_start(out=dbg["dbg_st"].ap, in_=stG.ap),
            reads=[stG], writes=[dbg["dbg_st"]])
    mrg = st_g[2]
    for a, b_ in ((0, 8), (8, 16), (16, 24)):
        vop(lambda v, a=a, b_=b_: v.tensor_scalar(mrg.ap[:, a:b_], stG.ap[:, a:b_],
                                                  1.0 / NG, None, Alu.mult),
            reads=[stG], writes=[mrg])
    sqm = st_g[3]
    vop(lambda v: v.tensor_tensor(sqm.ap[:, 0:8], mrg.ap[:, 0:8], mrg.ap[:, 0:8], Alu.mult),
        reads=[mrg], writes=[sqm])
    vop(lambda v: v.tensor_tensor(sqm.ap[:, 8:16], mrg.ap[:, 8:16], mrg.ap[:, 8:16],
                                  Alu.mult), reads=[mrg], writes=[sqm])
    vop(lambda v: v.tensor_tensor(st_g[4].ap[:, 0:8], mrg.ap[:, 16:24], sqm.ap[:, 0:8],
                                  Alu.subtract), reads=[mrg, sqm], writes=[st_g[4]])
    vop(lambda v: v.tensor_tensor(st_g[5].ap[:, 0:8], st_g[4].ap[:, 0:8], sqm.ap[:, 8:16],
                                  Alu.subtract), reads=[st_g[4], sqm], writes=[st_g[5]])
    sop(lambda s: s.activation(st_g[6].ap[:, 0:8], st_g[5].ap[:, 0:8], Act.Sqrt,
                               bias=cst.ap[:, 1:2]), reads=[st_g[5], cst], writes=[st_g[6]])
    vop(lambda v: v.reciprocal(st_g[7].ap[:, 0:8], st_g[6].ap[:, 0:8]),
        reads=[st_g[6]], writes=[st_g[7]])
    vop(lambda v: v.tensor_tensor(gnA.ap, st_g[7].ap[:, 0:8], v_gns.ap, Alu.mult),
        reads=[st_g[7], v_gns], writes=[gnA])
    vop(lambda v: v.tensor_tensor(st_g[8].ap[:, 0:8], mrg.ap[:, 0:8], gnA.ap, Alu.mult),
        reads=[mrg, gnA], writes=[st_g[8]])
    vop(lambda v: v.tensor_tensor(gnB.ap, v_gnb.ap, st_g[8].ap[:, 0:8], Alu.subtract),
        reads=[v_gnb, st_g[8]], writes=[gnB])

    for k in range(8):
        dma("gd", lambda g, k=k: g.dma_start(
            out=wreg[k].ap, in_=d_Wo.ap[k * 128:(k + 1) * 128, :]),
            reads=[d_Wo], writes=[wreg[k]])
    for ct in range(NCT):
        dma("qd", lambda q, ct=ct: q.dma_start(
            out=bc(13), in_=d_sg.ap[ct * 128:(ct + 1) * 128, :]),
            reads=[d_sg], writes=[sl[13]])
        for h in range(2):
            SS = sl if ct % 2 == 0 else slB
            yn = SS[0] if h == 0 else SS[1]
            yrl = SS[2] if h == 0 else SS[3]
            dma("qd", lambda q, ct=ct, h=h, yrl=yrl: q.dma_start(
                out=yrl.ap, in_=d_yr.ap[ct * 128:(ct + 1) * 128,
                                        h * FD:(h + 1) * FD]),
                reads=[d_yr], writes=[yrl])
            pop(lambda g, ct=ct, h=h, yn=yn, yrl=yrl: g.tensor_scalar(
                yn.ap, yrl.ap,
                gnA.ap[:, ct:ct + 1], gnB.ap[:, ct:ct + 1], Alu.mult, Alu.add),
                reads=[yrl, gnA, gnB], writes=[yn])
            pop(lambda g, ct=ct, h=h, yn=yn: g.tensor_tensor(
                b32[ct].ap[:, h * FD:(h + 1) * FD],
                yn.ap, bc(13)[:, h * FD:(h + 1) * FD], Alu.mult),
                reads=[yn, sl[13]], writes=[b32[ct]])
    if debug_outputs:
        for k in range(NCT):
            for h in range(2):
                tb = sl[0]
                vop(lambda v, k=k, h=h, tb=tb: v.tensor_copy(
                    tb.ap.bitcast(BF16)[:, 0:FD],
                    b32[k].ap[:, h * FD:(h + 1) * FD]),
                    reads=[b32[k]], writes=[tb])
                tb2 = sl[1]
                vop(lambda v, tb=tb, tb2=tb2: v.tensor_copy(
                    tb2.ap, tb.ap.bitcast(BF16)[:, 0:FD]),
                    reads=[tb], writes=[tb2])
                dma("qd", lambda q, k=k, h=h, tb2=tb2: q.dma_start(
                    out=dbg["dbg_z"].ap[k * 128:(k + 1) * 128, h * FD:(h + 1) * FD],
                    in_=tb2.ap), reads=[tb2], writes=[dbg["dbg_z"]])
    for dt in range(8):
        for tc in range(4):
            bank = pb[(dt * 4 + tc) % 8]
            chunks = []
            for k in range(8):
                chunks.append(lambda t, dt=dt, tc=tc, k=k, bank=bank: t.matmul(
                    bank.ap, wreg[k].ap[:, dt * 128:(dt + 1) * 128],
                    big32ap[:, k * T + tc * 512:k * T + (tc + 1) * 512],
                    start=(k == 0), stop=(k == 7)))
            mm_acc(chunks, reads=b32 + wreg[:8], writes=[bank])
            xnc = sl[2 + (dt * 4 + tc) % 2]
            dma("qd", lambda q, dt=dt, tc=tc, xnc=xnc: q.dma_start(
                out=xnc.ap[:, 0:512],
                in_=d_xn.ap[dt * 128:(dt + 1) * 128, tc * 512:(tc + 1) * 512]),
                reads=[d_xn], writes=[xnc])
            vop(lambda v, dt=dt, tc=tc, bank=bank, xnc=xnc: v.tensor_tensor(
                b64[dt].ap[:, tc * 512:(tc + 1) * 512],
                bank.ap, xnc.ap[:, 0:512], Alu.add),
                reads=[bank, xnc], writes=[b64[dt]])
    if debug_outputs:
        for k in range(NCT):
            dma("qd", lambda q, k=k: q.dma_start(
                out=dbg["dbg_mix"].ap[k * 128:(k + 1) * 128, :],
                in_=b64[k].ap), reads=[b64[k]], writes=[dbg["dbg_mix"]])
    mu_h, iv_h = ln_rows(b64, 1)
    for k in range(NCT):
        for h in range(2):
            t0, t1_, xa = sl[0], sl[1], sl[9]
            vop(lambda v, k=k, h=h, t0=t0: v.tensor_tensor(
                t0.ap, b64[k].ap[:, h * FD:(h + 1) * FD],
                mu_h[h].ap, Alu.subtract), reads=[b64[k], mu_h[h]], writes=[t0])
            vop(lambda v, h=h, t0=t0, t1_=t1_: v.tensor_tensor(
                t1_.ap, t0.ap, iv_h[h].ap, Alu.mult),
                reads=[t0, iv_h[h]], writes=[t1_])
            pop(lambda g, k=k, t1_=t1_, xa=xa: g.tensor_scalar(
                xa.ap, t1_.ap, v_l1s.ap[:, k:k + 1], v_l1b.ap[:, k:k + 1],
                Alu.mult, Alu.add), reads=[t1_, v_l1s, v_l1b], writes=[xa])
            sop(lambda s, k=k, h=h, xa=xa: s.copy(
                b32[k].ap[:, h * FD:(h + 1) * FD], xa.ap),
                reads=[xa], writes=[b32[k]])
            dma("qd", lambda q, k=k, h=h, xa=xa: q.dma_start(
                out=d_xat.ap[k * 128:(k + 1) * 128, h * FD:(h + 1) * FD], in_=xa.ap),
                reads=[xa], writes=[d_xat])
    if debug_outputs:
        for k in range(NCT):
            for h in range(2):
                tb = sl[0]
                dma("qd", lambda q, k=k, h=h, tb=tb: q.dma_start(
                    out=tb.ap, in_=d_xat.ap[k * 128:(k + 1) * 128, h * FD:(h + 1) * FD]),
                    reads=[d_xat], writes=[tb])
                dma("qd", lambda q, k=k, h=h, tb=tb: q.dma_start(
                    out=dbg["dbg_xat"].ap[k * 128:(k + 1) * 128, h * FD:(h + 1) * FD],
                    in_=tb.ap), reads=[tb], writes=[dbg["dbg_xat"]])

    def ew(fq, r):
        if fq % 2 == 0:
            s_i = 6 + r // 2
            return sl[s_i], bc(s_i)[:, (r % 2) * 1024:(r % 2) * 1024 + 1024]
        return wreg[r], wreg[r].ap

    hsb = [[Buf(bc(2 + st * 2 + j // 4)[:, (j % 4) * 512:(j % 4) * 512 + 512],
                f"hs{st}_{j}") for j in range(8)] for st in range(2)]

    def hsre(tc, j):
        b = hsb[tc % 2][j]
        return b, b.ap

    def eload(fq):
        for r in range(16):
            wb_, wap = ew(fq, r)
            ft = fq * 8 + (r % 8)
            if r < 8:
                dma("gd", lambda g, ft=ft, wap=wap: g.dma_start(
                    out=wap, in_=d_W1.ap[ft]), reads=[d_W1], writes=[wb_])
            else:
                dma("gd", lambda g, ft=ft, wap=wap: g.dma_start(
                    out=wap, in_=d_W2.ap[ft * 128:(ft + 1) * 128, :]),
                    reads=[d_W2], writes=[wb_])

    eload(0)
    eload(1)
    import os as _os
    EFQ = int(_os.environ.get('KEFQ', '4'))
    for fq in range(EFQ):
        for tc in range(4):
            def w1mm(j, fq=fq, tc=tc):
                hbank = pb[j % 2]
                wb_, wap = ew(fq, j)
                chunks = []
                for k in range(8):
                    chunks.append(lambda t, k=k, hbank=hbank, wap=wap, tc=tc:
                                  t.matmul(
                        hbank.ap, wap[:, k * 128:(k + 1) * 128],
                        big32ap[:, k * T + tc * 512:k * T + (tc + 1) * 512],
                        start=(k == 0), stop=(k == 7)))
                mm_acc(chunks, reads=[wb_] + b32, writes=[hbank])

            def gelu(j, fq=fq, tc=tc):
                hbank = pb[j % 2]
                hb_, hap = hsre(tc, j)
                ft = fq * 8 + j
                sop(lambda s, hap=hap, hbank=hbank, ft=ft: s.activation(
                    hap, hbank.ap, Act.Gelu_apprx_tanh,
                    bias=v_b1.ap[:, ft:ft + 1]),
                    reads=[hbank, v_b1], writes=[hb_])

            def w2mm(j, dts, fq=fq, tc=tc):
                wb_, wap = ew(fq, 8 + j)
                hb_, hap = hsre(tc, j)
                for dt in dts:
                    bank = pb[2 + (dt if dt < 6 else dt - 6)]
                    E.op("t", lambda t, wap=wap, hap=hap, dt=dt, bank=bank, j=j:
                         t.matmul(bank.ap, wap[:, dt * 128:(dt + 1) * 128],
                                  hap, start=(j == 0), stop=(j == 7)),
                         reads=[wb_, hb_],
                         writes=([bank] if (j == 0 or j == 7) else []),
                         wtok_update=(j == 7))

            def evac(dt, bank, fq=fq, tc=tc):
                reg_ap = b64[dt].ap[:, tc * 512:(tc + 1) * 512]
                if fq == 0:
                    sop(lambda s, bank=bank, reg_ap=reg_ap: s.copy(
                        reg_ap, bank.ap), reads=[bank], writes=[b64[dt]])
                else:
                    tslot = sl[dt % 2]
                    toff = ((dt // 2) % 2) * 512
                    tap = tslot.ap[:, toff:toff + 512]
                    vop(lambda v, bank=bank, tap=tap, reg_ap=reg_ap:
                        v.tensor_tensor(tap, bank.ap, reg_ap, Alu.add),
                        reads=[bank, b64[dt]], writes=[tslot])
                    sop(lambda s, tap=tap, reg_ap=reg_ap: s.copy(reg_ap, tap),
                        reads=[tslot], writes=[b64[dt]])

            w1mm(0)
            for j in range(8):
                gelu(j)
                if j + 1 < 8:
                    w1mm(j + 1)
                w2mm(j, range(6))
            evac(0, pb[2])
            evac(1, pb[3])
            for j in range(8):
                w2mm(j, (6, 7))
            for dt in (2, 3, 4, 5):
                evac(dt, pb[2 + dt])
            evac(6, pb[2])
            evac(7, pb[3])
        if fq + 2 < EFQ:
            eload(fq + 2)

    if debug_outputs:
        for k in range(NCT):
            dma("qd", lambda q, k=k: q.dma_start(
                out=dbg["dbg_o2"].ap[k * 128:(k + 1) * 128, :],
                in_=b64[k].ap), reads=[b64[k]], writes=[dbg["dbg_o2"]])

    for k in range(NCT):
        for h in range(2):
            xac, t0 = sl[1], sl[9]
            dma("qd", lambda q, k=k, h=h, xac=xac: q.dma_start(
                out=xac.ap, in_=d_xat.ap[k * 128:(k + 1) * 128, h * FD:(h + 1) * FD]),
                reads=[d_xat], writes=[xac])
            pop(lambda g, k=k, h=h, t0=t0: g.tensor_scalar(
                t0.ap, b64[k].ap[:, h * FD:(h + 1) * FD],
                1.0, v_b2.ap[:, k:k + 1], Alu.mult, Alu.add),
                reads=[b64[k], v_b2], writes=[t0])
            vop(lambda v, k=k, h=h, xac=xac, t0=t0: v.tensor_tensor(
                b64[k].ap[:, h * FD:(h + 1) * FD], t0.ap, xac.ap, Alu.add),
                reads=[t0, xac], writes=[b64[k]])
    mu_h, iv_h = ln_rows(b64, 1)
    out_toks = []
    for k in range(NCT):
        for h in range(2):
            t0, t1_, af = sl[0], sl[1], sl[9]
            vop(lambda v, k=k, h=h, t0=t0: v.tensor_tensor(
                t0.ap, b64[k].ap[:, h * FD:(h + 1) * FD],
                mu_h[h].ap, Alu.subtract), reads=[b64[k], mu_h[h]], writes=[t0])
            vop(lambda v, h=h, t0=t0, t1_=t1_: v.tensor_tensor(
                t1_.ap, t0.ap, iv_h[h].ap, Alu.mult),
                reads=[t0, iv_h[h]], writes=[t1_])
            pop(lambda g, k=k, t1_=t1_, af=af: g.tensor_scalar(
                af.ap, t1_.ap, v_l2s.ap[:, k:k + 1], v_l2b.ap[:, k:k + 1],
                Alu.mult, Alu.add), reads=[t1_, v_l2s, v_l2b], writes=[af])
            fc, xc2 = sl[3], sl[4]
            dma("qd", lambda q, k=k, h=h, fc=fc: q.dma_start(
                out=fc.ap, in_=d_ffn.ap[k * 128:(k + 1) * 128, h * FD:(h + 1) * FD]),
                reads=[d_ffn], writes=[fc])
            dma("qd", lambda q, k=k, h=h, xc2=xc2: q.dma_start(
                out=xc2.ap, in_=d_xT.ap[k * 128:(k + 1) * 128, h * FD:(h + 1) * FD]),
                reads=[d_xT], writes=[xc2])
            t2_ = sl[10]
            vop(lambda v, fc=fc, af=af, t2_=t2_: v.tensor_tensor(
                t2_.ap, af.ap, fc.ap, Alu.add), reads=[af, fc], writes=[t2_])
            ot = sl[11]
            vop(lambda v, xc2=xc2, t2_=t2_, ot=ot: v.tensor_tensor(
                ot.ap, t2_.ap, xc2.ap, Alu.add), reads=[t2_, xc2], writes=[ot])
            tk = dma("qd", lambda q, k=k, h=h, ot=ot: q.dma_start(
                out=d_y.ap[k * 128:(k + 1) * 128, h * FD:(h + 1) * FD], in_=ot.ap),
                reads=[ot], writes=[d_y])
            out_toks.append(tk)
    E.wait_on("q", out_toks)

    sems = {s: ctx.enter_context(nc.semaphore(f"sem_{s}")) for s in SEMS}
    runners = E.play(sems)
    with nc.Block() as blk:
        blk.tensor(runners["t"])
        blk.vector(runners["v"])
        blk.scalar(runners["s"])
        blk.gpsimd(runners["g"])
        blk.sync(runners["q"])
    ctx.close()
    return nc


def _host_inputs(inputs):
    x = np.asarray(inputs["L0_attn_x"], np.float32)
    Wq = np.asarray(inputs["L0_attn_Wq"], np.float32)
    Wk = np.asarray(inputs["L0_attn_Wk"], np.float32)
    Wv = np.asarray(inputs["L0_attn_Wv"], np.float32)
    Wa = np.asarray(inputs["L0_attn_Wa"], np.float32)
    Wg = np.asarray(inputs["L0_attn_Wg"], np.float32)
    Wo = np.asarray(inputs["L0_attn_Wo"], np.float32)
    W1 = np.asarray(inputs["L0_attn_W1"], np.float32)
    W2 = np.asarray(inputs["L0_attn_W2"], np.float32)
    W_in = np.asarray(inputs["L0_ffn_W_in"], np.float32)
    W_out = np.asarray(inputs["L0_ffn_W_out"], np.float32)
    b_in = np.asarray(inputs["L0_ffn_b_in"], np.float32)

    def pan(w):
        K, N = w.shape
        kt, npan = K // 128, N // 128
        return np.ascontiguousarray(
            w.reshape(kt, 128, npan, 128).transpose(2, 1, 0, 3).reshape(npan, 128, kt * 128))

    def vec_pp(v, cols):
        return np.ascontiguousarray(np.asarray(v, np.float32).reshape(cols, 128).T)

    WinT = np.zeros((D, WPAD), np.float32)
    WinT.reshape(D, PAR, 256)[:, :, :255] = W_in.T.reshape(D, PAR, 255)
    WoutT = np.zeros((WPAD, D), np.float32)
    WoutT.reshape(PAR, 256, D)[:, :255, :] = W_out.T.reshape(PAR, 255, D)
    binrow = np.zeros((1, WPAD), np.float32)
    binrow.reshape(PAR, 256)[:, :255] = np.asarray(b_in, np.float32).reshape(PAR, 255)

    shared = {
        "Wq_p": pan(Wq), "Wk_p": pan(Wk), "Wv_p": pan(Wv), "Wg_p": pan(Wg),
        "Wa_p": pan(Wa), "W1_p": pan(W1),
        "Wo_r": np.ascontiguousarray(Wo), "W2_r": np.ascontiguousarray(W2),
        "WinT_r": np.ascontiguousarray(WinT), "WoutT_r": np.ascontiguousarray(WoutT),
        "gn_s": vec_pp(inputs["L0_attn_gn_scale"], 8),
        "gn_b": vec_pp(inputs["L0_attn_gn_bias"], 8),
        "ln1_s": vec_pp(inputs["L0_attn_ln1_scale"], 8),
        "ln1_b": vec_pp(inputs["L0_attn_ln1_bias"], 8),
        "ln2_s": vec_pp(inputs["L0_attn_ln2_scale"], 8),
        "ln2_b": vec_pp(inputs["L0_attn_ln2_bias"], 8),
        "b1_pp": vec_pp(inputs["L0_attn_b1"], 32),
        "b2_pp": vec_pp(inputs["L0_attn_b2"], 8),
        "binrow": binrow,
    }
    in_maps = []
    for c in range(8):
        b, qd = c // 4, c % 4
        m = dict(shared)
        m["xT"] = np.ascontiguousarray(x[b, qd * T:(qd + 1) * T, :].T)
        in_maps.append(m)
    return in_maps


def kernel(**inputs):
    in_maps = _host_inputs(inputs)
    nc = build_program()
    res = run_bass_kernel_spmd(nc, in_maps, core_ids=list(range(8)))
    out = np.empty((2, 8192, D), np.float32)
    for c in range(8):
        b, qd = c // 4, c % 4
        out[b, qd * T:(qd + 1) * T, :] = np.asarray(res.results[c]["y"]).T
    return out


# version 12
# speedup vs baseline: 1.0503x; 1.0503x over previous
import contextlib
import os
import numpy as np
import concourse.bass as bass
import concourse.mybir as mybir
from concourse.bass_utils import run_bass_kernel_spmd

F32 = mybir.dt.float32
BF16 = mybir.dt.bfloat16
Alu = mybir.AluOpType
Act = mybir.ActivationFunctionType

PI = float(np.pi)
TWO_PI = float(2 * np.pi)
HALF_PI = float(np.pi / 2)

D = 1024
T = 2048
NCT = 8
FD = 1024
FF = 4096
WPAD = 2048
PAR = 8
EPS_TOP = 1e-5
EPS_BLK = 1e-6
NG = float(8192 * 32)

NSLOTS_DMA = 8
_DMA_Q = ("qd", "sd", "gd")
SEMS = ("t", "v", "s", "g", "q", "cc") + tuple(
    f"{q}{i}" for q in _DMA_Q for i in range(NSLOTS_DMA))
ENG_OF = {"t": "t", "v": "v", "s": "s", "g": "g", "q": "q", "cc": "g"}
for _i in range(NSLOTS_DMA):
    ENG_OF[f"qd{_i}"] = "q"
    ENG_OF[f"sd{_i}"] = "s"
    ENG_OF[f"gd{_i}"] = "g"
DMA_SEMS = frozenset(f"{q}{i}" for q in _DMA_Q for i in range(NSLOTS_DMA))


class Buf:
    __slots__ = ("ap", "wtok", "rtoks", "name", "excl")

    def __init__(self, ap, name="", excl=False):
        self.ap = ap
        self.wtok = None
        self.rtoks = []
        self.name = name
        self.excl = excl


class Emitter:
    def __init__(self):
        self.prog = {e: [] for e in "tvsgq"}
        self.seq = {s: 0 for s in SEMS}
        self.needed = set()
        self.dma_rr = {q: 0 for q in _DMA_Q}
        self.dma_slot_buf = {f"{q}{i}": Buf(None, f"slot_{q}{i}")
                             for q in _DMA_Q for i in range(NSLOTS_DMA)}

    def op(self, sem, fn, reads=(), writes=(), wtok_update=True):
        eng = ENG_OF[sem]
        deps = set()
        for b in reads:
            if b.wtok is not None:
                deps.add(b.wtok)
            if b.excl:
                for t in b.rtoks:
                    if ENG_OF[t[0]] != eng:
                        deps.add(t)
        for b in writes:
            if b.wtok is not None:
                deps.add(b.wtok)
            deps.update(b.rtoks)
        tok = (sem, self.seq[sem])
        self.seq[sem] += 1
        deps.discard(tok)
        self.needed.update(deps)
        if sem in DMA_SEMS or sem == "cc":
            self.needed.add(tok)
        self.prog[eng].append((deps, fn, sem, tok))
        for b in writes:
            if wtok_update:
                b.wtok = tok
                b.rtoks = []
        for b in reads:
            if tok not in b.rtoks:
                b.rtoks.append(tok)
        return tok

    def wait_on(self, sem, toks):
        self.needed.update(toks)
        self.prog[ENG_OF[sem]].append((set(toks), None, sem, None))

    def play(self, sems):
        val_after = {}
        cnt = {s: 0 for s in SEMS}
        for e in "tvsgq":
            for deps, fn, sem, tok in self.prog[e]:
                if tok is not None and tok in self.needed and fn is not None:
                    cnt[sem] += 16 if sem in DMA_SEMS else 1
                if tok is not None:
                    val_after[tok] = cnt[sem]

        def make_runner(e):
            def run(engobj):
                watermark = {}
                for deps, fn, sem, tok in self.prog[e]:
                    for dd in sorted(deps):
                        ds, v = dd[0], val_after[dd]
                        if v > watermark.get(ds, 0):
                            engobj.wait_ge(sems[ds], v)
                            watermark[ds] = v
                    if fn is None:
                        continue
                    ins = fn(engobj)
                    if isinstance(ins, list):
                        ins = ins[-1]
                    if tok in self.needed:
                        ins.then_inc(sems[sem], 16 if sem in DMA_SEMS else 1)
            return run

        return {e: make_runner(e) for e in "tvsgq"}


def build_program(debug_outputs=False):
    nc = bass.Bass("TRN2", target_bir_lowering=False, num_devices=8)
    E = Emitter()
    ctx = contextlib.ExitStack()

    def din(name, shape, dt=F32):
        return Buf(nc.dram_tensor(name, list(shape), dt, kind="ExternalInput").ap(), name)

    d_xT = din("xT", [D, T])
    d_Wq = din("Wq_p", [8, 128, 1024])
    d_Wk = din("Wk_p", [8, 128, 1024])
    d_Wv = din("Wv_p", [8, 128, 1024])
    d_Wg = din("Wg_p", [8, 128, 1024])
    d_Wa = din("Wa_p", [16, 128, 1024])
    d_W1 = din("W1_p", [32, 128, 1024])
    d_Wo = din("Wo_r", [D, D])
    d_W2 = din("W2_r", [FF, D])
    d_WinT = din("WinT_r", [D, WPAD])
    d_WoutT = din("WoutT_r", [WPAD, D])
    d_gns = din("gn_s", [128, 8])
    d_gnb = din("gn_b", [128, 8])
    d_l1s = din("ln1_s", [128, 8])
    d_l1b = din("ln1_b", [128, 8])
    d_l2s = din("ln2_s", [128, 8])
    d_l2b = din("ln2_b", [128, 8])
    d_b1 = din("b1_pp", [128, 32])
    d_b2 = din("b2_pp", [128, 8])
    d_binrow = din("binrow", [1, WPAD])

    d_y = Buf(nc.dram_tensor("y", [D, T], F32, kind="ExternalOutput").ap(), "y")
    d_xn = Buf(nc.dram_tensor("xn_sc", [D, T], F32, kind="Internal").ap(), "xn_sc")
    d_xat = Buf(nc.dram_tensor("xat_sc", [D, T], F32, kind="Internal").ap(), "xat_sc")
    d_ffn = Buf(nc.dram_tensor("ffn_sc", [D, T], F32, kind="Internal").ap(), "ffn_sc")
    d_sg = Buf(nc.dram_tensor("sg_sc", [D, T], BF16, kind="Internal").ap(), "sg_sc")
    d_yr = Buf(nc.dram_tensor("yr_sc", [D, T], F32, kind="Internal").ap(), "yr_sc")
    d_o2 = Buf(nc.dram_tensor("o2_sc", [D, T], F32, kind="Internal").ap(), "o2_sc")
    d_ccin = Buf(nc.dram_tensor("cc_in", [128, 24], F32, kind="Internal").ap(), "cc_in")
    d_ccout = Buf(nc.dram_tensor("cc_out", [128, 24], F32, kind="Internal").ap(), "cc_out")
    dbg = {}
    if debug_outputs:
        for nm in ("dbg_xn", "dbg_yr", "dbg_z", "dbg_mix", "dbg_xat", "dbg_ffn", "dbg_o2"):
            dbg[nm] = Buf(nc.dram_tensor(nm, [D, T], F32, kind="ExternalOutput").ap(), nm)
        dbg["dbg_st"] = Buf(
            nc.dram_tensor("dbg_st", [128, 24], F32, kind="ExternalOutput").ap(), "dbg_st")

    def sbt(name, shape, dt=F32):
        h = ctx.enter_context(nc.sbuf_tensor(name, list(shape), dt))
        return Buf(h[:], name)

    big64_t = ctx.enter_context(nc.sbuf_tensor("big64", [128, NCT * T], F32))
    big32_t = ctx.enter_context(nc.sbuf_tensor("big32", [128, NCT * T], BF16))
    big64ap = big64_t[:]
    big32ap = big32_t[:]
    b64 = [Buf(big64ap[:, k * T:(k + 1) * T], f"b64_{k}") for k in range(NCT)]
    b32 = [Buf(big32ap[:, k * T:(k + 1) * T], f"b32_{k}") for k in range(NCT)]
    NSLOT = 14
    sl = [sbt(f"sl{i}", [128, FD]) for i in range(NSLOT)]
    wbuf_t = ctx.enter_context(nc.sbuf_tensor("wbuf", [128, 16384], BF16))
    wreg = [Buf(wbuf_t[:][:, i * 1024:(i + 1) * 1024], f"wreg{i}") for i in range(16)]
    r_mu = sbt("r_mu", [1, T])
    r_iv = sbt("r_iv", [1, T])
    onescol = sbt("onescol", [128, 1], BF16)
    onesrow = sbt("onesrow", [1, 128], BF16)
    onesrow_f = sbt("onesrow_f", [1, 128])
    binbf = sbt("binbf", [1, WPAD], BF16)
    cst = sbt("cst", [128, 6])
    v_gns = sbt("v_gns", [128, 8])
    v_gnb = sbt("v_gnb", [128, 8])
    v_l1s = sbt("v_l1s", [128, 8])
    v_l1b = sbt("v_l1b", [128, 8])
    v_l2s = sbt("v_l2s", [128, 8])
    v_l2b = sbt("v_l2b", [128, 8])
    v_b1 = sbt("v_b1", [128, 32])
    v_b2 = sbt("v_b2", [128, 8])
    st_yr = sbt("st_yr", [128, 16])
    st_yi = sbt("st_yi", [128, 16])
    st_q2 = sbt("st_q2", [128, 32])
    st_yr_c = [Buf(st_yr.ap[:, i:i + 1], f"styr{i}") for i in range(16)]
    st_yi_c = [Buf(st_yi.ap[:, i:i + 1], f"styi{i}") for i in range(16)]
    st_q2_c = [Buf(st_q2.ap[:, i:i + 1], f"stq2{i}") for i in range(32)]
    st_g = [sbt(f"st_g{i}", [128, 24]) for i in range(14)]
    gnA = sbt("gnA", [128, 8])
    gnB = sbt("gnB", [128, 8])
    carry = [sbt(f"carry{i}", [128, 1]) for i in range(3)]
    carryO = [sbt(f"carryO{i}", [128, 1]) for i in range(3)]
    slB = [Buf(big64ap[:, i * FD:(i + 1) * FD], f"slB{i}") for i in range(14)]
    pb = [Buf(ctx.enter_context(nc.psum_tensor(f"pb{i}", [128, 512], F32))[:], f"pb{i}",
              excl=True) for i in range(8)]
    idn = sbt("idn", [128, 128], BF16)

    def bc(i):
        return sl[i].ap.bitcast(BF16)

    def row_ap(buf, c):
        return buf.ap[0:1, c * 512:(c + 1) * 512]

    def vop(fn, reads=(), writes=()):
        return E.op("v", fn, reads, writes)

    def sop(fn, reads=(), writes=()):
        return E.op("s", fn, reads, writes)

    def top(fn, reads=(), writes=()):
        return E.op("t", fn, reads, writes)

    def pop(fn, reads=(), writes=()):
        return E.op("g", fn, reads, writes)

    def dma(q, fn, reads=(), writes=()):
        slot = E.dma_rr[q] % NSLOTS_DMA
        E.dma_rr[q] += 1
        sem = f"{q}{slot}"
        sbuf = E.dma_slot_buf[sem]
        return E.op(sem, fn, list(reads), list(writes) + [sbuf])

    def mm_acc(chunks, reads, writes):
        n = len(chunks)
        for i, f in enumerate(chunks):
            if i == 0:
                E.op("t", f, reads=reads, writes=writes, wtok_update=(n == 1))
            elif i == n - 1:
                E.op("t", f, reads=reads, writes=writes)
            else:
                E.op("t", f, reads=(), writes=())

    vop(lambda v: v.memset(onescol.ap, 1.0), writes=[onescol])
    vop(lambda v: v.memset(cst.ap[:, 0:1], EPS_TOP), writes=[cst])
    vop(lambda v: v.memset(cst.ap[:, 1:2], EPS_BLK), writes=[cst])
    vop(lambda v: v.memset(cst.ap[:, 2:3], 1e-38), writes=[cst])
    vop(lambda v: v.memset(cst.ap[:, 3:4], 0.5), writes=[cst])
    vop(lambda v: v.memset(cst.ap[:, 4:5], HALF_PI), writes=[cst])
    vop(lambda v: v.memset(cst.ap[:, 5:6], -0.5), writes=[cst])
    pop(lambda g: g.iota(sl[0].ap[:, 0:128], pattern=[[-1, 128]], base=0,
                         channel_multiplier=1,
                         allow_small_or_imprecise_dtypes=True), writes=[sl[0]])
    vop(lambda v: v.tensor_scalar(idn.ap, sl[0].ap[:, 0:128], 0.0, None,
                                  Alu.is_equal),
        reads=[sl[0]], writes=[idn])
    vop(lambda v: v.memset(onesrow.ap, 1.0), writes=[onesrow])
    vop(lambda v: v.memset(onesrow_f.ap, 1.0), writes=[onesrow_f])
    for buf, src in [(v_gns, d_gns), (v_gnb, d_gnb), (v_l1s, d_l1s), (v_l1b, d_l1b),
                     (v_l2s, d_l2s), (v_l2b, d_l2b), (v_b1, d_b1), (v_b2, d_b2)]:
        dma("qd", lambda q, b=buf, s=src: q.dma_start(out=b.ap, in_=s.ap),
            reads=[src], writes=[buf])
    dma("gd", lambda g: g.dma_start(out=binbf.ap, in_=d_binrow.ap),
        reads=[d_binrow], writes=[binbf])

    def ln_rows(src, eps_col):
        for k in range(NCT):
            cbuf, qbuf = sl[2], sl[3]
            sop(lambda s, k=k: s.copy(bc(2), src[k].ap),
                reads=[src[k]], writes=[cbuf])
            sop(lambda s, k=k: s.activation(bc(3), src[k].ap, Act.Square),
                reads=[src[k]], writes=[qbuf])
            for c in range(4):
                top(lambda t, k=k, c=c: t.matmul(
                    pb[c].ap[0:1, :], onescol.ap, bc(2)[:, c * 512:(c + 1) * 512],
                    start=(k == 0), stop=(k == NCT - 1)),
                    reads=[cbuf, onescol],
                    writes=([pb[c]] if k in (0, NCT - 1) else []))
                top(lambda t, k=k, c=c: t.matmul(
                    pb[4 + c].ap[0:1, :], onescol.ap, bc(3)[:, c * 512:(c + 1) * 512],
                    start=(k == 0), stop=(k == NCT - 1)),
                    reads=[qbuf, onescol],
                    writes=([pb[4 + c]] if k in (0, NCT - 1) else []))
        for c in range(4):
            vop(lambda v, c=c: v.tensor_scalar(row_ap(r_mu, c), pb[c].ap[0:1, :],
                                               1.0 / D, None, Alu.mult),
                reads=[pb[c]], writes=[r_mu])
            s2d = Buf(sl[2].ap[0:1, 0:512], "s2d")
            msq = Buf(sl[3].ap[0:1, 0:512], "msq")
            vrr = Buf(sl[4].ap[0:1, 0:512], "vrr")
            sdr = Buf(sl[2].ap[0:1, 512:1024], "sdr")
            vop(lambda v, c=c, s2d=s2d: v.tensor_scalar(s2d.ap, pb[4 + c].ap[0:1, :],
                                                        1.0 / D, None, Alu.mult),
                reads=[pb[4 + c]], writes=[s2d, sl[2]])
            vop(lambda v, c=c, msq=msq: v.tensor_tensor(msq.ap, row_ap(r_mu, c),
                                                        row_ap(r_mu, c), Alu.mult),
                reads=[r_mu], writes=[msq, sl[3]])
            vop(lambda v, s2d=s2d, msq=msq, vrr=vrr: v.tensor_tensor(
                vrr.ap, s2d.ap, msq.ap, Alu.subtract),
                reads=[s2d, msq, sl[2], sl[3]], writes=[vrr, sl[4]])
            sop(lambda s, c=c, vrr=vrr: s.activation(
                row_ap(r_iv, c), vrr.ap, Act.Sqrt,
                bias=cst.ap[0:1, eps_col:eps_col + 1]),
                reads=[vrr, sl[4], cst], writes=[r_iv])
        for c in range(4):
            top(lambda t, c=c: t.matmul(pb[c].ap, onesrow_f.ap, row_ap(r_mu, c),
                                        start=True, stop=True),
                reads=[r_mu, onesrow_f], writes=[pb[c]])
            top(lambda t, c=c: t.matmul(pb[4 + c].ap, onesrow_f.ap, row_ap(r_iv, c),
                                        start=True, stop=True),
                reads=[r_iv, onesrow_f], writes=[pb[4 + c]])
        for c in range(4):
            dst_m = sl[5 + c // 2]
            dst_i = sl[7 + c // 2]
            vop(lambda v, c=c, dst_m=dst_m: v.tensor_copy(
                dst_m.ap[:, (c % 2) * 512:(c % 2) * 512 + 512], pb[c].ap),
                reads=[pb[c]], writes=[dst_m])
            vop(lambda v, c=c, dst_i=dst_i: v.reciprocal(
                dst_i.ap[:, (c % 2) * 512:(c % 2) * 512 + 512], pb[4 + c].ap),
                reads=[pb[4 + c]], writes=[dst_i])
        return (sl[5], sl[6]), (sl[7], sl[8])

    for k in range(NCT):
        dma("qd", lambda q, k=k: q.dma_start(
            out=b64[k].ap, in_=d_xT.ap[k * 128:(k + 1) * 128, :]),
            reads=[d_xT], writes=[b64[k]])
    mu_h, iv_h = ln_rows(b64, 0)
    for k in range(NCT):
        for h in range(2):
            t0, xnf = sl[0], sl[1]
            vop(lambda v, k=k, h=h, t0=t0: v.tensor_tensor(
                t0.ap, b64[k].ap[:, h * FD:(h + 1) * FD],
                mu_h[h].ap, Alu.subtract), reads=[b64[k], mu_h[h]], writes=[t0])
            vop(lambda v, h=h, t0=t0, xnf=xnf: v.tensor_tensor(
                xnf.ap, t0.ap, iv_h[h].ap, Alu.mult),
                reads=[t0, iv_h[h]], writes=[xnf])
            sop(lambda s, k=k, h=h, xnf=xnf: s.copy(
                b32[k].ap[:, h * FD:(h + 1) * FD], xnf.ap),
                reads=[xnf], writes=[b32[k]])
            dma("qd", lambda q, k=k, h=h, xnf=xnf: q.dma_start(
                out=d_xn.ap[k * 128:(k + 1) * 128, h * FD:(h + 1) * FD], in_=xnf.ap),
                reads=[xnf], writes=[d_xn])
    if debug_outputs:
        for k in range(NCT):
            dma("qd", lambda q, k=k: q.dma_start(
                out=dbg["dbg_xn"].ap[k * 128:(k + 1) * 128, :],
                in_=d_xn.ap[k * 128:(k + 1) * 128, :]),
                reads=[d_xn], writes=[dbg["dbg_xn"]])

    PANELS = [("q", d_Wq, None), ("k", d_Wk, None), ("v", d_Wv, None),
              ("g", d_Wg, None), ("ar", d_Wa, 0), ("ai", d_Wa, 8)]

    def load_panels(ct):
        base = 0 if ct % 2 == 0 else 6
        regs = []
        for j, (_, dten, off) in enumerate(PANELS):
            idx = ct if off is None else off + ct
            r = wreg[base + j]
            dma("gd", lambda g, idx=idx, dten=dten, r=r: g.dma_start(
                out=r.ap, in_=dten.ap[idx]), reads=[dten], writes=[r])
            regs.append(r)
        return regs

    def b_chain(ct, regs):
        SLt = sl if ct % 2 == 0 else slB
        car = carry if ct % 2 == 0 else carryO
        grp = pb[0:4] if ct % 2 == 0 else pb[4:8]

        def bcx(i):
            return SLt[i].ap.bitcast(BF16)

        TH, GR, GI = SLt[10], SLt[11], SLt[12]
        u_s, q_s = SLt[0], SLt[1]
        for h in range(2):
            def rhs(k, c, h=h):
                return big32ap[:, k * T + h * FD + c * 512:
                               k * T + h * FD + (c + 1) * 512]

            def mixer_mm(pj, b0, b1, h=h):
                for c, bank in ((0, b0), (1, b1)):
                    chunks = []
                    for k in range(8):
                        lap = regs[pj].ap[:, k * 128:(k + 1) * 128]
                        rap = rhs(k, c)
                        chunks.append(lambda t, k=k, bank=bank, lap=lap,
                                      rap=rap: t.matmul(
                            bank.ap, lap, rap, start=(k == 0), stop=(k == 7)))
                    mm_acc(chunks, reads=[regs[pj]] + b32, writes=[bank])

            mixer_mm(1, grp[0], grp[1])
            mixer_mm(2, grp[2], grp[3])
            yield
            for c in range(2):
                sop(lambda s, c=c, va=grp[2 + c].ap: s.copy(
                    SLt[2].ap[:, c * 512:(c + 1) * 512], va),
                    reads=[grp[2 + c]], writes=[SLt[2]])
                yield
            for c in range(2):
                vop(lambda v, c=c, u_s=u_s, ka=grp[c].ap: v.tensor_tensor(
                    u_s.ap[:, c * 512:(c + 1) * 512], ka,
                    SLt[2].ap[:, c * 512:(c + 1) * 512], Alu.mult),
                    reads=[grp[c], SLt[2]], writes=[u_s])
                yield
            mixer_mm(4, grp[0], grp[1])
            mixer_mm(5, grp[2], grp[3])
            yield
            for c in range(2):
                sop(lambda s, c=c, aa=grp[c].ap: s.activation(
                    SLt[3].ap[:, c * 512:(c + 1) * 512], aa, Act.Square),
                    reads=[grp[c]], writes=[SLt[3]])
                yield
                sop(lambda s, c=c, aa=grp[2 + c].ap: s.activation(
                    SLt[4].ap[:, c * 512:(c + 1) * 512], aa, Act.Square),
                    reads=[grp[2 + c]], writes=[SLt[4]])
                yield
                sop(lambda s, c=c, aa=grp[c].ap: s.sign(
                    bcx(5)[:, c * 512:(c + 1) * 512], aa),
                    reads=[grp[c]], writes=[SLt[5]])
                yield
                sop(lambda s, c=c, aa=grp[2 + c].ap: s.sign(
                    bcx(5)[:, 1024 + c * 512:1024 + (c + 1) * 512], aa),
                    reads=[grp[2 + c]], writes=[SLt[5]])
                yield
            mixer_mm(0, grp[0], grp[1])
            mixer_mm(3, grp[2], grp[3])
            yield
            for c in range(2):
                sop(lambda s, c=c, aa=grp[2 + c].ap: s.activation(
                    bcx(13)[:, c * 512:(c + 1) * 512], aa, Act.Silu),
                    reads=[grp[2 + c]], writes=[SLt[13]])
                yield
            dma("sd", lambda s_, ct=ct, h=h: s_.dma_start(
                out=d_sg.ap[ct * 128:(ct + 1) * 128, h * FD:(h + 1) * FD],
                in_=bcx(13)[:, 0:FD]), reads=[SLt[13]], writes=[d_sg])
            yield

            vop(lambda v: v.tensor_tensor(SLt[2].ap, SLt[3].ap, SLt[4].ap,
                                          Alu.add),
                reads=[SLt[3], SLt[4]], writes=[SLt[2]])
            yield
            vop(lambda v: v.tensor_tensor(bcx(6)[:, 0:1024], SLt[4].ap,
                                          SLt[3].ap, Alu.is_gt),
                reads=[SLt[3], SLt[4]], writes=[SLt[6]])
            yield
            sop(lambda s: s.activation(SLt[1].ap, SLt[3].ap, Act.Ln,
                                       bias=cst.ap[:, 2:3]),
                reads=[SLt[3], cst], writes=[SLt[1]])
            yield
            sop(lambda s: s.activation(SLt[3].ap, SLt[4].ap, Act.Ln,
                                       bias=cst.ap[:, 2:3]),
                reads=[SLt[4], cst], writes=[SLt[3]])
            yield
            vop(lambda v: v.tensor_tensor(SLt[4].ap, SLt[1].ap, SLt[3].ap,
                                          Alu.subtract),
                reads=[SLt[1], SLt[3]], writes=[SLt[4]])
            yield
            sop(lambda s: s.activation(SLt[1].ap, SLt[4].ap, Act.Abs),
                reads=[SLt[4]], writes=[SLt[1]])
            yield
            sop(lambda s: s.activation(SLt[3].ap, SLt[1].ap, Act.Exp,
                                       scale=cst.ap[:, 5:6]),
                reads=[SLt[1], cst], writes=[SLt[3]])
            yield
            sop(lambda s: s.activation(SLt[4].ap, SLt[2].ap, Act.Ln,
                                       bias=cst.ap[:, 2:3]),
                reads=[SLt[2], cst], writes=[SLt[4]])
            yield
            sop(lambda s: s.activation(SLt[2].ap, SLt[4].ap, Act.Exp,
                                       scale=cst.ap[:, 3:4]),
                reads=[SLt[4], cst], writes=[SLt[2]])
            yield
            sop(lambda s: s.activation(SLt[7].ap, SLt[2].ap, Act.Sigmoid),
                reads=[SLt[2]], writes=[SLt[7]])
            yield
            sop(lambda s: s.activation(SLt[1].ap, SLt[3].ap, Act.Arctan),
                reads=[SLt[3]], writes=[SLt[1]])
            yield
            pop(lambda g: g.tensor_scalar(SLt[2].ap, SLt[1].ap, 1.0, -HALF_PI,
                                          Alu.mult, Alu.add),
                reads=[SLt[1]], writes=[SLt[2]])
            yield
            pop(lambda g: g.tensor_scalar(SLt[3].ap, SLt[1].ap, -2.0, HALF_PI,
                                          Alu.mult, Alu.add),
                reads=[SLt[1]], writes=[SLt[3]])
            yield
            pop(lambda g: g.tensor_tensor(bcx(6)[:, 1024:2048],
                                          bcx(5)[:, 0:1024],
                                          bcx(5)[:, 1024:2048], Alu.mult),
                reads=[SLt[5]], writes=[SLt[6]])
            yield
            vop(lambda v: v.tensor_tensor(SLt[4].ap, bcx(6)[:, 0:1024],
                                          SLt[3].ap, Alu.mult),
                reads=[SLt[6], SLt[3]], writes=[SLt[4]])
            yield
            vop(lambda v: v.tensor_tensor(SLt[3].ap, SLt[4].ap, SLt[2].ap,
                                          Alu.add),
                reads=[SLt[4], SLt[2]], writes=[SLt[3]])
            yield
            vop(lambda v: v.tensor_tensor(SLt[2].ap, bcx(6)[:, 1024:2048],
                                          SLt[3].ap, Alu.mult),
                reads=[SLt[6], SLt[3]], writes=[SLt[2]])
            yield
            vop(lambda v: v.scalar_tensor_tensor(SLt[4].ap,
                                                 bcx(5)[:, 1024:2048],
                                                 HALF_PI, SLt[2].ap,
                                                 Alu.mult, Alu.add),
                reads=[SLt[5], SLt[2]], writes=[SLt[4]])
            yield

            if h == 0:
                vop(lambda v: v.tensor_tensor_scan(TH.ap, SLt[4].ap, SLt[4].ap,
                                                   0.0, Alu.add, Alu.bypass),
                    reads=[SLt[4]], writes=[TH])
                yield
                vop(lambda v: v.tensor_copy(car[0].ap, TH.ap[:, FD - 1:FD]),
                    reads=[TH], writes=[car[0]])
                yield
            else:
                vop(lambda v: v.tensor_tensor_scan(TH.ap, SLt[4].ap, SLt[4].ap,
                                                   car[0].ap[:, 0:1], Alu.add,
                                                   Alu.bypass),
                    reads=[SLt[4], car[0]], writes=[TH])
                yield

            MAGIC = 12582912.0
            pop(lambda g: g.tensor_scalar(SLt[3].ap, TH.ap, 1.0 / TWO_PI, MAGIC,
                                          Alu.mult, Alu.add),
                reads=[TH], writes=[SLt[3]])
            yield
            pop(lambda g: g.tensor_scalar(SLt[2].ap, TH.ap, 1.0 / TWO_PI, 0.25,
                                          Alu.mult, Alu.add),
                reads=[TH], writes=[SLt[2]])
            yield
            pop(lambda g: g.tensor_scalar(SLt[4].ap, SLt[2].ap, 1.0, MAGIC,
                                          Alu.mult, Alu.add),
                reads=[SLt[2]], writes=[SLt[4]])
            yield
            pop(lambda g: g.tensor_scalar(SLt[2].ap, SLt[3].ap, 1.0, -MAGIC,
                                          Alu.mult, Alu.add),
                reads=[SLt[3]], writes=[SLt[2]])
            yield
            pop(lambda g: g.tensor_scalar(SLt[3].ap, SLt[4].ap, 1.0, -MAGIC,
                                          Alu.mult, Alu.add),
                reads=[SLt[4]], writes=[SLt[3]])
            yield
            vop(lambda v: v.scalar_tensor_tensor(SLt[4].ap, SLt[2].ap, -TWO_PI,
                                                 TH.ap, Alu.mult, Alu.add),
                reads=[SLt[2], TH], writes=[SLt[4]])
            yield
            vop(lambda v: v.tensor_scalar(SLt[2].ap, SLt[4].ap, -PI, PI,
                                          Alu.max, Alu.min),
                reads=[SLt[4]], writes=[SLt[2]])
            yield
            vop(lambda v: v.scalar_tensor_tensor(SLt[4].ap, SLt[3].ap, -TWO_PI,
                                                 TH.ap, Alu.mult, Alu.add),
                reads=[SLt[3], TH], writes=[SLt[4]])
            yield
            vop(lambda v: v.tensor_scalar(SLt[3].ap, SLt[4].ap,
                                          -3.0 * HALF_PI, HALF_PI,
                                          Alu.max, Alu.min),
                reads=[SLt[4]], writes=[SLt[3]])
            yield
            sop(lambda s: s.activation(SLt[8].ap, SLt[2].ap, Act.Sin),
                reads=[SLt[2]], writes=[SLt[8]])
            yield
            sop(lambda s: s.activation(SLt[9].ap, SLt[3].ap, Act.Sin,
                                       bias=cst.ap[:, 4:5]),
                reads=[SLt[3], cst], writes=[SLt[9]])
            yield

            vop(lambda v, u_s=u_s: v.tensor_tensor(SLt[2].ap, u_s.ap,
                                                   SLt[9].ap, Alu.mult),
                reads=[u_s, SLt[9]], writes=[SLt[2]])
            yield
            vop(lambda v, u_s=u_s: v.scalar_tensor_tensor(
                SLt[3].ap, u_s.ap, -1.0, SLt[8].ap, Alu.mult, Alu.mult),
                reads=[u_s, SLt[8]], writes=[SLt[3]])
            yield
            if h == 0:
                vop(lambda v: v.tensor_tensor_scan(GR.ap, SLt[7].ap, SLt[2].ap,
                                                   0.0, Alu.mult, Alu.add),
                    reads=[SLt[7], SLt[2]], writes=[GR])
                yield
                vop(lambda v: v.tensor_tensor_scan(GI.ap, SLt[7].ap, SLt[3].ap,
                                                   0.0, Alu.mult, Alu.add),
                    reads=[SLt[7], SLt[3]], writes=[GI])
                yield
                vop(lambda v: v.tensor_copy(car[1].ap, GR.ap[:, FD - 1:FD]),
                    reads=[GR], writes=[car[1]])
                yield
                vop(lambda v: v.tensor_copy(car[2].ap, GI.ap[:, FD - 1:FD]),
                    reads=[GI], writes=[car[2]])
                yield
            else:
                vop(lambda v: v.tensor_tensor_scan(GR.ap, SLt[7].ap, SLt[2].ap,
                                                   car[1].ap[:, 0:1], Alu.mult,
                                                   Alu.add),
                    reads=[SLt[7], SLt[2], car[1]], writes=[GR])
                yield
                vop(lambda v: v.tensor_tensor_scan(GI.ap, SLt[7].ap, SLt[3].ap,
                                                   car[2].ap[:, 0:1], Alu.mult,
                                                   Alu.add),
                    reads=[SLt[7], SLt[3], car[2]], writes=[GI])
                yield

            for c in range(2):
                sop(lambda s, c=c, q_s=q_s, aa=grp[c].ap: s.copy(
                    q_s.ap[:, c * 512:(c + 1) * 512], aa),
                    reads=[grp[c]], writes=[q_s])
                yield

            ui = 2 * ct + h
            vop(lambda v, q_s=q_s: v.tensor_tensor(SLt[2].ap, q_s.ap,
                                                   SLt[9].ap, Alu.mult),
                reads=[q_s, SLt[9]], writes=[SLt[2]])
            yield
            vop(lambda v, q_s=q_s: v.scalar_tensor_tensor(
                SLt[3].ap, q_s.ap, -1.0, SLt[8].ap, Alu.mult, Alu.mult),
                reads=[q_s, SLt[8]], writes=[SLt[3]])
            yield
            vop(lambda v: v.tensor_tensor(SLt[4].ap, SLt[2].ap, GR.ap,
                                          Alu.mult),
                reads=[SLt[2], GR], writes=[SLt[4]])
            yield
            vop(lambda v: v.tensor_tensor(SLt[5].ap, SLt[3].ap, GI.ap,
                                          Alu.mult),
                reads=[SLt[3], GI], writes=[SLt[5]])
            yield
            vop(lambda v, ui=ui: v.scalar_tensor_tensor(
                SLt[0].ap, SLt[4].ap, 1.0, SLt[5].ap, Alu.mult, Alu.add,
                accum_out=st_yr.ap[:, ui:ui + 1]),
                reads=[SLt[4], SLt[5]], writes=[SLt[0], st_yr_c[ui]])
            yield
            dma("qd", lambda q, ct=ct, h=h, u_s=u_s: q.dma_start(
                out=d_yr.ap[ct * 128:(ct + 1) * 128, h * FD:(h + 1) * FD],
                in_=u_s.ap), reads=[u_s], writes=[d_yr])
            yield
            vop(lambda v: v.tensor_tensor(SLt[4].ap, SLt[2].ap, GI.ap,
                                          Alu.mult),
                reads=[SLt[2], GI], writes=[SLt[4]])
            yield
            vop(lambda v: v.tensor_tensor(SLt[5].ap, SLt[3].ap, GR.ap,
                                          Alu.mult),
                reads=[SLt[3], GR], writes=[SLt[5]])
            yield
            vop(lambda v, ui=ui: v.scalar_tensor_tensor(
                SLt[6].ap, SLt[5].ap, -1.0, SLt[4].ap, Alu.mult, Alu.add,
                accum_out=st_yi.ap[:, ui:ui + 1]),
                reads=[SLt[5], SLt[4]], writes=[SLt[6], st_yi_c[ui]])
            yield
            sop(lambda s, ui=ui, u_s=u_s: s.activation(
                SLt[2].ap, u_s.ap, Act.Square,
                accum_out=st_q2.ap[:, 2 * ui:2 * ui + 1]),
                reads=[u_s], writes=[SLt[2], st_q2_c[2 * ui]])
            yield
            sop(lambda s, ui=ui: s.activation(
                SLt[3].ap, SLt[6].ap, Act.Square,
                accum_out=st_q2.ap[:, 2 * ui + 1:2 * ui + 2]),
                reads=[SLt[6]], writes=[SLt[3], st_q2_c[2 * ui + 1]])
            yield

    SKEW = 27
    regsA = load_panels(0)
    regsB = load_panels(1)
    for p in range(4):
        gens = [b_chain(2 * p, regsA), b_chain(2 * p + 1, regsB)]
        alive = [True, True]
        for _ in range(SKEW):
            try:
                next(gens[0])
            except StopIteration:
                alive[0] = False
        while any(alive):
            for gi_ in range(2):
                if alive[gi_]:
                    try:
                        next(gens[gi_])
                    except StopIteration:
                        alive[gi_] = False
        if p + 1 < 4:
            regsA = load_panels(2 * p + 2)
            regsB = load_panels(2 * p + 3)

    if debug_outputs:
        for k in range(NCT):
            tb = sl[0]
            dma("qd", lambda q, k=k, tb=tb: q.dma_start(
                out=tb.ap, in_=d_yr.ap[k * 128:(k + 1) * 128, 0:FD]),
                reads=[d_yr], writes=[tb])
            dma("qd", lambda q, k=k, tb=tb: q.dma_start(
                out=dbg["dbg_yr"].ap[k * 128:(k + 1) * 128, 0:FD], in_=tb.ap),
                reads=[tb], writes=[dbg["dbg_yr"]])
            tb2 = sl[1]
            dma("qd", lambda q, k=k, tb2=tb2: q.dma_start(
                out=tb2.ap, in_=d_yr.ap[k * 128:(k + 1) * 128, FD:T]),
                reads=[d_yr], writes=[tb2])
            dma("qd", lambda q, k=k, tb2=tb2: q.dma_start(
                out=dbg["dbg_yr"].ap[k * 128:(k + 1) * 128, FD:T], in_=tb2.ap),
                reads=[tb2], writes=[dbg["dbg_yr"]])

    for k in range(8):
        for half in range(2):
            r = wreg[2 * k + half]
            dma("gd", lambda g, k=k, half=half, r=r: g.dma_start(
                out=r.ap, in_=d_WinT.ap[k * 128:(k + 1) * 128,
                                        half * 1024:(half + 1) * 1024]),
                reads=[d_WinT], writes=[r])
    woutb = [sl[i] for i in range(8)]
    for wt in range(16):
        s_i, off = wt // 2, (wt % 2) * 1024
        dma("gd", lambda g, wt=wt, s_i=s_i, off=off: g.dma_start(
            out=bc(s_i)[:, off:off + 1024],
            in_=d_WoutT.ap[wt * 128:(wt + 1) * 128, :]),
            reads=[d_WoutT], writes=[sl[s_i]])

    stA = st_g[0]
    for ct in range(NCT):
        vop(lambda v, ct=ct: v.tensor_tensor(
            stA.ap[:, ct:ct + 1], st_yr.ap[:, 2 * ct:2 * ct + 1],
            st_yr.ap[:, 2 * ct + 1:2 * ct + 2], Alu.add),
            reads=[st_yr_c[2 * ct], st_yr_c[2 * ct + 1]], writes=[stA])
        vop(lambda v, ct=ct: v.tensor_tensor(
            stA.ap[:, 8 + ct:9 + ct], st_yi.ap[:, 2 * ct:2 * ct + 1],
            st_yi.ap[:, 2 * ct + 1:2 * ct + 2], Alu.add),
            reads=[st_yi_c[2 * ct], st_yi_c[2 * ct + 1]], writes=[stA])
        vop(lambda v, ct=ct: v.tensor_tensor(
            st_g[1].ap[:, ct:ct + 1], st_q2.ap[:, 4 * ct:4 * ct + 1],
            st_q2.ap[:, 4 * ct + 1:4 * ct + 2], Alu.add),
            reads=[st_q2_c[4 * ct], st_q2_c[4 * ct + 1]], writes=[st_g[1]])
        vop(lambda v, ct=ct: v.tensor_tensor(
            st_g[2].ap[:, ct:ct + 1], st_q2.ap[:, 4 * ct + 2:4 * ct + 3],
            st_q2.ap[:, 4 * ct + 3:4 * ct + 4], Alu.add),
            reads=[st_q2_c[4 * ct + 2], st_q2_c[4 * ct + 3]], writes=[st_g[2]])
        vop(lambda v, ct=ct: v.tensor_tensor(
            stA.ap[:, 16 + ct:17 + ct], st_g[1].ap[:, ct:ct + 1],
            st_g[2].ap[:, ct:ct + 1], Alu.add), reads=[st_g[1], st_g[2]], writes=[stA])
    cur = stA
    nxt = 3
    for step in (16, 8, 4, 2, 1):
        mask = [(i + step) % 32 for i in range(32)]
        shb, sm = st_g[nxt], st_g[nxt + 1]
        vop(lambda v, cur=cur, shb=shb, mask=mask: v.stream_shuffle(shb.ap, cur.ap, mask),
            reads=[cur], writes=[shb])
        vop(lambda v, cur=cur, shb=shb, sm=sm: v.tensor_tensor(
            sm.ap, cur.ap, shb.ap, Alu.add), reads=[cur, shb], writes=[sm])
        cur = sm
        nxt += 2
    dma("qd", lambda q, cur=cur: q.dma_start(out=d_ccin.ap, in_=cur.ap),
        reads=[cur], writes=[d_ccin])
    E.op("cc", lambda g: g.collective_compute(
        "AllReduce", Alu.add, replica_groups=[[0, 1, 2, 3], [4, 5, 6, 7]],
        ins=[d_ccin.ap], outs=[d_ccout.ap]), reads=[d_ccin], writes=[d_ccout])

    def g1(tt):
        for c in range(4):
            chunks = []
            for k in range(8):
                wr_i, woff = 2 * k + c // 2, (c % 2) * 512
                chunks.append(lambda t, k=k, tt=tt, ba=pb[c].ap, wr_i=wr_i,
                              woff=woff: t.matmul(
                    ba, big32ap[:, k * T + tt * 128:k * T + (tt + 1) * 128],
                    wreg[wr_i].ap[:, woff:woff + 512], start=(k == 0), stop=False))
            chunks.append(lambda t, c=c, ba=pb[c].ap: t.matmul(
                ba, onesrow.ap, binbf.ap[0:1, c * 512:(c + 1) * 512],
                start=False, stop=True))
            mm_acc(chunks, reads=b32 + [onesrow, binbf] + wreg, writes=[pb[c]])
        for c in range(4):
            vop(lambda v, c=c, aa=pb[c].ap: v.tensor_scalar(
                bc(8)[:, c * 512:(c + 1) * 512], aa, 0.0, None, Alu.is_gt),
                reads=[pb[c]], writes=[sl[8]])
            sop(lambda s, c=c, aa=pb[c].ap: s.activation(
                bc(11)[:, c * 512:(c + 1) * 512], aa, Act.Silu),
                reads=[pb[c]], writes=[sl[11]])

    def g_masks(tt):
        vop(lambda v: v.memset(bc(10), 0.0), writes=[sl[10]])
        dmr = bc(10).rearrange("p (r n) -> p r n", r=PAR)
        der = bc(8).rearrange("p (r n) -> p r n", r=PAR)
        vop(lambda v, dmr=dmr: v.memset(dmr[:, :, 0:1], 1.0), writes=[sl[10]])
        for lv in range(7):
            bp, bchld, npar = 2 ** lv - 1, 2 ** (lv + 1) - 1, 2 ** lv
            vop(lambda v, dmr=dmr, der=der, bp=bp, bchld=bchld, npar=npar:
                v.tensor_mul(
                dmr[:, :, bchld + 1:bchld + 2 * npar:2], dmr[:, :, bp:bp + npar],
                der[:, :, bp:bp + npar]), reads=[sl[10], sl[8]], writes=[sl[10]])
            vop(lambda v, dmr=dmr, bp=bp, bchld=bchld, npar=npar:
                v.tensor_tensor(
                dmr[:, :, bchld:bchld + 2 * npar:2], dmr[:, :, bp:bp + npar],
                dmr[:, :, bchld + 1:bchld + 2 * npar:2], Alu.subtract),
                reads=[sl[10]], writes=[sl[10]])
        vop(lambda v: v.tensor_mul(bc(8), bc(11), bc(10)),
            reads=[sl[11], sl[10], sl[8]], writes=[sl[8]])
        for grp4 in range(4):
            tb_ = pb[6 + grp4 % 2]
            chunks = []
            for j4 in range(4):
                wt = grp4 * 4 + j4
                chunks.append(lambda t, j4=j4, wt=wt, tb_=tb_: t.matmul(
                    tb_.ap[:, j4 * 128:(j4 + 1) * 128],
                    bc(8)[:, wt * 128:(wt + 1) * 128], idn.ap,
                    start=(j4 == 0), stop=(j4 == 3)))
            mm_acc(chunks, reads=[sl[8], idn], writes=[tb_])
            for j4 in range(4):
                wt = grp4 * 4 + j4
                s_i = 12 + wt // 8
                off = (wt % 8) * 256 + (tt % 2) * 128
                sop(lambda s, j4=j4, s_i=s_i, off=off, tb_=tb_: s.copy(
                    bc(s_i)[:, off:off + 128],
                    tb_.ap[:, j4 * 128:(j4 + 1) * 128]),
                    reads=[tb_], writes=[sl[s_i]])

    def mtT(wt):
        s_i = 12 + wt // 8
        off = (wt % 8) * 256
        return bc(s_i)[:, off:off + 256]

    def g2(tt0):
        for sweep in range(2):
            for dt4 in range(4):
                dt = sweep * 4 + dt4
                bank = pb[4 + dt4 // 2]
                coff = (dt4 % 2) * 256
                chunks = []
                for wt in range(16):
                    s_i, woff = wt // 2, (wt % 2) * 1024
                    st_f = (wt == 0 and dt4 % 2 == 0)
                    sp_f = (wt == 15 and dt4 % 2 == 1)
                    chunks.append(lambda t, wt=wt, dt=dt, bank=bank, coff=coff,
                                  s_i=s_i, woff=woff, st_f=st_f, sp_f=sp_f:
                                  t.matmul(
                        bank.ap[:, coff:coff + 256],
                        bc(s_i)[:, woff + dt * 128:woff + (dt + 1) * 128],
                        mtT(wt), start=st_f, stop=sp_f))
                mm_acc(chunks, reads=[sl[12], sl[13]] + woutb, writes=[bank])
            for b_i in range(2):
                vop(lambda v, b_i=b_i: v.tensor_copy(
                    sl[9].ap[:, b_i * 512:(b_i + 1) * 512], pb[4 + b_i].ap),
                    reads=[pb[4 + b_i]], writes=[sl[9]])
            dma("qd", lambda q, tt0=tt0, sweep=sweep: q.dma_start(
                out=d_ffn.ap.rearrange("(a p) t -> p a t", p=128)[
                    :, sweep * 4:(sweep + 1) * 4, tt0 * 128:(tt0 + 2) * 128],
                in_=sl[9].ap.rearrange("p (a t) -> p a t", a=4)),
                reads=[sl[9]], writes=[d_ffn])

    g1(0)
    for tt in range(16):
        g_masks(tt)
        if tt + 1 < 16:
            g1(tt + 1)
        if tt % 2 == 1:
            g2(tt - 1)
    if debug_outputs:
        for k in range(NCT):
            for h in range(2):
                tb = sl[9]
                dma("qd", lambda q, k=k, h=h, tb=tb: q.dma_start(
                    out=tb.ap, in_=d_ffn.ap[k * 128:(k + 1) * 128, h * FD:(h + 1) * FD]),
                    reads=[d_ffn], writes=[tb])
                dma("qd", lambda q, k=k, h=h, tb=tb: q.dma_start(
                    out=dbg["dbg_ffn"].ap[k * 128:(k + 1) * 128, h * FD:(h + 1) * FD],
                    in_=tb.ap), reads=[tb], writes=[dbg["dbg_ffn"]])

    stG = st_g[1]
    dma("qd", lambda q: q.dma_start(out=stG.ap, in_=d_ccout.ap),
        reads=[d_ccout], writes=[stG])
    if debug_outputs:
        dma("qd", lambda q: q.dma_start(out=dbg["dbg_st"].ap, in_=stG.ap),
            reads=[stG], writes=[dbg["dbg_st"]])
    mrg = st_g[2]
    for a, b_ in ((0, 8), (8, 16), (16, 24)):
        vop(lambda v, a=a, b_=b_: v.tensor_scalar(mrg.ap[:, a:b_], stG.ap[:, a:b_],
                                                  1.0 / NG, None, Alu.mult),
            reads=[stG], writes=[mrg])
    sqm = st_g[3]
    vop(lambda v: v.tensor_tensor(sqm.ap[:, 0:8], mrg.ap[:, 0:8], mrg.ap[:, 0:8], Alu.mult),
        reads=[mrg], writes=[sqm])
    vop(lambda v: v.tensor_tensor(sqm.ap[:, 8:16], mrg.ap[:, 8:16], mrg.ap[:, 8:16],
                                  Alu.mult), reads=[mrg], writes=[sqm])
    vop(lambda v: v.tensor_tensor(st_g[4].ap[:, 0:8], mrg.ap[:, 16:24], sqm.ap[:, 0:8],
                                  Alu.subtract), reads=[mrg, sqm], writes=[st_g[4]])
    vop(lambda v: v.tensor_tensor(st_g[5].ap[:, 0:8], st_g[4].ap[:, 0:8], sqm.ap[:, 8:16],
                                  Alu.subtract), reads=[st_g[4], sqm], writes=[st_g[5]])
    sop(lambda s: s.activation(st_g[6].ap[:, 0:8], st_g[5].ap[:, 0:8], Act.Sqrt,
                               bias=cst.ap[:, 1:2]), reads=[st_g[5], cst], writes=[st_g[6]])
    vop(lambda v: v.reciprocal(st_g[7].ap[:, 0:8], st_g[6].ap[:, 0:8]),
        reads=[st_g[6]], writes=[st_g[7]])
    vop(lambda v: v.tensor_tensor(gnA.ap, st_g[7].ap[:, 0:8], v_gns.ap, Alu.mult),
        reads=[st_g[7], v_gns], writes=[gnA])
    vop(lambda v: v.tensor_tensor(st_g[8].ap[:, 0:8], mrg.ap[:, 0:8], gnA.ap, Alu.mult),
        reads=[mrg, gnA], writes=[st_g[8]])
    vop(lambda v: v.tensor_tensor(gnB.ap, v_gnb.ap, st_g[8].ap[:, 0:8], Alu.subtract),
        reads=[v_gnb, st_g[8]], writes=[gnB])

    for k in range(8):
        dma("gd", lambda g, k=k: g.dma_start(
            out=wreg[k].ap, in_=d_Wo.ap[k * 128:(k + 1) * 128, :]),
            reads=[d_Wo], writes=[wreg[k]])
    for ct in range(NCT):
        dma("qd", lambda q, ct=ct: q.dma_start(
            out=bc(13), in_=d_sg.ap[ct * 128:(ct + 1) * 128, :]),
            reads=[d_sg], writes=[sl[13]])
        for h in range(2):
            SS = sl if ct % 2 == 0 else slB
            yn = SS[0] if h == 0 else SS[1]
            yrl = SS[2] if h == 0 else SS[3]
            dma("qd", lambda q, ct=ct, h=h, yrl=yrl: q.dma_start(
                out=yrl.ap, in_=d_yr.ap[ct * 128:(ct + 1) * 128,
                                        h * FD:(h + 1) * FD]),
                reads=[d_yr], writes=[yrl])
            pop(lambda g, ct=ct, h=h, yn=yn, yrl=yrl: g.tensor_scalar(
                yn.ap, yrl.ap,
                gnA.ap[:, ct:ct + 1], gnB.ap[:, ct:ct + 1], Alu.mult, Alu.add),
                reads=[yrl, gnA, gnB], writes=[yn])
            pop(lambda g, ct=ct, h=h, yn=yn: g.tensor_tensor(
                b32[ct].ap[:, h * FD:(h + 1) * FD],
                yn.ap, bc(13)[:, h * FD:(h + 1) * FD], Alu.mult),
                reads=[yn, sl[13]], writes=[b32[ct]])
    if debug_outputs:
        for k in range(NCT):
            for h in range(2):
                tb = sl[0]
                vop(lambda v, k=k, h=h, tb=tb: v.tensor_copy(
                    tb.ap.bitcast(BF16)[:, 0:FD],
                    b32[k].ap[:, h * FD:(h + 1) * FD]),
                    reads=[b32[k]], writes=[tb])
                tb2 = sl[1]
                vop(lambda v, tb=tb, tb2=tb2: v.tensor_copy(
                    tb2.ap, tb.ap.bitcast(BF16)[:, 0:FD]),
                    reads=[tb], writes=[tb2])
                dma("qd", lambda q, k=k, h=h, tb2=tb2: q.dma_start(
                    out=dbg["dbg_z"].ap[k * 128:(k + 1) * 128, h * FD:(h + 1) * FD],
                    in_=tb2.ap), reads=[tb2], writes=[dbg["dbg_z"]])
    for dt in range(8):
        for tc in range(4):
            bank = pb[(dt * 4 + tc) % 8]
            chunks = []
            for k in range(8):
                chunks.append(lambda t, dt=dt, tc=tc, k=k, bank=bank: t.matmul(
                    bank.ap, wreg[k].ap[:, dt * 128:(dt + 1) * 128],
                    big32ap[:, k * T + tc * 512:k * T + (tc + 1) * 512],
                    start=(k == 0), stop=(k == 7)))
            mm_acc(chunks, reads=b32 + wreg[:8], writes=[bank])
            xnc = sl[2 + (dt * 4 + tc) % 2]
            dma("qd", lambda q, dt=dt, tc=tc, xnc=xnc: q.dma_start(
                out=xnc.ap[:, 0:512],
                in_=d_xn.ap[dt * 128:(dt + 1) * 128, tc * 512:(tc + 1) * 512]),
                reads=[d_xn], writes=[xnc])
            vop(lambda v, dt=dt, tc=tc, bank=bank, xnc=xnc: v.tensor_tensor(
                b64[dt].ap[:, tc * 512:(tc + 1) * 512],
                bank.ap, xnc.ap[:, 0:512], Alu.add),
                reads=[bank, xnc], writes=[b64[dt]])
    if debug_outputs:
        for k in range(NCT):
            dma("qd", lambda q, k=k: q.dma_start(
                out=dbg["dbg_mix"].ap[k * 128:(k + 1) * 128, :],
                in_=b64[k].ap), reads=[b64[k]], writes=[dbg["dbg_mix"]])
    mu_h, iv_h = ln_rows(b64, 1)
    for k in range(NCT):
        for h in range(2):
            t0, t1_, xa = sl[0], sl[1], sl[9]
            vop(lambda v, k=k, h=h, t0=t0: v.tensor_tensor(
                t0.ap, b64[k].ap[:, h * FD:(h + 1) * FD],
                mu_h[h].ap, Alu.subtract), reads=[b64[k], mu_h[h]], writes=[t0])
            vop(lambda v, h=h, t0=t0, t1_=t1_: v.tensor_tensor(
                t1_.ap, t0.ap, iv_h[h].ap, Alu.mult),
                reads=[t0, iv_h[h]], writes=[t1_])
            pop(lambda g, k=k, t1_=t1_, xa=xa: g.tensor_scalar(
                xa.ap, t1_.ap, v_l1s.ap[:, k:k + 1], v_l1b.ap[:, k:k + 1],
                Alu.mult, Alu.add), reads=[t1_, v_l1s, v_l1b], writes=[xa])
            sop(lambda s, k=k, h=h, xa=xa: s.copy(
                b32[k].ap[:, h * FD:(h + 1) * FD], xa.ap),
                reads=[xa], writes=[b32[k]])
            dma("qd", lambda q, k=k, h=h, xa=xa: q.dma_start(
                out=d_xat.ap[k * 128:(k + 1) * 128, h * FD:(h + 1) * FD], in_=xa.ap),
                reads=[xa], writes=[d_xat])
    if debug_outputs:
        for k in range(NCT):
            for h in range(2):
                tb = sl[0]
                dma("qd", lambda q, k=k, h=h, tb=tb: q.dma_start(
                    out=tb.ap, in_=d_xat.ap[k * 128:(k + 1) * 128, h * FD:(h + 1) * FD]),
                    reads=[d_xat], writes=[tb])
                dma("qd", lambda q, k=k, h=h, tb=tb: q.dma_start(
                    out=dbg["dbg_xat"].ap[k * 128:(k + 1) * 128, h * FD:(h + 1) * FD],
                    in_=tb.ap), reads=[tb], writes=[dbg["dbg_xat"]])

    def ew(fq, r):
        if fq % 2 == 0:
            s_i = 6 + r // 2
            return sl[s_i], bc(s_i)[:, (r % 2) * 1024:(r % 2) * 1024 + 1024]
        return wreg[r], wreg[r].ap

    hsb = [[Buf(bc(2 + st * 2 + j // 4)[:, (j % 4) * 512:(j % 4) * 512 + 512],
                f"hs{st}_{j}") for j in range(8)] for st in range(2)]

    def hsre(tc, j):
        b = hsb[tc % 2][j]
        return b, b.ap

    def eload(fq):
        for r in range(16):
            wb_, wap = ew(fq, r)
            ft = fq * 8 + (r % 8)
            if r < 8:
                dma("gd", lambda g, ft=ft, wap=wap: g.dma_start(
                    out=wap, in_=d_W1.ap[ft]), reads=[d_W1], writes=[wb_])
            else:
                dma("gd", lambda g, ft=ft, wap=wap: g.dma_start(
                    out=wap, in_=d_W2.ap[ft * 128:(ft + 1) * 128, :]),
                    reads=[d_W2], writes=[wb_])

    eload(0)
    eload(1)
    import os as _os
    EFQ = int(_os.environ.get('KEFQ', '4'))
    for fq in range(EFQ):
        for tc in range(4):
            def w1mm(j, fq=fq, tc=tc):
                hbank = pb[j % 2]
                wb_, wap = ew(fq, j)
                chunks = []
                for k in range(8):
                    chunks.append(lambda t, k=k, hbank=hbank, wap=wap, tc=tc:
                                  t.matmul(
                        hbank.ap, wap[:, k * 128:(k + 1) * 128],
                        big32ap[:, k * T + tc * 512:k * T + (tc + 1) * 512],
                        start=(k == 0), stop=(k == 7)))
                mm_acc(chunks, reads=[wb_] + b32, writes=[hbank])

            def gelu(j, fq=fq, tc=tc):
                hbank = pb[j % 2]
                hb_, hap = hsre(tc, j)
                ft = fq * 8 + j
                sop(lambda s, hap=hap, hbank=hbank, ft=ft: s.activation(
                    hap, hbank.ap, Act.Gelu_apprx_tanh,
                    bias=v_b1.ap[:, ft:ft + 1]),
                    reads=[hbank, v_b1], writes=[hb_])

            def w2mm(j, dts, fq=fq, tc=tc):
                wb_, wap = ew(fq, 8 + j)
                hb_, hap = hsre(tc, j)
                for dt in dts:
                    bank = pb[2 + (dt if dt < 6 else dt - 6)]
                    E.op("t", lambda t, wap=wap, hap=hap, dt=dt, bank=bank, j=j:
                         t.matmul(bank.ap, wap[:, dt * 128:(dt + 1) * 128],
                                  hap, start=(j == 0), stop=(j == 7)),
                         reads=[wb_, hb_],
                         writes=([bank] if (j == 0 or j == 7) else []),
                         wtok_update=(j == 7))

            def evac(dt, bank, fq=fq, tc=tc):
                reg_ap = b64[dt].ap[:, tc * 512:(tc + 1) * 512]
                if fq == 0:
                    sop(lambda s, bank=bank, reg_ap=reg_ap: s.copy(
                        reg_ap, bank.ap), reads=[bank], writes=[b64[dt]])
                else:
                    tslot = sl[dt % 2]
                    toff = ((dt // 2) % 2) * 512
                    tap = tslot.ap[:, toff:toff + 512]
                    vop(lambda v, bank=bank, tap=tap, reg_ap=reg_ap:
                        v.tensor_tensor(tap, bank.ap, reg_ap, Alu.add),
                        reads=[bank, b64[dt]], writes=[tslot])
                    sop(lambda s, tap=tap, reg_ap=reg_ap: s.copy(reg_ap, tap),
                        reads=[tslot], writes=[b64[dt]])

            w1mm(0)
            for j in range(8):
                gelu(j)
                if j + 1 < 8:
                    w1mm(j + 1)
                w2mm(j, range(6))
            evac(0, pb[2])
            evac(1, pb[3])
            for j in range(8):
                w2mm(j, (6, 7))
            for dt in (2, 3, 4, 5):
                evac(dt, pb[2 + dt])
            evac(6, pb[2])
            evac(7, pb[3])
        if fq + 2 < EFQ:
            eload(fq + 2)

    if debug_outputs:
        for k in range(NCT):
            dma("qd", lambda q, k=k: q.dma_start(
                out=dbg["dbg_o2"].ap[k * 128:(k + 1) * 128, :],
                in_=b64[k].ap), reads=[b64[k]], writes=[dbg["dbg_o2"]])

    for k in range(NCT):
        for h in range(2):
            xac, t0 = sl[1], sl[9]
            dma("qd", lambda q, k=k, h=h, xac=xac: q.dma_start(
                out=xac.ap, in_=d_xat.ap[k * 128:(k + 1) * 128, h * FD:(h + 1) * FD]),
                reads=[d_xat], writes=[xac])
            pop(lambda g, k=k, h=h, t0=t0: g.tensor_scalar(
                t0.ap, b64[k].ap[:, h * FD:(h + 1) * FD],
                1.0, v_b2.ap[:, k:k + 1], Alu.mult, Alu.add),
                reads=[b64[k], v_b2], writes=[t0])
            vop(lambda v, k=k, h=h, xac=xac, t0=t0: v.tensor_tensor(
                b64[k].ap[:, h * FD:(h + 1) * FD], t0.ap, xac.ap, Alu.add),
                reads=[t0, xac], writes=[b64[k]])
    mu_h, iv_h = ln_rows(b64, 1)
    out_toks = []
    for k in range(NCT):
        for h in range(2):
            t0, t1_, af = sl[0], sl[1], sl[9]
            vop(lambda v, k=k, h=h, t0=t0: v.tensor_tensor(
                t0.ap, b64[k].ap[:, h * FD:(h + 1) * FD],
                mu_h[h].ap, Alu.subtract), reads=[b64[k], mu_h[h]], writes=[t0])
            vop(lambda v, h=h, t0=t0, t1_=t1_: v.tensor_tensor(
                t1_.ap, t0.ap, iv_h[h].ap, Alu.mult),
                reads=[t0, iv_h[h]], writes=[t1_])
            pop(lambda g, k=k, t1_=t1_, af=af: g.tensor_scalar(
                af.ap, t1_.ap, v_l2s.ap[:, k:k + 1], v_l2b.ap[:, k:k + 1],
                Alu.mult, Alu.add), reads=[t1_, v_l2s, v_l2b], writes=[af])
            fc, xc2 = sl[3], sl[4]
            dma("qd", lambda q, k=k, h=h, fc=fc: q.dma_start(
                out=fc.ap, in_=d_ffn.ap[k * 128:(k + 1) * 128, h * FD:(h + 1) * FD]),
                reads=[d_ffn], writes=[fc])
            dma("qd", lambda q, k=k, h=h, xc2=xc2: q.dma_start(
                out=xc2.ap, in_=d_xT.ap[k * 128:(k + 1) * 128, h * FD:(h + 1) * FD]),
                reads=[d_xT], writes=[xc2])
            t2_ = sl[10]
            vop(lambda v, fc=fc, af=af, t2_=t2_: v.tensor_tensor(
                t2_.ap, af.ap, fc.ap, Alu.add), reads=[af, fc], writes=[t2_])
            ot = sl[11]
            vop(lambda v, xc2=xc2, t2_=t2_, ot=ot: v.tensor_tensor(
                ot.ap, t2_.ap, xc2.ap, Alu.add), reads=[t2_, xc2], writes=[ot])
            tk = dma("qd", lambda q, k=k, h=h, ot=ot: q.dma_start(
                out=d_y.ap[k * 128:(k + 1) * 128, h * FD:(h + 1) * FD], in_=ot.ap),
                reads=[ot], writes=[d_y])
            out_toks.append(tk)
    E.wait_on("q", out_toks)

    sems = {s: ctx.enter_context(nc.semaphore(f"sem_{s}")) for s in SEMS}
    runners = E.play(sems)
    with nc.Block() as blk:
        blk.tensor(runners["t"])
        blk.vector(runners["v"])
        blk.scalar(runners["s"])
        blk.gpsimd(runners["g"])
        blk.sync(runners["q"])
    ctx.close()
    return nc


def _host_inputs(inputs):
    x = np.asarray(inputs["L0_attn_x"], np.float32)
    Wq = np.asarray(inputs["L0_attn_Wq"], np.float32)
    Wk = np.asarray(inputs["L0_attn_Wk"], np.float32)
    Wv = np.asarray(inputs["L0_attn_Wv"], np.float32)
    Wa = np.asarray(inputs["L0_attn_Wa"], np.float32)
    Wg = np.asarray(inputs["L0_attn_Wg"], np.float32)
    Wo = np.asarray(inputs["L0_attn_Wo"], np.float32)
    W1 = np.asarray(inputs["L0_attn_W1"], np.float32)
    W2 = np.asarray(inputs["L0_attn_W2"], np.float32)
    W_in = np.asarray(inputs["L0_ffn_W_in"], np.float32)
    W_out = np.asarray(inputs["L0_ffn_W_out"], np.float32)
    b_in = np.asarray(inputs["L0_ffn_b_in"], np.float32)

    def pan(w):
        K, N = w.shape
        kt, npan = K // 128, N // 128
        return np.ascontiguousarray(
            w.reshape(kt, 128, npan, 128).transpose(2, 1, 0, 3).reshape(npan, 128, kt * 128))

    def vec_pp(v, cols):
        return np.ascontiguousarray(np.asarray(v, np.float32).reshape(cols, 128).T)

    WinT = np.zeros((D, WPAD), np.float32)
    WinT.reshape(D, PAR, 256)[:, :, :255] = W_in.T.reshape(D, PAR, 255)
    WoutT = np.zeros((WPAD, D), np.float32)
    WoutT.reshape(PAR, 256, D)[:, :255, :] = W_out.T.reshape(PAR, 255, D)
    binrow = np.zeros((1, WPAD), np.float32)
    binrow.reshape(PAR, 256)[:, :255] = np.asarray(b_in, np.float32).reshape(PAR, 255)

    shared = {
        "Wq_p": pan(Wq), "Wk_p": pan(Wk), "Wv_p": pan(Wv), "Wg_p": pan(Wg),
        "Wa_p": pan(Wa), "W1_p": pan(W1),
        "Wo_r": np.ascontiguousarray(Wo), "W2_r": np.ascontiguousarray(W2),
        "WinT_r": np.ascontiguousarray(WinT), "WoutT_r": np.ascontiguousarray(WoutT),
        "gn_s": vec_pp(inputs["L0_attn_gn_scale"], 8),
        "gn_b": vec_pp(inputs["L0_attn_gn_bias"], 8),
        "ln1_s": vec_pp(inputs["L0_attn_ln1_scale"], 8),
        "ln1_b": vec_pp(inputs["L0_attn_ln1_bias"], 8),
        "ln2_s": vec_pp(inputs["L0_attn_ln2_scale"], 8),
        "ln2_b": vec_pp(inputs["L0_attn_ln2_bias"], 8),
        "b1_pp": vec_pp(inputs["L0_attn_b1"], 32),
        "b2_pp": vec_pp(inputs["L0_attn_b2"], 8),
        "binrow": binrow,
    }
    in_maps = []
    for c in range(8):
        b, qd = c // 4, c % 4
        m = dict(shared)
        m["xT"] = np.ascontiguousarray(x[b, qd * T:(qd + 1) * T, :].T)
        in_maps.append(m)
    return in_maps


def kernel(**inputs):
    in_maps = _host_inputs(inputs)
    nc = build_program()
    res = run_bass_kernel_spmd(nc, in_maps, core_ids=list(range(8)))
    out = np.empty((2, 8192, D), np.float32)
    for c in range(8):
        b, qd = c // 4, c % 4
        out[b, qd * T:(qd + 1) * T, :] = np.asarray(res.results[c]["y"]).T
    return out


# version 14
# speedup vs baseline: 1.0653x; 1.0142x over previous
import contextlib
import os
import numpy as np
import concourse.bass as bass
import concourse.mybir as mybir
from concourse.bass_utils import run_bass_kernel_spmd

F32 = mybir.dt.float32
BF16 = mybir.dt.bfloat16
Alu = mybir.AluOpType
Act = mybir.ActivationFunctionType

PI = float(np.pi)
TWO_PI = float(2 * np.pi)
HALF_PI = float(np.pi / 2)

D = 1024
T = 2048
NCT = 8
FD = 1024
FF = 4096
WPAD = 2048
PAR = 8
EPS_TOP = 1e-5
EPS_BLK = 1e-6
NG = float(8192 * 32)

NSLOTS_DMA = 8
_DMA_Q = ("qd", "sd", "gd")
SEMS = ("t", "v", "s", "g", "q", "cc") + tuple(
    f"{q}{i}" for q in _DMA_Q for i in range(NSLOTS_DMA))
ENG_OF = {"t": "t", "v": "v", "s": "s", "g": "g", "q": "q", "cc": "g"}
for _i in range(NSLOTS_DMA):
    ENG_OF[f"qd{_i}"] = "q"
    ENG_OF[f"sd{_i}"] = "s"
    ENG_OF[f"gd{_i}"] = "g"
DMA_SEMS = frozenset(f"{q}{i}" for q in _DMA_Q for i in range(NSLOTS_DMA))


class Buf:
    __slots__ = ("ap", "wtok", "rtoks", "name", "excl")

    def __init__(self, ap, name="", excl=False):
        self.ap = ap
        self.wtok = None
        self.rtoks = []
        self.name = name
        self.excl = excl


class Emitter:
    def __init__(self):
        self.prog = {e: [] for e in "tvsgq"}
        self.seq = {s: 0 for s in SEMS}
        self.needed = set()
        self.dma_rr = {q: 0 for q in _DMA_Q}
        self.dma_slot_buf = {f"{q}{i}": Buf(None, f"slot_{q}{i}")
                             for q in _DMA_Q for i in range(NSLOTS_DMA)}

    def op(self, sem, fn, reads=(), writes=(), wtok_update=True):
        eng = ENG_OF[sem]
        deps = set()
        for b in reads:
            if b.wtok is not None:
                deps.add(b.wtok)
            if b.excl:
                for t in b.rtoks:
                    if ENG_OF[t[0]] != eng:
                        deps.add(t)
        for b in writes:
            if b.wtok is not None:
                deps.add(b.wtok)
            deps.update(b.rtoks)
        tok = (sem, self.seq[sem])
        self.seq[sem] += 1
        deps.discard(tok)
        self.needed.update(deps)
        if sem in DMA_SEMS or sem == "cc":
            self.needed.add(tok)
        self.prog[eng].append((deps, fn, sem, tok))
        for b in writes:
            if wtok_update:
                b.wtok = tok
                b.rtoks = []
        for b in reads:
            if tok not in b.rtoks:
                b.rtoks.append(tok)
        return tok

    def wait_on(self, sem, toks):
        self.needed.update(toks)
        self.prog[ENG_OF[sem]].append((set(toks), None, sem, None))

    def play(self, sems):
        val_after = {}
        cnt = {s: 0 for s in SEMS}
        for e in "tvsgq":
            for deps, fn, sem, tok in self.prog[e]:
                if tok is not None and tok in self.needed and fn is not None:
                    cnt[sem] += 16 if sem in DMA_SEMS else 1
                if tok is not None:
                    val_after[tok] = cnt[sem]

        def make_runner(e):
            def run(engobj):
                watermark = {}
                for deps, fn, sem, tok in self.prog[e]:
                    for dd in sorted(deps):
                        ds, v = dd[0], val_after[dd]
                        if v > watermark.get(ds, 0):
                            engobj.wait_ge(sems[ds], v)
                            watermark[ds] = v
                    if fn is None:
                        continue
                    ins = fn(engobj)
                    if isinstance(ins, list):
                        ins = ins[-1]
                    if tok in self.needed:
                        ins.then_inc(sems[sem], 16 if sem in DMA_SEMS else 1)
            return run

        return {e: make_runner(e) for e in "tvsgq"}


def build_program(debug_outputs=False):
    nc = bass.Bass("TRN2", target_bir_lowering=False, num_devices=8)
    E = Emitter()
    ctx = contextlib.ExitStack()

    def din(name, shape, dt=F32):
        return Buf(nc.dram_tensor(name, list(shape), dt, kind="ExternalInput").ap(), name)

    d_xT = din("xT", [D, T])
    d_Wq = din("Wq_p", [8, 128, 1024])
    d_Wk = din("Wk_p", [8, 128, 1024])
    d_Wv = din("Wv_p", [8, 128, 1024])
    d_Wg = din("Wg_p", [8, 128, 1024])
    d_Wa = din("Wa_p", [16, 128, 1024])
    d_W1 = din("W1_p", [32, 128, 1024])
    d_Wo = din("Wo_r", [D, D])
    d_W2 = din("W2_r", [FF, D])
    d_WinT = din("WinT_r", [D, WPAD])
    d_WoutT = din("WoutT_r", [WPAD, D])
    d_gns = din("gn_s", [128, 8])
    d_gnb = din("gn_b", [128, 8])
    d_l1s = din("ln1_s", [128, 8])
    d_l1b = din("ln1_b", [128, 8])
    d_l2s = din("ln2_s", [128, 8])
    d_l2b = din("ln2_b", [128, 8])
    d_b1 = din("b1_pp", [128, 32])
    d_b2 = din("b2_pp", [128, 8])
    d_binrow = din("binrow", [1, WPAD])

    d_y = Buf(nc.dram_tensor("y", [D, T], F32, kind="ExternalOutput").ap(), "y")
    d_xn = Buf(nc.dram_tensor("xn_sc", [D, T], F32, kind="Internal").ap(), "xn_sc")
    d_xat = Buf(nc.dram_tensor("xat_sc", [D, T], F32, kind="Internal").ap(), "xat_sc")
    d_ffn = Buf(nc.dram_tensor("ffn_sc", [D, T], F32, kind="Internal").ap(), "ffn_sc")
    d_sg = Buf(nc.dram_tensor("sg_sc", [D, T], BF16, kind="Internal").ap(), "sg_sc")
    d_yr = Buf(nc.dram_tensor("yr_sc", [D, T], F32, kind="Internal").ap(), "yr_sc")
    d_o2 = Buf(nc.dram_tensor("o2_sc", [D, T], F32, kind="Internal").ap(), "o2_sc")
    d_ccin = Buf(nc.dram_tensor("cc_in", [128, 24], F32, kind="Internal").ap(), "cc_in")
    d_ccout = Buf(nc.dram_tensor("cc_out", [128, 24], F32, kind="Internal").ap(), "cc_out")
    dbg = {}
    if debug_outputs:
        for nm in ("dbg_xn", "dbg_yr", "dbg_z", "dbg_mix", "dbg_xat", "dbg_ffn", "dbg_o2"):
            dbg[nm] = Buf(nc.dram_tensor(nm, [D, T], F32, kind="ExternalOutput").ap(), nm)
        dbg["dbg_st"] = Buf(
            nc.dram_tensor("dbg_st", [128, 24], F32, kind="ExternalOutput").ap(), "dbg_st")

    def sbt(name, shape, dt=F32):
        h = ctx.enter_context(nc.sbuf_tensor(name, list(shape), dt))
        return Buf(h[:], name)

    big64_t = ctx.enter_context(nc.sbuf_tensor("big64", [128, NCT * T], F32))
    big32_t = ctx.enter_context(nc.sbuf_tensor("big32", [128, NCT * T], BF16))
    big64ap = big64_t[:]
    big32ap = big32_t[:]
    b64 = [Buf(big64ap[:, k * T:(k + 1) * T], f"b64_{k}") for k in range(NCT)]
    b32 = [Buf(big32ap[:, k * T:(k + 1) * T], f"b32_{k}") for k in range(NCT)]
    NSLOT = 14
    sl = [sbt(f"sl{i}", [128, FD]) for i in range(NSLOT)]
    wbuf_t = ctx.enter_context(nc.sbuf_tensor("wbuf", [128, 16384], BF16))
    wreg = [Buf(wbuf_t[:][:, i * 1024:(i + 1) * 1024], f"wreg{i}") for i in range(16)]
    r_mu = sbt("r_mu", [1, T])
    r_iv = sbt("r_iv", [1, T])
    onescol = sbt("onescol", [128, 1], BF16)
    onesrow = sbt("onesrow", [1, 128], BF16)
    onesrow_f = sbt("onesrow_f", [1, 128])
    binbf = sbt("binbf", [1, WPAD], BF16)
    cst = sbt("cst", [128, 6])
    v_gns = sbt("v_gns", [128, 8])
    v_gnb = sbt("v_gnb", [128, 8])
    v_l1s = sbt("v_l1s", [128, 8])
    v_l1b = sbt("v_l1b", [128, 8])
    v_l2s = sbt("v_l2s", [128, 8])
    v_l2b = sbt("v_l2b", [128, 8])
    v_b1 = sbt("v_b1", [128, 32])
    v_b2 = sbt("v_b2", [128, 8])
    st_yr = sbt("st_yr", [128, 16])
    st_yi = sbt("st_yi", [128, 16])
    st_q2 = sbt("st_q2", [128, 32])
    st_yr_c = [Buf(st_yr.ap[:, i:i + 1], f"styr{i}") for i in range(16)]
    st_yi_c = [Buf(st_yi.ap[:, i:i + 1], f"styi{i}") for i in range(16)]
    st_q2_c = [Buf(st_q2.ap[:, i:i + 1], f"stq2{i}") for i in range(32)]
    st_g = [sbt(f"st_g{i}", [128, 24]) for i in range(14)]
    gnA = sbt("gnA", [128, 8])
    gnB = sbt("gnB", [128, 8])
    carry = [sbt(f"carry{i}", [128, 1]) for i in range(3)]
    carryO = [sbt(f"carryO{i}", [128, 1]) for i in range(3)]
    slB = [Buf(big64ap[:, i * FD:(i + 1) * FD], f"slB{i}") for i in range(14)]
    pb = [Buf(ctx.enter_context(nc.psum_tensor(f"pb{i}", [128, 512], F32))[:], f"pb{i}",
              excl=True) for i in range(8)]
    idn = sbt("idn", [128, 128], BF16)

    def bc(i):
        return sl[i].ap.bitcast(BF16)

    def row_ap(buf, c):
        return buf.ap[0:1, c * 512:(c + 1) * 512]

    def vop(fn, reads=(), writes=()):
        return E.op("v", fn, reads, writes)

    def sop(fn, reads=(), writes=()):
        return E.op("s", fn, reads, writes)

    def top(fn, reads=(), writes=()):
        return E.op("t", fn, reads, writes)

    def pop(fn, reads=(), writes=()):
        return E.op("g", fn, reads, writes)

    def dma(q, fn, reads=(), writes=()):
        slot = E.dma_rr[q] % NSLOTS_DMA
        E.dma_rr[q] += 1
        sem = f"{q}{slot}"
        sbuf = E.dma_slot_buf[sem]
        return E.op(sem, fn, list(reads), list(writes) + [sbuf])

    def mm_acc(chunks, reads, writes):
        n = len(chunks)
        for i, f in enumerate(chunks):
            if i == 0:
                E.op("t", f, reads=reads, writes=writes, wtok_update=(n == 1))
            elif i == n - 1:
                E.op("t", f, reads=reads, writes=writes)
            else:
                E.op("t", f, reads=(), writes=())

    vop(lambda v: v.memset(onescol.ap, 1.0), writes=[onescol])
    vop(lambda v: v.memset(cst.ap[:, 0:1], EPS_TOP), writes=[cst])
    vop(lambda v: v.memset(cst.ap[:, 1:2], EPS_BLK), writes=[cst])
    vop(lambda v: v.memset(cst.ap[:, 2:3], 1e-38), writes=[cst])
    vop(lambda v: v.memset(cst.ap[:, 3:4], 0.5), writes=[cst])
    vop(lambda v: v.memset(cst.ap[:, 4:5], HALF_PI), writes=[cst])
    vop(lambda v: v.memset(cst.ap[:, 5:6], -0.5), writes=[cst])
    pop(lambda g: g.iota(sl[0].ap[:, 0:128], pattern=[[-1, 128]], base=0,
                         channel_multiplier=1,
                         allow_small_or_imprecise_dtypes=True), writes=[sl[0]])
    vop(lambda v: v.tensor_scalar(idn.ap, sl[0].ap[:, 0:128], 0.0, None,
                                  Alu.is_equal),
        reads=[sl[0]], writes=[idn])
    vop(lambda v: v.memset(onesrow.ap, 1.0), writes=[onesrow])
    vop(lambda v: v.memset(onesrow_f.ap, 1.0), writes=[onesrow_f])
    for buf, src in [(v_gns, d_gns), (v_gnb, d_gnb), (v_l1s, d_l1s), (v_l1b, d_l1b),
                     (v_l2s, d_l2s), (v_l2b, d_l2b), (v_b1, d_b1), (v_b2, d_b2)]:
        dma("qd", lambda q, b=buf, s=src: q.dma_start(out=b.ap, in_=s.ap),
            reads=[src], writes=[buf])
    dma("gd", lambda g: g.dma_start(out=binbf.ap, in_=d_binrow.ap),
        reads=[d_binrow], writes=[binbf])

    def ln_rows(src, eps_col):
        for k in range(NCT):
            cbuf, qbuf = sl[2], sl[3]
            sop(lambda s, k=k: s.copy(bc(2), src[k].ap),
                reads=[src[k]], writes=[cbuf])
            sop(lambda s, k=k: s.activation(bc(3), src[k].ap, Act.Square),
                reads=[src[k]], writes=[qbuf])
            for c in range(4):
                top(lambda t, k=k, c=c: t.matmul(
                    pb[c].ap[0:1, :], onescol.ap, bc(2)[:, c * 512:(c + 1) * 512],
                    start=(k == 0), stop=(k == NCT - 1)),
                    reads=[cbuf, onescol],
                    writes=([pb[c]] if k in (0, NCT - 1) else []))
                top(lambda t, k=k, c=c: t.matmul(
                    pb[4 + c].ap[0:1, :], onescol.ap, bc(3)[:, c * 512:(c + 1) * 512],
                    start=(k == 0), stop=(k == NCT - 1)),
                    reads=[qbuf, onescol],
                    writes=([pb[4 + c]] if k in (0, NCT - 1) else []))
        for c in range(4):
            vop(lambda v, c=c: v.tensor_scalar(row_ap(r_mu, c), pb[c].ap[0:1, :],
                                               1.0 / D, None, Alu.mult),
                reads=[pb[c]], writes=[r_mu])
            s2d = Buf(sl[2].ap[0:1, 0:512], "s2d")
            msq = Buf(sl[3].ap[0:1, 0:512], "msq")
            vrr = Buf(sl[4].ap[0:1, 0:512], "vrr")
            sdr = Buf(sl[2].ap[0:1, 512:1024], "sdr")
            vop(lambda v, c=c, s2d=s2d: v.tensor_scalar(s2d.ap, pb[4 + c].ap[0:1, :],
                                                        1.0 / D, None, Alu.mult),
                reads=[pb[4 + c]], writes=[s2d, sl[2]])
            vop(lambda v, c=c, msq=msq: v.tensor_tensor(msq.ap, row_ap(r_mu, c),
                                                        row_ap(r_mu, c), Alu.mult),
                reads=[r_mu], writes=[msq, sl[3]])
            vop(lambda v, s2d=s2d, msq=msq, vrr=vrr: v.tensor_tensor(
                vrr.ap, s2d.ap, msq.ap, Alu.subtract),
                reads=[s2d, msq, sl[2], sl[3]], writes=[vrr, sl[4]])
            sop(lambda s, c=c, vrr=vrr: s.activation(
                row_ap(r_iv, c), vrr.ap, Act.Sqrt,
                bias=cst.ap[0:1, eps_col:eps_col + 1]),
                reads=[vrr, sl[4], cst], writes=[r_iv])
        for c in range(4):
            top(lambda t, c=c: t.matmul(pb[c].ap, onesrow_f.ap, row_ap(r_mu, c),
                                        start=True, stop=True),
                reads=[r_mu, onesrow_f], writes=[pb[c]])
            top(lambda t, c=c: t.matmul(pb[4 + c].ap, onesrow_f.ap, row_ap(r_iv, c),
                                        start=True, stop=True),
                reads=[r_iv, onesrow_f], writes=[pb[4 + c]])
        for c in range(4):
            dst_m = sl[5 + c // 2]
            dst_i = sl[7 + c // 2]
            vop(lambda v, c=c, dst_m=dst_m: v.tensor_copy(
                dst_m.ap[:, (c % 2) * 512:(c % 2) * 512 + 512], pb[c].ap),
                reads=[pb[c]], writes=[dst_m])
            vop(lambda v, c=c, dst_i=dst_i: v.reciprocal(
                dst_i.ap[:, (c % 2) * 512:(c % 2) * 512 + 512], pb[4 + c].ap),
                reads=[pb[4 + c]], writes=[dst_i])
        return (sl[5], sl[6]), (sl[7], sl[8])

    for k in range(NCT):
        dma("qd", lambda q, k=k: q.dma_start(
            out=b64[k].ap, in_=d_xT.ap[k * 128:(k + 1) * 128, :]),
            reads=[d_xT], writes=[b64[k]])
    mu_h, iv_h = ln_rows(b64, 0)
    for k in range(NCT):
        for h in range(2):
            t0, xnf = sl[0], sl[1]
            vop(lambda v, k=k, h=h, t0=t0: v.tensor_tensor(
                t0.ap, b64[k].ap[:, h * FD:(h + 1) * FD],
                mu_h[h].ap, Alu.subtract), reads=[b64[k], mu_h[h]], writes=[t0])
            vop(lambda v, h=h, t0=t0, xnf=xnf: v.tensor_tensor(
                xnf.ap, t0.ap, iv_h[h].ap, Alu.mult),
                reads=[t0, iv_h[h]], writes=[xnf])
            sop(lambda s, k=k, h=h, xnf=xnf: s.copy(
                b32[k].ap[:, h * FD:(h + 1) * FD], xnf.ap),
                reads=[xnf], writes=[b32[k]])
            dma("qd", lambda q, k=k, h=h, xnf=xnf: q.dma_start(
                out=d_xn.ap[k * 128:(k + 1) * 128, h * FD:(h + 1) * FD], in_=xnf.ap),
                reads=[xnf], writes=[d_xn])
    if debug_outputs:
        for k in range(NCT):
            dma("qd", lambda q, k=k: q.dma_start(
                out=dbg["dbg_xn"].ap[k * 128:(k + 1) * 128, :],
                in_=d_xn.ap[k * 128:(k + 1) * 128, :]),
                reads=[d_xn], writes=[dbg["dbg_xn"]])

    PANELS = [("q", d_Wq, None), ("k", d_Wk, None), ("v", d_Wv, None),
              ("g", d_Wg, None), ("ar", d_Wa, 0), ("ai", d_Wa, 8)]

    def load_panels(ct):
        base = 0 if ct % 2 == 0 else 6
        regs = []
        for j, (_, dten, off) in enumerate(PANELS):
            idx = ct if off is None else off + ct
            r = wreg[base + j]
            dma("gd", lambda g, idx=idx, dten=dten, r=r: g.dma_start(
                out=r.ap, in_=dten.ap[idx]), reads=[dten], writes=[r])
            regs.append(r)
        return regs

    def b_chain(ct, regs):
        SLt = sl if ct % 2 == 0 else slB
        car = carry if ct % 2 == 0 else carryO
        grp = pb[0:4] if ct % 2 == 0 else pb[4:8]

        def bcx(i):
            return SLt[i].ap.bitcast(BF16)

        TH, GR, GI = SLt[10], SLt[11], SLt[12]
        u_s, q_s = SLt[0], SLt[1]
        for h in range(2):
            def rhs(k, c, h=h):
                return big32ap[:, k * T + h * FD + c * 512:
                               k * T + h * FD + (c + 1) * 512]

            def mixer_mm(pj, b0, b1, h=h):
                for c, bank in ((0, b0), (1, b1)):
                    chunks = []
                    for k in range(8):
                        lap = regs[pj].ap[:, k * 128:(k + 1) * 128]
                        rap = rhs(k, c)
                        chunks.append(lambda t, k=k, bank=bank, lap=lap,
                                      rap=rap: t.matmul(
                            bank.ap, lap, rap, start=(k == 0), stop=(k == 7)))
                    mm_acc(chunks, reads=[regs[pj]] + b32, writes=[bank])

            mixer_mm(1, grp[0], grp[1])
            mixer_mm(2, grp[2], grp[3])
            yield
            for c in range(2):
                sop(lambda s, c=c, va=grp[2 + c].ap: s.copy(
                    SLt[2].ap[:, c * 512:(c + 1) * 512], va),
                    reads=[grp[2 + c]], writes=[SLt[2]])
                yield
            for c in range(2):
                vop(lambda v, c=c, u_s=u_s, ka=grp[c].ap: v.tensor_tensor(
                    u_s.ap[:, c * 512:(c + 1) * 512], ka,
                    SLt[2].ap[:, c * 512:(c + 1) * 512], Alu.mult),
                    reads=[grp[c], SLt[2]], writes=[u_s])
                yield
            mixer_mm(4, grp[0], grp[1])
            mixer_mm(5, grp[2], grp[3])
            yield
            for c in range(2):
                sop(lambda s, c=c, aa=grp[c].ap: s.activation(
                    SLt[3].ap[:, c * 512:(c + 1) * 512], aa, Act.Square),
                    reads=[grp[c]], writes=[SLt[3]])
                yield
                sop(lambda s, c=c, aa=grp[2 + c].ap: s.activation(
                    SLt[4].ap[:, c * 512:(c + 1) * 512], aa, Act.Square),
                    reads=[grp[2 + c]], writes=[SLt[4]])
                yield
                sop(lambda s, c=c, aa=grp[c].ap: s.sign(
                    bcx(5)[:, c * 512:(c + 1) * 512], aa),
                    reads=[grp[c]], writes=[SLt[5]])
                yield
                sop(lambda s, c=c, aa=grp[2 + c].ap: s.sign(
                    bcx(5)[:, 1024 + c * 512:1024 + (c + 1) * 512], aa),
                    reads=[grp[2 + c]], writes=[SLt[5]])
                yield
            mixer_mm(0, grp[0], grp[1])
            mixer_mm(3, grp[2], grp[3])
            yield

            vop(lambda v: v.tensor_tensor(SLt[2].ap, SLt[3].ap, SLt[4].ap,
                                          Alu.add),
                reads=[SLt[3], SLt[4]], writes=[SLt[2]])
            yield
            vop(lambda v: v.tensor_tensor(bcx(6)[:, 0:1024], SLt[4].ap,
                                          SLt[3].ap, Alu.is_gt),
                reads=[SLt[3], SLt[4]], writes=[SLt[6]])
            yield
            sop(lambda s: s.activation(SLt[1].ap, SLt[3].ap, Act.Ln,
                                       bias=cst.ap[:, 2:3]),
                reads=[SLt[3], cst], writes=[SLt[1]])
            yield
            sop(lambda s: s.activation(SLt[3].ap, SLt[4].ap, Act.Ln,
                                       bias=cst.ap[:, 2:3]),
                reads=[SLt[4], cst], writes=[SLt[3]])
            yield
            vop(lambda v: v.tensor_tensor(SLt[4].ap, SLt[1].ap, SLt[3].ap,
                                          Alu.subtract),
                reads=[SLt[1], SLt[3]], writes=[SLt[4]])
            yield
            sop(lambda s: s.activation(SLt[1].ap, SLt[4].ap, Act.Abs),
                reads=[SLt[4]], writes=[SLt[1]])
            yield
            sop(lambda s: s.activation(SLt[3].ap, SLt[1].ap, Act.Exp,
                                       scale=cst.ap[:, 5:6]),
                reads=[SLt[1], cst], writes=[SLt[3]])
            yield
            sop(lambda s: s.activation(SLt[4].ap, SLt[2].ap, Act.Ln,
                                       bias=cst.ap[:, 2:3]),
                reads=[SLt[2], cst], writes=[SLt[4]])
            yield
            sop(lambda s: s.activation(SLt[2].ap, SLt[4].ap, Act.Exp,
                                       scale=cst.ap[:, 3:4]),
                reads=[SLt[4], cst], writes=[SLt[2]])
            yield
            sop(lambda s: s.activation(SLt[7].ap, SLt[2].ap, Act.Sigmoid),
                reads=[SLt[2]], writes=[SLt[7]])
            yield
            sop(lambda s: s.activation(SLt[1].ap, SLt[3].ap, Act.Arctan),
                reads=[SLt[3]], writes=[SLt[1]])
            yield
            for c in range(2):
                sop(lambda s, c=c, aa=grp[2 + c].ap: s.activation(
                    bcx(13)[:, c * 512:(c + 1) * 512], aa, Act.Silu),
                    reads=[grp[2 + c]], writes=[SLt[13]])
                yield
            dma("sd", lambda s_, ct=ct, h=h: s_.dma_start(
                out=d_sg.ap[ct * 128:(ct + 1) * 128, h * FD:(h + 1) * FD],
                in_=bcx(13)[:, 0:FD]), reads=[SLt[13]], writes=[d_sg])
            yield
            pop(lambda g: g.tensor_scalar(SLt[2].ap, SLt[1].ap, 1.0, -HALF_PI,
                                          Alu.mult, Alu.add),
                reads=[SLt[1]], writes=[SLt[2]])
            yield
            pop(lambda g: g.tensor_scalar(SLt[3].ap, SLt[1].ap, -2.0, HALF_PI,
                                          Alu.mult, Alu.add),
                reads=[SLt[1]], writes=[SLt[3]])
            yield
            pop(lambda g: g.tensor_tensor(bcx(6)[:, 1024:2048],
                                          bcx(5)[:, 0:1024],
                                          bcx(5)[:, 1024:2048], Alu.mult),
                reads=[SLt[5]], writes=[SLt[6]])
            yield
            vop(lambda v: v.tensor_tensor(SLt[4].ap, bcx(6)[:, 0:1024],
                                          SLt[3].ap, Alu.mult),
                reads=[SLt[6], SLt[3]], writes=[SLt[4]])
            yield
            vop(lambda v: v.tensor_tensor(SLt[3].ap, SLt[4].ap, SLt[2].ap,
                                          Alu.add),
                reads=[SLt[4], SLt[2]], writes=[SLt[3]])
            yield
            vop(lambda v: v.tensor_tensor(SLt[2].ap, bcx(6)[:, 1024:2048],
                                          SLt[3].ap, Alu.mult),
                reads=[SLt[6], SLt[3]], writes=[SLt[2]])
            yield
            vop(lambda v: v.scalar_tensor_tensor(SLt[4].ap,
                                                 bcx(5)[:, 1024:2048],
                                                 HALF_PI, SLt[2].ap,
                                                 Alu.mult, Alu.add),
                reads=[SLt[5], SLt[2]], writes=[SLt[4]])
            yield

            if h == 0:
                vop(lambda v: v.tensor_tensor_scan(TH.ap, SLt[4].ap, SLt[4].ap,
                                                   0.0, Alu.add, Alu.bypass),
                    reads=[SLt[4]], writes=[TH])
                yield
                vop(lambda v: v.tensor_copy(car[0].ap, TH.ap[:, FD - 1:FD]),
                    reads=[TH], writes=[car[0]])
                yield
            else:
                vop(lambda v: v.tensor_tensor_scan(TH.ap, SLt[4].ap, SLt[4].ap,
                                                   car[0].ap[:, 0:1], Alu.add,
                                                   Alu.bypass),
                    reads=[SLt[4], car[0]], writes=[TH])
                yield

            MAGIC = 12582912.0
            pop(lambda g: g.tensor_scalar(SLt[3].ap, TH.ap, 1.0 / TWO_PI, MAGIC,
                                          Alu.mult, Alu.add),
                reads=[TH], writes=[SLt[3]])
            yield
            pop(lambda g: g.tensor_scalar(SLt[2].ap, TH.ap, 1.0 / TWO_PI, 0.25,
                                          Alu.mult, Alu.add),
                reads=[TH], writes=[SLt[2]])
            yield
            pop(lambda g: g.tensor_scalar(SLt[4].ap, SLt[2].ap, 1.0, MAGIC,
                                          Alu.mult, Alu.add),
                reads=[SLt[2]], writes=[SLt[4]])
            yield
            pop(lambda g: g.tensor_scalar(SLt[2].ap, SLt[3].ap, 1.0, -MAGIC,
                                          Alu.mult, Alu.add),
                reads=[SLt[3]], writes=[SLt[2]])
            yield
            pop(lambda g: g.tensor_scalar(SLt[3].ap, SLt[4].ap, 1.0, -MAGIC,
                                          Alu.mult, Alu.add),
                reads=[SLt[4]], writes=[SLt[3]])
            yield
            vop(lambda v: v.scalar_tensor_tensor(SLt[4].ap, SLt[2].ap, -TWO_PI,
                                                 TH.ap, Alu.mult, Alu.add),
                reads=[SLt[2], TH], writes=[SLt[4]])
            yield
            vop(lambda v: v.tensor_scalar(SLt[2].ap, SLt[4].ap, -PI, PI,
                                          Alu.max, Alu.min),
                reads=[SLt[4]], writes=[SLt[2]])
            yield
            vop(lambda v: v.scalar_tensor_tensor(SLt[4].ap, SLt[3].ap, -TWO_PI,
                                                 TH.ap, Alu.mult, Alu.add),
                reads=[SLt[3], TH], writes=[SLt[4]])
            yield
            vop(lambda v: v.tensor_scalar(SLt[3].ap, SLt[4].ap,
                                          -3.0 * HALF_PI, HALF_PI,
                                          Alu.max, Alu.min),
                reads=[SLt[4]], writes=[SLt[3]])
            yield
            sop(lambda s: s.activation(SLt[8].ap, SLt[2].ap, Act.Sin),
                reads=[SLt[2]], writes=[SLt[8]])
            yield
            sop(lambda s: s.activation(SLt[9].ap, SLt[3].ap, Act.Sin,
                                       bias=cst.ap[:, 4:5]),
                reads=[SLt[3], cst], writes=[SLt[9]])
            yield

            vop(lambda v, u_s=u_s: v.tensor_tensor(SLt[2].ap, u_s.ap,
                                                   SLt[9].ap, Alu.mult),
                reads=[u_s, SLt[9]], writes=[SLt[2]])
            yield
            vop(lambda v, u_s=u_s: v.scalar_tensor_tensor(
                SLt[3].ap, u_s.ap, -1.0, SLt[8].ap, Alu.mult, Alu.mult),
                reads=[u_s, SLt[8]], writes=[SLt[3]])
            yield
            if h == 0:
                vop(lambda v: v.tensor_tensor_scan(GR.ap, SLt[7].ap, SLt[2].ap,
                                                   0.0, Alu.mult, Alu.add),
                    reads=[SLt[7], SLt[2]], writes=[GR])
                yield
                vop(lambda v: v.tensor_tensor_scan(GI.ap, SLt[7].ap, SLt[3].ap,
                                                   0.0, Alu.mult, Alu.add),
                    reads=[SLt[7], SLt[3]], writes=[GI])
                yield
                vop(lambda v: v.tensor_copy(car[1].ap, GR.ap[:, FD - 1:FD]),
                    reads=[GR], writes=[car[1]])
                yield
                vop(lambda v: v.tensor_copy(car[2].ap, GI.ap[:, FD - 1:FD]),
                    reads=[GI], writes=[car[2]])
                yield
            else:
                vop(lambda v: v.tensor_tensor_scan(GR.ap, SLt[7].ap, SLt[2].ap,
                                                   car[1].ap[:, 0:1], Alu.mult,
                                                   Alu.add),
                    reads=[SLt[7], SLt[2], car[1]], writes=[GR])
                yield
                vop(lambda v: v.tensor_tensor_scan(GI.ap, SLt[7].ap, SLt[3].ap,
                                                   car[2].ap[:, 0:1], Alu.mult,
                                                   Alu.add),
                    reads=[SLt[7], SLt[3], car[2]], writes=[GI])
                yield

            for c in range(2):
                sop(lambda s, c=c, q_s=q_s, aa=grp[c].ap: s.copy(
                    q_s.ap[:, c * 512:(c + 1) * 512], aa),
                    reads=[grp[c]], writes=[q_s])
                yield

            ui = 2 * ct + h
            vop(lambda v, q_s=q_s: v.tensor_tensor(SLt[2].ap, q_s.ap,
                                                   SLt[9].ap, Alu.mult),
                reads=[q_s, SLt[9]], writes=[SLt[2]])
            yield
            vop(lambda v, q_s=q_s: v.scalar_tensor_tensor(
                SLt[3].ap, q_s.ap, -1.0, SLt[8].ap, Alu.mult, Alu.mult),
                reads=[q_s, SLt[8]], writes=[SLt[3]])
            yield
            vop(lambda v: v.tensor_tensor(SLt[4].ap, SLt[2].ap, GR.ap,
                                          Alu.mult),
                reads=[SLt[2], GR], writes=[SLt[4]])
            yield
            vop(lambda v: v.tensor_tensor(SLt[5].ap, SLt[3].ap, GI.ap,
                                          Alu.mult),
                reads=[SLt[3], GI], writes=[SLt[5]])
            yield
            vop(lambda v, ui=ui: v.scalar_tensor_tensor(
                SLt[0].ap, SLt[4].ap, 1.0, SLt[5].ap, Alu.mult, Alu.add,
                accum_out=st_yr.ap[:, ui:ui + 1]),
                reads=[SLt[4], SLt[5]], writes=[SLt[0], st_yr_c[ui]])
            yield
            dma("qd", lambda q, ct=ct, h=h, u_s=u_s: q.dma_start(
                out=d_yr.ap[ct * 128:(ct + 1) * 128, h * FD:(h + 1) * FD],
                in_=u_s.ap), reads=[u_s], writes=[d_yr])
            yield
            vop(lambda v: v.tensor_tensor(SLt[4].ap, SLt[2].ap, GI.ap,
                                          Alu.mult),
                reads=[SLt[2], GI], writes=[SLt[4]])
            yield
            vop(lambda v: v.tensor_tensor(SLt[5].ap, SLt[3].ap, GR.ap,
                                          Alu.mult),
                reads=[SLt[3], GR], writes=[SLt[5]])
            yield
            vop(lambda v, ui=ui: v.scalar_tensor_tensor(
                SLt[6].ap, SLt[5].ap, -1.0, SLt[4].ap, Alu.mult, Alu.add,
                accum_out=st_yi.ap[:, ui:ui + 1]),
                reads=[SLt[5], SLt[4]], writes=[SLt[6], st_yi_c[ui]])
            yield
            sop(lambda s, ui=ui, u_s=u_s: s.activation(
                SLt[2].ap, u_s.ap, Act.Square,
                accum_out=st_q2.ap[:, 2 * ui:2 * ui + 1]),
                reads=[u_s], writes=[SLt[2], st_q2_c[2 * ui]])
            yield
            sop(lambda s, ui=ui: s.activation(
                SLt[3].ap, SLt[6].ap, Act.Square,
                accum_out=st_q2.ap[:, 2 * ui + 1:2 * ui + 2]),
                reads=[SLt[6]], writes=[SLt[3], st_q2_c[2 * ui + 1]])
            yield

    SKEW = 40
    regsA = load_panels(0)
    regsB = load_panels(1)
    for p in range(4):
        gens = [b_chain(2 * p, regsA), b_chain(2 * p + 1, regsB)]
        alive = [True, True]
        for _ in range(SKEW):
            try:
                next(gens[0])
            except StopIteration:
                alive[0] = False
        while any(alive):
            for gi_ in range(2):
                if alive[gi_]:
                    try:
                        next(gens[gi_])
                    except StopIteration:
                        alive[gi_] = False
        if p + 1 < 4:
            regsA = load_panels(2 * p + 2)
            regsB = load_panels(2 * p + 3)

    if debug_outputs:
        for k in range(NCT):
            tb = sl[0]
            dma("qd", lambda q, k=k, tb=tb: q.dma_start(
                out=tb.ap, in_=d_yr.ap[k * 128:(k + 1) * 128, 0:FD]),
                reads=[d_yr], writes=[tb])
            dma("qd", lambda q, k=k, tb=tb: q.dma_start(
                out=dbg["dbg_yr"].ap[k * 128:(k + 1) * 128, 0:FD], in_=tb.ap),
                reads=[tb], writes=[dbg["dbg_yr"]])
            tb2 = sl[1]
            dma("qd", lambda q, k=k, tb2=tb2: q.dma_start(
                out=tb2.ap, in_=d_yr.ap[k * 128:(k + 1) * 128, FD:T]),
                reads=[d_yr], writes=[tb2])
            dma("qd", lambda q, k=k, tb2=tb2: q.dma_start(
                out=dbg["dbg_yr"].ap[k * 128:(k + 1) * 128, FD:T], in_=tb2.ap),
                reads=[tb2], writes=[dbg["dbg_yr"]])

    for k in range(8):
        for half in range(2):
            r = wreg[2 * k + half]
            dma("gd", lambda g, k=k, half=half, r=r: g.dma_start(
                out=r.ap, in_=d_WinT.ap[k * 128:(k + 1) * 128,
                                        half * 1024:(half + 1) * 1024]),
                reads=[d_WinT], writes=[r])
    woutb = [sl[i] for i in range(8)]
    for wt in range(16):
        s_i, off = wt // 2, (wt % 2) * 1024
        dma("gd", lambda g, wt=wt, s_i=s_i, off=off: g.dma_start(
            out=bc(s_i)[:, off:off + 1024],
            in_=d_WoutT.ap[wt * 128:(wt + 1) * 128, :]),
            reads=[d_WoutT], writes=[sl[s_i]])

    stA = st_g[0]
    for ct in range(NCT):
        vop(lambda v, ct=ct: v.tensor_tensor(
            stA.ap[:, ct:ct + 1], st_yr.ap[:, 2 * ct:2 * ct + 1],
            st_yr.ap[:, 2 * ct + 1:2 * ct + 2], Alu.add),
            reads=[st_yr_c[2 * ct], st_yr_c[2 * ct + 1]], writes=[stA])
        vop(lambda v, ct=ct: v.tensor_tensor(
            stA.ap[:, 8 + ct:9 + ct], st_yi.ap[:, 2 * ct:2 * ct + 1],
            st_yi.ap[:, 2 * ct + 1:2 * ct + 2], Alu.add),
            reads=[st_yi_c[2 * ct], st_yi_c[2 * ct + 1]], writes=[stA])
        vop(lambda v, ct=ct: v.tensor_tensor(
            st_g[1].ap[:, ct:ct + 1], st_q2.ap[:, 4 * ct:4 * ct + 1],
            st_q2.ap[:, 4 * ct + 1:4 * ct + 2], Alu.add),
            reads=[st_q2_c[4 * ct], st_q2_c[4 * ct + 1]], writes=[st_g[1]])
        vop(lambda v, ct=ct: v.tensor_tensor(
            st_g[2].ap[:, ct:ct + 1], st_q2.ap[:, 4 * ct + 2:4 * ct + 3],
            st_q2.ap[:, 4 * ct + 3:4 * ct + 4], Alu.add),
            reads=[st_q2_c[4 * ct + 2], st_q2_c[4 * ct + 3]], writes=[st_g[2]])
        vop(lambda v, ct=ct: v.tensor_tensor(
            stA.ap[:, 16 + ct:17 + ct], st_g[1].ap[:, ct:ct + 1],
            st_g[2].ap[:, ct:ct + 1], Alu.add), reads=[st_g[1], st_g[2]], writes=[stA])
    cur = stA
    nxt = 3
    for step in (16, 8, 4, 2, 1):
        mask = [(i + step) % 32 for i in range(32)]
        shb, sm = st_g[nxt], st_g[nxt + 1]
        vop(lambda v, cur=cur, shb=shb, mask=mask: v.stream_shuffle(shb.ap, cur.ap, mask),
            reads=[cur], writes=[shb])
        vop(lambda v, cur=cur, shb=shb, sm=sm: v.tensor_tensor(
            sm.ap, cur.ap, shb.ap, Alu.add), reads=[cur, shb], writes=[sm])
        cur = sm
        nxt += 2
    dma("qd", lambda q, cur=cur: q.dma_start(out=d_ccin.ap, in_=cur.ap),
        reads=[cur], writes=[d_ccin])
    E.op("cc", lambda g: g.collective_compute(
        "AllReduce", Alu.add, replica_groups=[[0, 1, 2, 3], [4, 5, 6, 7]],
        ins=[d_ccin.ap], outs=[d_ccout.ap]), reads=[d_ccin], writes=[d_ccout])

    def g1(tt):
        for c in range(4):
            chunks = []
            for k in range(8):
                wr_i, woff = 2 * k + c // 2, (c % 2) * 512
                chunks.append(lambda t, k=k, tt=tt, ba=pb[c].ap, wr_i=wr_i,
                              woff=woff: t.matmul(
                    ba, big32ap[:, k * T + tt * 128:k * T + (tt + 1) * 128],
                    wreg[wr_i].ap[:, woff:woff + 512], start=(k == 0), stop=False))
            chunks.append(lambda t, c=c, ba=pb[c].ap: t.matmul(
                ba, onesrow.ap, binbf.ap[0:1, c * 512:(c + 1) * 512],
                start=False, stop=True))
            mm_acc(chunks, reads=b32 + [onesrow, binbf] + wreg, writes=[pb[c]])
        for c in range(4):
            vop(lambda v, c=c, aa=pb[c].ap: v.tensor_scalar(
                bc(8)[:, c * 512:(c + 1) * 512], aa, 0.0, None, Alu.is_gt),
                reads=[pb[c]], writes=[sl[8]])
            sop(lambda s, c=c, aa=pb[c].ap: s.activation(
                bc(11)[:, c * 512:(c + 1) * 512], aa, Act.Silu),
                reads=[pb[c]], writes=[sl[11]])

    def g_masks(tt):
        vop(lambda v: v.memset(bc(10), 0.0), writes=[sl[10]])
        dmr = bc(10).rearrange("p (r n) -> p r n", r=PAR)
        der = bc(8).rearrange("p (r n) -> p r n", r=PAR)
        vop(lambda v, dmr=dmr: v.memset(dmr[:, :, 0:1], 1.0), writes=[sl[10]])
        for lv in range(7):
            bp, bchld, npar = 2 ** lv - 1, 2 ** (lv + 1) - 1, 2 ** lv
            vop(lambda v, dmr=dmr, der=der, bp=bp, bchld=bchld, npar=npar:
                v.tensor_mul(
                dmr[:, :, bchld + 1:bchld + 2 * npar:2], dmr[:, :, bp:bp + npar],
                der[:, :, bp:bp + npar]), reads=[sl[10], sl[8]], writes=[sl[10]])
            vop(lambda v, dmr=dmr, bp=bp, bchld=bchld, npar=npar:
                v.tensor_tensor(
                dmr[:, :, bchld:bchld + 2 * npar:2], dmr[:, :, bp:bp + npar],
                dmr[:, :, bchld + 1:bchld + 2 * npar:2], Alu.subtract),
                reads=[sl[10]], writes=[sl[10]])
        vop(lambda v: v.tensor_mul(bc(8), bc(11), bc(10)),
            reads=[sl[11], sl[10], sl[8]], writes=[sl[8]])
        for grp4 in range(4):
            tb_ = pb[6 + grp4 % 2]
            chunks = []
            for j4 in range(4):
                wt = grp4 * 4 + j4
                chunks.append(lambda t, j4=j4, wt=wt, tb_=tb_: t.matmul(
                    tb_.ap[:, j4 * 128:(j4 + 1) * 128],
                    bc(8)[:, wt * 128:(wt + 1) * 128], idn.ap,
                    start=(j4 == 0), stop=(j4 == 3)))
            mm_acc(chunks, reads=[sl[8], idn], writes=[tb_])
            for j4 in range(4):
                wt = grp4 * 4 + j4
                s_i = 12 + wt // 8
                off = (wt % 8) * 256 + (tt % 2) * 128
                sop(lambda s, j4=j4, s_i=s_i, off=off, tb_=tb_: s.copy(
                    bc(s_i)[:, off:off + 128],
                    tb_.ap[:, j4 * 128:(j4 + 1) * 128]),
                    reads=[tb_], writes=[sl[s_i]])

    def mtT(wt):
        s_i = 12 + wt // 8
        off = (wt % 8) * 256
        return bc(s_i)[:, off:off + 256]

    def g2(tt0):
        for sweep in range(2):
            for dt4 in range(4):
                dt = sweep * 4 + dt4
                bank = pb[4 + dt4 // 2]
                coff = (dt4 % 2) * 256
                chunks = []
                for wt in range(16):
                    s_i, woff = wt // 2, (wt % 2) * 1024
                    st_f = (wt == 0 and dt4 % 2 == 0)
                    sp_f = (wt == 15 and dt4 % 2 == 1)
                    chunks.append(lambda t, wt=wt, dt=dt, bank=bank, coff=coff,
                                  s_i=s_i, woff=woff, st_f=st_f, sp_f=sp_f:
                                  t.matmul(
                        bank.ap[:, coff:coff + 256],
                        bc(s_i)[:, woff + dt * 128:woff + (dt + 1) * 128],
                        mtT(wt), start=st_f, stop=sp_f))
                mm_acc(chunks, reads=[sl[12], sl[13]] + woutb, writes=[bank])
            for b_i in range(2):
                vop(lambda v, b_i=b_i: v.tensor_copy(
                    sl[9].ap[:, b_i * 512:(b_i + 1) * 512], pb[4 + b_i].ap),
                    reads=[pb[4 + b_i]], writes=[sl[9]])
            dma("qd", lambda q, tt0=tt0, sweep=sweep: q.dma_start(
                out=d_ffn.ap.rearrange("(a p) t -> p a t", p=128)[
                    :, sweep * 4:(sweep + 1) * 4, tt0 * 128:(tt0 + 2) * 128],
                in_=sl[9].ap.rearrange("p (a t) -> p a t", a=4)),
                reads=[sl[9]], writes=[d_ffn])

    g1(0)
    for tt in range(16):
        g_masks(tt)
        if tt + 1 < 16:
            g1(tt + 1)
        if tt % 2 == 1:
            g2(tt - 1)
    if debug_outputs:
        for k in range(NCT):
            for h in range(2):
                tb = sl[9]
                dma("qd", lambda q, k=k, h=h, tb=tb: q.dma_start(
                    out=tb.ap, in_=d_ffn.ap[k * 128:(k + 1) * 128, h * FD:(h + 1) * FD]),
                    reads=[d_ffn], writes=[tb])
                dma("qd", lambda q, k=k, h=h, tb=tb: q.dma_start(
                    out=dbg["dbg_ffn"].ap[k * 128:(k + 1) * 128, h * FD:(h + 1) * FD],
                    in_=tb.ap), reads=[tb], writes=[dbg["dbg_ffn"]])

    stG = st_g[1]
    dma("qd", lambda q: q.dma_start(out=stG.ap, in_=d_ccout.ap),
        reads=[d_ccout], writes=[stG])
    if debug_outputs:
        dma("qd", lambda q: q.dma_start(out=dbg["dbg_st"].ap, in_=stG.ap),
            reads=[stG], writes=[dbg["dbg_st"]])
    mrg = st_g[2]
    for a, b_ in ((0, 8), (8, 16), (16, 24)):
        vop(lambda v, a=a, b_=b_: v.tensor_scalar(mrg.ap[:, a:b_], stG.ap[:, a:b_],
                                                  1.0 / NG, None, Alu.mult),
            reads=[stG], writes=[mrg])
    sqm = st_g[3]
    vop(lambda v: v.tensor_tensor(sqm.ap[:, 0:8], mrg.ap[:, 0:8], mrg.ap[:, 0:8], Alu.mult),
        reads=[mrg], writes=[sqm])
    vop(lambda v: v.tensor_tensor(sqm.ap[:, 8:16], mrg.ap[:, 8:16], mrg.ap[:, 8:16],
                                  Alu.mult), reads=[mrg], writes=[sqm])
    vop(lambda v: v.tensor_tensor(st_g[4].ap[:, 0:8], mrg.ap[:, 16:24], sqm.ap[:, 0:8],
                                  Alu.subtract), reads=[mrg, sqm], writes=[st_g[4]])
    vop(lambda v: v.tensor_tensor(st_g[5].ap[:, 0:8], st_g[4].ap[:, 0:8], sqm.ap[:, 8:16],
                                  Alu.subtract), reads=[st_g[4], sqm], writes=[st_g[5]])
    sop(lambda s: s.activation(st_g[6].ap[:, 0:8], st_g[5].ap[:, 0:8], Act.Sqrt,
                               bias=cst.ap[:, 1:2]), reads=[st_g[5], cst], writes=[st_g[6]])
    vop(lambda v: v.reciprocal(st_g[7].ap[:, 0:8], st_g[6].ap[:, 0:8]),
        reads=[st_g[6]], writes=[st_g[7]])
    vop(lambda v: v.tensor_tensor(gnA.ap, st_g[7].ap[:, 0:8], v_gns.ap, Alu.mult),
        reads=[st_g[7], v_gns], writes=[gnA])
    vop(lambda v: v.tensor_tensor(st_g[8].ap[:, 0:8], mrg.ap[:, 0:8], gnA.ap, Alu.mult),
        reads=[mrg, gnA], writes=[st_g[8]])
    vop(lambda v: v.tensor_tensor(gnB.ap, v_gnb.ap, st_g[8].ap[:, 0:8], Alu.subtract),
        reads=[v_gnb, st_g[8]], writes=[gnB])

    for k in range(8):
        dma("gd", lambda g, k=k: g.dma_start(
            out=wreg[k].ap, in_=d_Wo.ap[k * 128:(k + 1) * 128, :]),
            reads=[d_Wo], writes=[wreg[k]])
    for ct in range(NCT):
        dma("qd", lambda q, ct=ct: q.dma_start(
            out=bc(13), in_=d_sg.ap[ct * 128:(ct + 1) * 128, :]),
            reads=[d_sg], writes=[sl[13]])
        for h in range(2):
            SS = sl if ct % 2 == 0 else slB
            yn = SS[0] if h == 0 else SS[1]
            yrl = SS[2] if h == 0 else SS[3]
            dma("qd", lambda q, ct=ct, h=h, yrl=yrl: q.dma_start(
                out=yrl.ap, in_=d_yr.ap[ct * 128:(ct + 1) * 128,
                                        h * FD:(h + 1) * FD]),
                reads=[d_yr], writes=[yrl])
            pop(lambda g, ct=ct, h=h, yn=yn, yrl=yrl: g.tensor_scalar(
                yn.ap, yrl.ap,
                gnA.ap[:, ct:ct + 1], gnB.ap[:, ct:ct + 1], Alu.mult, Alu.add),
                reads=[yrl, gnA, gnB], writes=[yn])
            pop(lambda g, ct=ct, h=h, yn=yn: g.tensor_tensor(
                b32[ct].ap[:, h * FD:(h + 1) * FD],
                yn.ap, bc(13)[:, h * FD:(h + 1) * FD], Alu.mult),
                reads=[yn, sl[13]], writes=[b32[ct]])
    if debug_outputs:
        for k in range(NCT):
            for h in range(2):
                tb = sl[0]
                vop(lambda v, k=k, h=h, tb=tb: v.tensor_copy(
                    tb.ap.bitcast(BF16)[:, 0:FD],
                    b32[k].ap[:, h * FD:(h + 1) * FD]),
                    reads=[b32[k]], writes=[tb])
                tb2 = sl[1]
                vop(lambda v, tb=tb, tb2=tb2: v.tensor_copy(
                    tb2.ap, tb.ap.bitcast(BF16)[:, 0:FD]),
                    reads=[tb], writes=[tb2])
                dma("qd", lambda q, k=k, h=h, tb2=tb2: q.dma_start(
                    out=dbg["dbg_z"].ap[k * 128:(k + 1) * 128, h * FD:(h + 1) * FD],
                    in_=tb2.ap), reads=[tb2], writes=[dbg["dbg_z"]])
    for dt in range(8):
        for tc in range(4):
            bank = pb[(dt * 4 + tc) % 8]
            chunks = []
            for k in range(8):
                chunks.append(lambda t, dt=dt, tc=tc, k=k, bank=bank: t.matmul(
                    bank.ap, wreg[k].ap[:, dt * 128:(dt + 1) * 128],
                    big32ap[:, k * T + tc * 512:k * T + (tc + 1) * 512],
                    start=(k == 0), stop=(k == 7)))
            mm_acc(chunks, reads=b32 + wreg[:8], writes=[bank])
            xnc = sl[2 + (dt * 4 + tc) % 2]
            dma("qd", lambda q, dt=dt, tc=tc, xnc=xnc: q.dma_start(
                out=xnc.ap[:, 0:512],
                in_=d_xn.ap[dt * 128:(dt + 1) * 128, tc * 512:(tc + 1) * 512]),
                reads=[d_xn], writes=[xnc])
            vop(lambda v, dt=dt, tc=tc, bank=bank, xnc=xnc: v.tensor_tensor(
                b64[dt].ap[:, tc * 512:(tc + 1) * 512],
                bank.ap, xnc.ap[:, 0:512], Alu.add),
                reads=[bank, xnc], writes=[b64[dt]])
    if debug_outputs:
        for k in range(NCT):
            dma("qd", lambda q, k=k: q.dma_start(
                out=dbg["dbg_mix"].ap[k * 128:(k + 1) * 128, :],
                in_=b64[k].ap), reads=[b64[k]], writes=[dbg["dbg_mix"]])
    mu_h, iv_h = ln_rows(b64, 1)
    for k in range(NCT):
        for h in range(2):
            t0, t1_, xa = sl[0], sl[1], sl[9]
            vop(lambda v, k=k, h=h, t0=t0: v.tensor_tensor(
                t0.ap, b64[k].ap[:, h * FD:(h + 1) * FD],
                mu_h[h].ap, Alu.subtract), reads=[b64[k], mu_h[h]], writes=[t0])
            vop(lambda v, h=h, t0=t0, t1_=t1_: v.tensor_tensor(
                t1_.ap, t0.ap, iv_h[h].ap, Alu.mult),
                reads=[t0, iv_h[h]], writes=[t1_])
            pop(lambda g, k=k, t1_=t1_, xa=xa: g.tensor_scalar(
                xa.ap, t1_.ap, v_l1s.ap[:, k:k + 1], v_l1b.ap[:, k:k + 1],
                Alu.mult, Alu.add), reads=[t1_, v_l1s, v_l1b], writes=[xa])
            sop(lambda s, k=k, h=h, xa=xa: s.copy(
                b32[k].ap[:, h * FD:(h + 1) * FD], xa.ap),
                reads=[xa], writes=[b32[k]])
            dma("qd", lambda q, k=k, h=h, xa=xa: q.dma_start(
                out=d_xat.ap[k * 128:(k + 1) * 128, h * FD:(h + 1) * FD], in_=xa.ap),
                reads=[xa], writes=[d_xat])
    if debug_outputs:
        for k in range(NCT):
            for h in range(2):
                tb = sl[0]
                dma("qd", lambda q, k=k, h=h, tb=tb: q.dma_start(
                    out=tb.ap, in_=d_xat.ap[k * 128:(k + 1) * 128, h * FD:(h + 1) * FD]),
                    reads=[d_xat], writes=[tb])
                dma("qd", lambda q, k=k, h=h, tb=tb: q.dma_start(
                    out=dbg["dbg_xat"].ap[k * 128:(k + 1) * 128, h * FD:(h + 1) * FD],
                    in_=tb.ap), reads=[tb], writes=[dbg["dbg_xat"]])

    def ew(fq, r):
        if fq % 2 == 0:
            s_i = 6 + r // 2
            return sl[s_i], bc(s_i)[:, (r % 2) * 1024:(r % 2) * 1024 + 1024]
        return wreg[r], wreg[r].ap

    hsb = [[Buf(bc(2 + st * 2 + j // 4)[:, (j % 4) * 512:(j % 4) * 512 + 512],
                f"hs{st}_{j}") for j in range(8)] for st in range(2)]

    def hsre(tc, j):
        b = hsb[tc % 2][j]
        return b, b.ap

    def eload(fq):
        for r in range(16):
            wb_, wap = ew(fq, r)
            ft = fq * 8 + (r % 8)
            if r < 8:
                dma("gd", lambda g, ft=ft, wap=wap: g.dma_start(
                    out=wap, in_=d_W1.ap[ft]), reads=[d_W1], writes=[wb_])
            else:
                dma("gd", lambda g, ft=ft, wap=wap: g.dma_start(
                    out=wap, in_=d_W2.ap[ft * 128:(ft + 1) * 128, :]),
                    reads=[d_W2], writes=[wb_])

    eload(0)
    eload(1)
    import os as _os
    EFQ = int(_os.environ.get('KEFQ', '4'))
    for fq in range(EFQ):
        for tc in range(4):
            def w1mm(j, fq=fq, tc=tc):
                hbank = pb[j % 2]
                wb_, wap = ew(fq, j)
                chunks = []
                for k in range(8):
                    chunks.append(lambda t, k=k, hbank=hbank, wap=wap, tc=tc:
                                  t.matmul(
                        hbank.ap, wap[:, k * 128:(k + 1) * 128],
                        big32ap[:, k * T + tc * 512:k * T + (tc + 1) * 512],
                        start=(k == 0), stop=(k == 7)))
                mm_acc(chunks, reads=[wb_] + b32, writes=[hbank])

            def gelu(j, fq=fq, tc=tc):
                hbank = pb[j % 2]
                hb_, hap = hsre(tc, j)
                ft = fq * 8 + j
                sop(lambda s, hap=hap, hbank=hbank, ft=ft: s.activation(
                    hap, hbank.ap, Act.Gelu_apprx_tanh,
                    bias=v_b1.ap[:, ft:ft + 1]),
                    reads=[hbank, v_b1], writes=[hb_])

            def w2mm(j, dts, fq=fq, tc=tc):
                wb_, wap = ew(fq, 8 + j)
                hb_, hap = hsre(tc, j)
                for dt in dts:
                    bank = pb[2 + (dt if dt < 6 else dt - 6)]
                    E.op("t", lambda t, wap=wap, hap=hap, dt=dt, bank=bank, j=j:
                         t.matmul(bank.ap, wap[:, dt * 128:(dt + 1) * 128],
                                  hap, start=(j == 0), stop=(j == 7)),
                         reads=[wb_, hb_],
                         writes=([bank] if (j == 0 or j == 7) else []),
                         wtok_update=(j == 7))

            def evac(dt, bank, fq=fq, tc=tc):
                reg_ap = b64[dt].ap[:, tc * 512:(tc + 1) * 512]
                if fq == 0:
                    sop(lambda s, bank=bank, reg_ap=reg_ap: s.copy(
                        reg_ap, bank.ap), reads=[bank], writes=[b64[dt]])
                else:
                    tslot = sl[dt % 2]
                    toff = ((dt // 2) % 2) * 512
                    tap = tslot.ap[:, toff:toff + 512]
                    vop(lambda v, bank=bank, tap=tap, reg_ap=reg_ap:
                        v.tensor_tensor(tap, bank.ap, reg_ap, Alu.add),
                        reads=[bank, b64[dt]], writes=[tslot])
                    sop(lambda s, tap=tap, reg_ap=reg_ap: s.copy(reg_ap, tap),
                        reads=[tslot], writes=[b64[dt]])

            w1mm(0)
            for j in range(8):
                gelu(j)
                if j + 1 < 8:
                    w1mm(j + 1)
                w2mm(j, range(6))
            evac(0, pb[2])
            evac(1, pb[3])
            for j in range(8):
                w2mm(j, (6, 7))
            for dt in (2, 3, 4, 5):
                evac(dt, pb[2 + dt])
            evac(6, pb[2])
            evac(7, pb[3])
        if fq + 2 < EFQ:
            eload(fq + 2)

    if debug_outputs:
        for k in range(NCT):
            dma("qd", lambda q, k=k: q.dma_start(
                out=dbg["dbg_o2"].ap[k * 128:(k + 1) * 128, :],
                in_=b64[k].ap), reads=[b64[k]], writes=[dbg["dbg_o2"]])

    for k in range(NCT):
        for h in range(2):
            xac, t0 = sl[1], sl[9]
            dma("qd", lambda q, k=k, h=h, xac=xac: q.dma_start(
                out=xac.ap, in_=d_xat.ap[k * 128:(k + 1) * 128, h * FD:(h + 1) * FD]),
                reads=[d_xat], writes=[xac])
            pop(lambda g, k=k, h=h, t0=t0: g.tensor_scalar(
                t0.ap, b64[k].ap[:, h * FD:(h + 1) * FD],
                1.0, v_b2.ap[:, k:k + 1], Alu.mult, Alu.add),
                reads=[b64[k], v_b2], writes=[t0])
            vop(lambda v, k=k, h=h, xac=xac, t0=t0: v.tensor_tensor(
                b64[k].ap[:, h * FD:(h + 1) * FD], t0.ap, xac.ap, Alu.add),
                reads=[t0, xac], writes=[b64[k]])
    mu_h, iv_h = ln_rows(b64, 1)
    out_toks = []
    for k in range(NCT):
        for h in range(2):
            t0, t1_, af = sl[0], sl[1], sl[9]
            vop(lambda v, k=k, h=h, t0=t0: v.tensor_tensor(
                t0.ap, b64[k].ap[:, h * FD:(h + 1) * FD],
                mu_h[h].ap, Alu.subtract), reads=[b64[k], mu_h[h]], writes=[t0])
            vop(lambda v, h=h, t0=t0, t1_=t1_: v.tensor_tensor(
                t1_.ap, t0.ap, iv_h[h].ap, Alu.mult),
                reads=[t0, iv_h[h]], writes=[t1_])
            pop(lambda g, k=k, t1_=t1_, af=af: g.tensor_scalar(
                af.ap, t1_.ap, v_l2s.ap[:, k:k + 1], v_l2b.ap[:, k:k + 1],
                Alu.mult, Alu.add), reads=[t1_, v_l2s, v_l2b], writes=[af])
            fc, xc2 = sl[3], sl[4]
            dma("qd", lambda q, k=k, h=h, fc=fc: q.dma_start(
                out=fc.ap, in_=d_ffn.ap[k * 128:(k + 1) * 128, h * FD:(h + 1) * FD]),
                reads=[d_ffn], writes=[fc])
            dma("qd", lambda q, k=k, h=h, xc2=xc2: q.dma_start(
                out=xc2.ap, in_=d_xT.ap[k * 128:(k + 1) * 128, h * FD:(h + 1) * FD]),
                reads=[d_xT], writes=[xc2])
            t2_ = sl[10]
            vop(lambda v, fc=fc, af=af, t2_=t2_: v.tensor_tensor(
                t2_.ap, af.ap, fc.ap, Alu.add), reads=[af, fc], writes=[t2_])
            ot = sl[11]
            vop(lambda v, xc2=xc2, t2_=t2_, ot=ot: v.tensor_tensor(
                ot.ap, t2_.ap, xc2.ap, Alu.add), reads=[t2_, xc2], writes=[ot])
            tk = dma("qd", lambda q, k=k, h=h, ot=ot: q.dma_start(
                out=d_y.ap[k * 128:(k + 1) * 128, h * FD:(h + 1) * FD], in_=ot.ap),
                reads=[ot], writes=[d_y])
            out_toks.append(tk)
    E.wait_on("q", out_toks)

    sems = {s: ctx.enter_context(nc.semaphore(f"sem_{s}")) for s in SEMS}
    runners = E.play(sems)
    with nc.Block() as blk:
        blk.tensor(runners["t"])
        blk.vector(runners["v"])
        blk.scalar(runners["s"])
        blk.gpsimd(runners["g"])
        blk.sync(runners["q"])
    ctx.close()
    return nc


def _host_inputs(inputs):
    x = np.asarray(inputs["L0_attn_x"], np.float32)
    Wq = np.asarray(inputs["L0_attn_Wq"], np.float32)
    Wk = np.asarray(inputs["L0_attn_Wk"], np.float32)
    Wv = np.asarray(inputs["L0_attn_Wv"], np.float32)
    Wa = np.asarray(inputs["L0_attn_Wa"], np.float32)
    Wg = np.asarray(inputs["L0_attn_Wg"], np.float32)
    Wo = np.asarray(inputs["L0_attn_Wo"], np.float32)
    W1 = np.asarray(inputs["L0_attn_W1"], np.float32)
    W2 = np.asarray(inputs["L0_attn_W2"], np.float32)
    W_in = np.asarray(inputs["L0_ffn_W_in"], np.float32)
    W_out = np.asarray(inputs["L0_ffn_W_out"], np.float32)
    b_in = np.asarray(inputs["L0_ffn_b_in"], np.float32)

    def pan(w):
        K, N = w.shape
        kt, npan = K // 128, N // 128
        return np.ascontiguousarray(
            w.reshape(kt, 128, npan, 128).transpose(2, 1, 0, 3).reshape(npan, 128, kt * 128))

    def vec_pp(v, cols):
        return np.ascontiguousarray(np.asarray(v, np.float32).reshape(cols, 128).T)

    WinT = np.zeros((D, WPAD), np.float32)
    WinT.reshape(D, PAR, 256)[:, :, :255] = W_in.T.reshape(D, PAR, 255)
    WoutT = np.zeros((WPAD, D), np.float32)
    WoutT.reshape(PAR, 256, D)[:, :255, :] = W_out.T.reshape(PAR, 255, D)
    binrow = np.zeros((1, WPAD), np.float32)
    binrow.reshape(PAR, 256)[:, :255] = np.asarray(b_in, np.float32).reshape(PAR, 255)

    shared = {
        "Wq_p": pan(Wq), "Wk_p": pan(Wk), "Wv_p": pan(Wv), "Wg_p": pan(Wg),
        "Wa_p": pan(Wa), "W1_p": pan(W1),
        "Wo_r": np.ascontiguousarray(Wo), "W2_r": np.ascontiguousarray(W2),
        "WinT_r": np.ascontiguousarray(WinT), "WoutT_r": np.ascontiguousarray(WoutT),
        "gn_s": vec_pp(inputs["L0_attn_gn_scale"], 8),
        "gn_b": vec_pp(inputs["L0_attn_gn_bias"], 8),
        "ln1_s": vec_pp(inputs["L0_attn_ln1_scale"], 8),
        "ln1_b": vec_pp(inputs["L0_attn_ln1_bias"], 8),
        "ln2_s": vec_pp(inputs["L0_attn_ln2_scale"], 8),
        "ln2_b": vec_pp(inputs["L0_attn_ln2_bias"], 8),
        "b1_pp": vec_pp(inputs["L0_attn_b1"], 32),
        "b2_pp": vec_pp(inputs["L0_attn_b2"], 8),
        "binrow": binrow,
    }
    in_maps = []
    for c in range(8):
        b, qd = c // 4, c % 4
        m = dict(shared)
        m["xT"] = np.ascontiguousarray(x[b, qd * T:(qd + 1) * T, :].T)
        in_maps.append(m)
    return in_maps


def kernel(**inputs):
    in_maps = _host_inputs(inputs)
    nc = build_program()
    res = run_bass_kernel_spmd(nc, in_maps, core_ids=list(range(8)))
    out = np.empty((2, 8192, D), np.float32)
    for c in range(8):
        b, qd = c // 4, c % 4
        out[b, qd * T:(qd + 1) * T, :] = np.asarray(res.results[c]["y"]).T
    return out


# version 15 (final)
# speedup vs baseline: 1.0793x; 1.0131x over previous
import contextlib
import os
import numpy as np
import concourse.bass as bass
import concourse.mybir as mybir
from concourse.bass_utils import run_bass_kernel_spmd

F32 = mybir.dt.float32
BF16 = mybir.dt.bfloat16
Alu = mybir.AluOpType
Act = mybir.ActivationFunctionType

PI = float(np.pi)
TWO_PI = float(2 * np.pi)
HALF_PI = float(np.pi / 2)

D = 1024
T = 2048
NCT = 8
FD = 1024
FF = 4096
WPAD = 2048
PAR = 8
EPS_TOP = 1e-5
EPS_BLK = 1e-6
NG = float(8192 * 32)

NSLOTS_DMA = 8
_DMA_Q = ("qd", "sd", "gd")
SEMS = ("t", "v", "s", "g", "q", "cc") + tuple(
    f"{q}{i}" for q in _DMA_Q for i in range(NSLOTS_DMA))
ENG_OF = {"t": "t", "v": "v", "s": "s", "g": "g", "q": "q", "cc": "g"}
for _i in range(NSLOTS_DMA):
    ENG_OF[f"qd{_i}"] = "q"
    ENG_OF[f"sd{_i}"] = "s"
    ENG_OF[f"gd{_i}"] = "g"
DMA_SEMS = frozenset(f"{q}{i}" for q in _DMA_Q for i in range(NSLOTS_DMA))


class Buf:
    __slots__ = ("ap", "wtok", "rtoks", "name", "excl")

    def __init__(self, ap, name="", excl=False):
        self.ap = ap
        self.wtok = None
        self.rtoks = []
        self.name = name
        self.excl = excl


class Emitter:
    def __init__(self):
        self.prog = {e: [] for e in "tvsgq"}
        self.seq = {s: 0 for s in SEMS}
        self.needed = set()
        self.dma_rr = {q: 0 for q in _DMA_Q}
        self.dma_slot_buf = {f"{q}{i}": Buf(None, f"slot_{q}{i}")
                             for q in _DMA_Q for i in range(NSLOTS_DMA)}

    def op(self, sem, fn, reads=(), writes=(), wtok_update=True):
        eng = ENG_OF[sem]
        deps = set()
        for b in reads:
            if b.wtok is not None:
                deps.add(b.wtok)
            if b.excl:
                for t in b.rtoks:
                    if ENG_OF[t[0]] != eng:
                        deps.add(t)
        for b in writes:
            if b.wtok is not None:
                deps.add(b.wtok)
            deps.update(b.rtoks)
        tok = (sem, self.seq[sem])
        self.seq[sem] += 1
        deps.discard(tok)
        self.needed.update(deps)
        if sem in DMA_SEMS or sem == "cc":
            self.needed.add(tok)
        self.prog[eng].append((deps, fn, sem, tok))
        for b in writes:
            if wtok_update:
                b.wtok = tok
                b.rtoks = []
        for b in reads:
            if tok not in b.rtoks:
                b.rtoks.append(tok)
        return tok

    def wait_on(self, sem, toks):
        self.needed.update(toks)
        self.prog[ENG_OF[sem]].append((set(toks), None, sem, None))

    def play(self, sems):
        val_after = {}
        cnt = {s: 0 for s in SEMS}
        for e in "tvsgq":
            for deps, fn, sem, tok in self.prog[e]:
                if tok is not None and tok in self.needed and fn is not None:
                    cnt[sem] += 16 if sem in DMA_SEMS else 1
                if tok is not None:
                    val_after[tok] = cnt[sem]

        def make_runner(e):
            def run(engobj):
                watermark = {}
                for deps, fn, sem, tok in self.prog[e]:
                    for dd in sorted(deps):
                        ds, v = dd[0], val_after[dd]
                        if v > watermark.get(ds, 0):
                            engobj.wait_ge(sems[ds], v)
                            watermark[ds] = v
                    if fn is None:
                        continue
                    ins = fn(engobj)
                    if isinstance(ins, list):
                        ins = ins[-1]
                    if tok in self.needed:
                        ins.then_inc(sems[sem], 16 if sem in DMA_SEMS else 1)
            return run

        return {e: make_runner(e) for e in "tvsgq"}


def build_program(debug_outputs=False):
    nc = bass.Bass("TRN2", target_bir_lowering=False, num_devices=8)
    E = Emitter()
    ctx = contextlib.ExitStack()

    def din(name, shape, dt=F32):
        return Buf(nc.dram_tensor(name, list(shape), dt, kind="ExternalInput").ap(), name)

    d_xT = din("xT", [D, T])
    d_Wq = din("Wq_p", [8, 128, 1024])
    d_Wk = din("Wk_p", [8, 128, 1024])
    d_Wv = din("Wv_p", [8, 128, 1024])
    d_Wg = din("Wg_p", [8, 128, 1024])
    d_Wa = din("Wa_p", [16, 128, 1024])
    d_W1 = din("W1_p", [32, 128, 1024])
    d_Wo = din("Wo_r", [D, D])
    d_W2 = din("W2_r", [FF, D])
    d_WinT = din("WinT_r", [D, WPAD])
    d_WoutT = din("WoutT_r", [WPAD, D])
    d_gns = din("gn_s", [128, 8])
    d_gnb = din("gn_b", [128, 8])
    d_l1s = din("ln1_s", [128, 8])
    d_l1b = din("ln1_b", [128, 8])
    d_l2s = din("ln2_s", [128, 8])
    d_l2b = din("ln2_b", [128, 8])
    d_b1 = din("b1_pp", [128, 32])
    d_b2 = din("b2_pp", [128, 8])
    d_binrow = din("binrow", [1, WPAD])

    d_y = Buf(nc.dram_tensor("y", [D, T], F32, kind="ExternalOutput").ap(), "y")
    d_xn = Buf(nc.dram_tensor("xn_sc", [D, T], F32, kind="Internal").ap(), "xn_sc")
    d_xat = Buf(nc.dram_tensor("xat_sc", [D, T], F32, kind="Internal").ap(), "xat_sc")
    d_ffn = Buf(nc.dram_tensor("ffn_sc", [D, T], F32, kind="Internal").ap(), "ffn_sc")
    d_sg = Buf(nc.dram_tensor("sg_sc", [D, T], BF16, kind="Internal").ap(), "sg_sc")
    d_yr = Buf(nc.dram_tensor("yr_sc", [D, T], F32, kind="Internal").ap(), "yr_sc")
    d_o2 = Buf(nc.dram_tensor("o2_sc", [D, T], F32, kind="Internal").ap(), "o2_sc")
    d_ccin = Buf(nc.dram_tensor("cc_in", [128, 24], F32, kind="Internal").ap(), "cc_in")
    d_ccout = Buf(nc.dram_tensor("cc_out", [128, 24], F32, kind="Internal").ap(), "cc_out")
    dbg = {}
    if debug_outputs:
        for nm in ("dbg_xn", "dbg_yr", "dbg_z", "dbg_mix", "dbg_xat", "dbg_ffn", "dbg_o2"):
            dbg[nm] = Buf(nc.dram_tensor(nm, [D, T], F32, kind="ExternalOutput").ap(), nm)
        dbg["dbg_st"] = Buf(
            nc.dram_tensor("dbg_st", [128, 24], F32, kind="ExternalOutput").ap(), "dbg_st")

    def sbt(name, shape, dt=F32):
        h = ctx.enter_context(nc.sbuf_tensor(name, list(shape), dt))
        return Buf(h[:], name)

    big64_t = ctx.enter_context(nc.sbuf_tensor("big64", [128, NCT * T], F32))
    big32_t = ctx.enter_context(nc.sbuf_tensor("big32", [128, NCT * T], BF16))
    big64ap = big64_t[:]
    big32ap = big32_t[:]
    b64 = [Buf(big64ap[:, k * T:(k + 1) * T], f"b64_{k}") for k in range(NCT)]
    b32 = [Buf(big32ap[:, k * T:(k + 1) * T], f"b32_{k}") for k in range(NCT)]
    NSLOT = 14
    sl = [sbt(f"sl{i}", [128, FD]) for i in range(NSLOT)]
    wbuf_t = ctx.enter_context(nc.sbuf_tensor("wbuf", [128, 16384], BF16))
    wreg = [Buf(wbuf_t[:][:, i * 1024:(i + 1) * 1024], f"wreg{i}") for i in range(16)]
    r_mu = sbt("r_mu", [1, T])
    r_iv = sbt("r_iv", [1, T])
    onescol = sbt("onescol", [128, 1], BF16)
    onesrow = sbt("onesrow", [1, 128], BF16)
    onesrow_f = sbt("onesrow_f", [1, 128])
    binbf = sbt("binbf", [1, WPAD], BF16)
    cst = sbt("cst", [128, 6])
    v_gns = sbt("v_gns", [128, 8])
    v_gnb = sbt("v_gnb", [128, 8])
    v_l1s = sbt("v_l1s", [128, 8])
    v_l1b = sbt("v_l1b", [128, 8])
    v_l2s = sbt("v_l2s", [128, 8])
    v_l2b = sbt("v_l2b", [128, 8])
    v_b1 = sbt("v_b1", [128, 32])
    v_b2 = sbt("v_b2", [128, 8])
    st_yr = sbt("st_yr", [128, 16])
    st_yi = sbt("st_yi", [128, 16])
    st_q2 = sbt("st_q2", [128, 32])
    st_yr_c = [Buf(st_yr.ap[:, i:i + 1], f"styr{i}") for i in range(16)]
    st_yi_c = [Buf(st_yi.ap[:, i:i + 1], f"styi{i}") for i in range(16)]
    st_q2_c = [Buf(st_q2.ap[:, i:i + 1], f"stq2{i}") for i in range(32)]
    st_g = [sbt(f"st_g{i}", [128, 24]) for i in range(14)]
    gnA = sbt("gnA", [128, 8])
    gnB = sbt("gnB", [128, 8])
    carry = [sbt(f"carry{i}", [128, 1]) for i in range(3)]
    carryO = [sbt(f"carryO{i}", [128, 1]) for i in range(3)]
    slB = [Buf(big64ap[:, i * FD:(i + 1) * FD], f"slB{i}") for i in range(14)]
    pb = [Buf(ctx.enter_context(nc.psum_tensor(f"pb{i}", [128, 512], F32))[:], f"pb{i}",
              excl=True) for i in range(8)]
    idn = sbt("idn", [128, 128], BF16)

    def bc(i):
        return sl[i].ap.bitcast(BF16)

    def row_ap(buf, c):
        return buf.ap[0:1, c * 512:(c + 1) * 512]

    def vop(fn, reads=(), writes=()):
        return E.op("v", fn, reads, writes)

    def sop(fn, reads=(), writes=()):
        return E.op("s", fn, reads, writes)

    def top(fn, reads=(), writes=()):
        return E.op("t", fn, reads, writes)

    def pop(fn, reads=(), writes=()):
        return E.op("g", fn, reads, writes)

    def dma(q, fn, reads=(), writes=()):
        slot = E.dma_rr[q] % NSLOTS_DMA
        E.dma_rr[q] += 1
        sem = f"{q}{slot}"
        sbuf = E.dma_slot_buf[sem]
        return E.op(sem, fn, list(reads), list(writes) + [sbuf])

    def mm_acc(chunks, reads, writes):
        n = len(chunks)
        for i, f in enumerate(chunks):
            if i == 0:
                E.op("t", f, reads=reads, writes=writes, wtok_update=(n == 1))
            elif i == n - 1:
                E.op("t", f, reads=reads, writes=writes)
            else:
                E.op("t", f, reads=(), writes=())

    vop(lambda v: v.memset(onescol.ap, 1.0), writes=[onescol])
    vop(lambda v: v.memset(cst.ap[:, 0:1], EPS_TOP), writes=[cst])
    vop(lambda v: v.memset(cst.ap[:, 1:2], EPS_BLK), writes=[cst])
    vop(lambda v: v.memset(cst.ap[:, 2:3], 1e-38), writes=[cst])
    vop(lambda v: v.memset(cst.ap[:, 3:4], 0.5), writes=[cst])
    vop(lambda v: v.memset(cst.ap[:, 4:5], HALF_PI), writes=[cst])
    vop(lambda v: v.memset(cst.ap[:, 5:6], -0.5), writes=[cst])
    pop(lambda g: g.iota(sl[0].ap[:, 0:128], pattern=[[-1, 128]], base=0,
                         channel_multiplier=1,
                         allow_small_or_imprecise_dtypes=True), writes=[sl[0]])
    vop(lambda v: v.tensor_scalar(idn.ap, sl[0].ap[:, 0:128], 0.0, None,
                                  Alu.is_equal),
        reads=[sl[0]], writes=[idn])
    vop(lambda v: v.memset(onesrow.ap, 1.0), writes=[onesrow])
    vop(lambda v: v.memset(onesrow_f.ap, 1.0), writes=[onesrow_f])
    for buf, src in [(v_gns, d_gns), (v_gnb, d_gnb), (v_l1s, d_l1s), (v_l1b, d_l1b),
                     (v_l2s, d_l2s), (v_l2b, d_l2b), (v_b1, d_b1), (v_b2, d_b2)]:
        dma("qd", lambda q, b=buf, s=src: q.dma_start(out=b.ap, in_=s.ap),
            reads=[src], writes=[buf])
    dma("gd", lambda g: g.dma_start(out=binbf.ap, in_=d_binrow.ap),
        reads=[d_binrow], writes=[binbf])

    def ln_rows(src, eps_col):
        for k in range(NCT):
            cbuf, qbuf = sl[2], sl[3]
            sop(lambda s, k=k: s.copy(bc(2), src[k].ap),
                reads=[src[k]], writes=[cbuf])
            sop(lambda s, k=k: s.activation(bc(3), src[k].ap, Act.Square),
                reads=[src[k]], writes=[qbuf])
            for c in range(4):
                top(lambda t, k=k, c=c: t.matmul(
                    pb[c].ap[0:1, :], onescol.ap, bc(2)[:, c * 512:(c + 1) * 512],
                    start=(k == 0), stop=(k == NCT - 1)),
                    reads=[cbuf, onescol],
                    writes=([pb[c]] if k in (0, NCT - 1) else []))
                top(lambda t, k=k, c=c: t.matmul(
                    pb[4 + c].ap[0:1, :], onescol.ap, bc(3)[:, c * 512:(c + 1) * 512],
                    start=(k == 0), stop=(k == NCT - 1)),
                    reads=[qbuf, onescol],
                    writes=([pb[4 + c]] if k in (0, NCT - 1) else []))
        for c in range(4):
            vop(lambda v, c=c: v.tensor_scalar(row_ap(r_mu, c), pb[c].ap[0:1, :],
                                               1.0 / D, None, Alu.mult),
                reads=[pb[c]], writes=[r_mu])
            s2d = Buf(sl[2].ap[0:1, 0:512], "s2d")
            msq = Buf(sl[3].ap[0:1, 0:512], "msq")
            vrr = Buf(sl[4].ap[0:1, 0:512], "vrr")
            sdr = Buf(sl[2].ap[0:1, 512:1024], "sdr")
            vop(lambda v, c=c, s2d=s2d: v.tensor_scalar(s2d.ap, pb[4 + c].ap[0:1, :],
                                                        1.0 / D, None, Alu.mult),
                reads=[pb[4 + c]], writes=[s2d, sl[2]])
            vop(lambda v, c=c, msq=msq: v.tensor_tensor(msq.ap, row_ap(r_mu, c),
                                                        row_ap(r_mu, c), Alu.mult),
                reads=[r_mu], writes=[msq, sl[3]])
            vop(lambda v, s2d=s2d, msq=msq, vrr=vrr: v.tensor_tensor(
                vrr.ap, s2d.ap, msq.ap, Alu.subtract),
                reads=[s2d, msq, sl[2], sl[3]], writes=[vrr, sl[4]])
            sop(lambda s, c=c, vrr=vrr: s.activation(
                row_ap(r_iv, c), vrr.ap, Act.Sqrt,
                bias=cst.ap[0:1, eps_col:eps_col + 1]),
                reads=[vrr, sl[4], cst], writes=[r_iv])
        for c in range(4):
            top(lambda t, c=c: t.matmul(pb[c].ap, onesrow_f.ap, row_ap(r_mu, c),
                                        start=True, stop=True),
                reads=[r_mu, onesrow_f], writes=[pb[c]])
            top(lambda t, c=c: t.matmul(pb[4 + c].ap, onesrow_f.ap, row_ap(r_iv, c),
                                        start=True, stop=True),
                reads=[r_iv, onesrow_f], writes=[pb[4 + c]])
        for c in range(4):
            dst_m = sl[5 + c // 2]
            dst_i = sl[7 + c // 2]
            vop(lambda v, c=c, dst_m=dst_m: v.tensor_copy(
                dst_m.ap[:, (c % 2) * 512:(c % 2) * 512 + 512], pb[c].ap),
                reads=[pb[c]], writes=[dst_m])
            vop(lambda v, c=c, dst_i=dst_i: v.reciprocal(
                dst_i.ap[:, (c % 2) * 512:(c % 2) * 512 + 512], pb[4 + c].ap),
                reads=[pb[4 + c]], writes=[dst_i])
        return (sl[5], sl[6]), (sl[7], sl[8])

    for k in range(NCT):
        dma("qd", lambda q, k=k: q.dma_start(
            out=b64[k].ap, in_=d_xT.ap[k * 128:(k + 1) * 128, :]),
            reads=[d_xT], writes=[b64[k]])
    mu_h, iv_h = ln_rows(b64, 0)
    for k in range(NCT):
        for h in range(2):
            t0, xnf = sl[0], sl[1]
            vop(lambda v, k=k, h=h, t0=t0: v.tensor_tensor(
                t0.ap, b64[k].ap[:, h * FD:(h + 1) * FD],
                mu_h[h].ap, Alu.subtract), reads=[b64[k], mu_h[h]], writes=[t0])
            vop(lambda v, h=h, t0=t0, xnf=xnf: v.tensor_tensor(
                xnf.ap, t0.ap, iv_h[h].ap, Alu.mult),
                reads=[t0, iv_h[h]], writes=[xnf])
            sop(lambda s, k=k, h=h, xnf=xnf: s.copy(
                b32[k].ap[:, h * FD:(h + 1) * FD], xnf.ap),
                reads=[xnf], writes=[b32[k]])
            dma("qd", lambda q, k=k, h=h, xnf=xnf: q.dma_start(
                out=d_xn.ap[k * 128:(k + 1) * 128, h * FD:(h + 1) * FD], in_=xnf.ap),
                reads=[xnf], writes=[d_xn])
    if debug_outputs:
        for k in range(NCT):
            dma("qd", lambda q, k=k: q.dma_start(
                out=dbg["dbg_xn"].ap[k * 128:(k + 1) * 128, :],
                in_=d_xn.ap[k * 128:(k + 1) * 128, :]),
                reads=[d_xn], writes=[dbg["dbg_xn"]])

    PANELS = [("q", d_Wq, None), ("k", d_Wk, None), ("v", d_Wv, None),
              ("g", d_Wg, None), ("ar", d_Wa, 0), ("ai", d_Wa, 8)]

    def load_panels(ct):
        base = 0 if ct % 2 == 0 else 6
        regs = []
        for j, (_, dten, off) in enumerate(PANELS):
            idx = ct if off is None else off + ct
            r = wreg[base + j]
            dma("gd", lambda g, idx=idx, dten=dten, r=r: g.dma_start(
                out=r.ap, in_=dten.ap[idx]), reads=[dten], writes=[r])
            regs.append(r)
        return regs

    def b_chain(ct, regs):
        SLt = sl if ct % 2 == 0 else slB
        car = carry if ct % 2 == 0 else carryO
        grp = pb[0:4] if ct % 2 == 0 else pb[4:8]

        def bcx(i):
            return SLt[i].ap.bitcast(BF16)

        TH, GR, GI = SLt[10], SLt[11], SLt[12]
        u_s, q_s = SLt[0], SLt[1]
        for h in range(2):
            def rhs(k, c, h=h):
                return big32ap[:, k * T + h * FD + c * 512:
                               k * T + h * FD + (c + 1) * 512]

            def mixer_mm(pj, b0, b1, h=h):
                for c, bank in ((0, b0), (1, b1)):
                    chunks = []
                    for k in range(8):
                        lap = regs[pj].ap[:, k * 128:(k + 1) * 128]
                        rap = rhs(k, c)
                        chunks.append(lambda t, k=k, bank=bank, lap=lap,
                                      rap=rap: t.matmul(
                            bank.ap, lap, rap, start=(k == 0), stop=(k == 7)))
                    mm_acc(chunks, reads=[regs[pj]] + b32, writes=[bank])

            mixer_mm(1, grp[0], grp[1])
            mixer_mm(2, grp[2], grp[3])
            yield
            for c in range(2):
                sop(lambda s, c=c, va=grp[2 + c].ap: s.copy(
                    SLt[2].ap[:, c * 512:(c + 1) * 512], va),
                    reads=[grp[2 + c]], writes=[SLt[2]])
                yield
            for c in range(2):
                vop(lambda v, c=c, u_s=u_s, ka=grp[c].ap: v.tensor_tensor(
                    u_s.ap[:, c * 512:(c + 1) * 512], ka,
                    SLt[2].ap[:, c * 512:(c + 1) * 512], Alu.mult),
                    reads=[grp[c], SLt[2]], writes=[u_s])
                yield
            mixer_mm(4, grp[0], grp[1])
            mixer_mm(5, grp[2], grp[3])
            yield
            for c in range(2):
                sop(lambda s, c=c, aa=grp[c].ap: s.activation(
                    SLt[3].ap[:, c * 512:(c + 1) * 512], aa, Act.Square),
                    reads=[grp[c]], writes=[SLt[3]])
                yield
                sop(lambda s, c=c, aa=grp[2 + c].ap: s.activation(
                    SLt[4].ap[:, c * 512:(c + 1) * 512], aa, Act.Square),
                    reads=[grp[2 + c]], writes=[SLt[4]])
                yield
                sop(lambda s, c=c, aa=grp[c].ap: s.sign(
                    bcx(5)[:, c * 512:(c + 1) * 512], aa),
                    reads=[grp[c]], writes=[SLt[5]])
                yield
                sop(lambda s, c=c, aa=grp[2 + c].ap: s.sign(
                    bcx(5)[:, 1024 + c * 512:1024 + (c + 1) * 512], aa),
                    reads=[grp[2 + c]], writes=[SLt[5]])
                yield
            mixer_mm(0, grp[0], grp[1])
            mixer_mm(3, grp[2], grp[3])
            yield

            vop(lambda v: v.tensor_tensor(SLt[2].ap, SLt[3].ap, SLt[4].ap,
                                          Alu.add),
                reads=[SLt[3], SLt[4]], writes=[SLt[2]])
            yield
            vop(lambda v: v.tensor_tensor(bcx(6)[:, 0:1024], SLt[4].ap,
                                          SLt[3].ap, Alu.is_gt),
                reads=[SLt[3], SLt[4]], writes=[SLt[6]])
            yield
            sop(lambda s: s.activation(SLt[1].ap, SLt[3].ap, Act.Ln,
                                       bias=cst.ap[:, 2:3]),
                reads=[SLt[3], cst], writes=[SLt[1]])
            yield
            sop(lambda s: s.activation(SLt[3].ap, SLt[4].ap, Act.Ln,
                                       bias=cst.ap[:, 2:3]),
                reads=[SLt[4], cst], writes=[SLt[3]])
            yield
            vop(lambda v: v.tensor_tensor(SLt[4].ap, SLt[1].ap, SLt[3].ap,
                                          Alu.subtract),
                reads=[SLt[1], SLt[3]], writes=[SLt[4]])
            yield
            sop(lambda s: s.activation(SLt[1].ap, SLt[4].ap, Act.Abs),
                reads=[SLt[4]], writes=[SLt[1]])
            yield
            sop(lambda s: s.activation(SLt[3].ap, SLt[1].ap, Act.Exp,
                                       scale=cst.ap[:, 5:6]),
                reads=[SLt[1], cst], writes=[SLt[3]])
            yield
            sop(lambda s: s.activation(SLt[4].ap, SLt[2].ap, Act.Ln,
                                       bias=cst.ap[:, 2:3]),
                reads=[SLt[2], cst], writes=[SLt[4]])
            yield
            sop(lambda s: s.activation(SLt[2].ap, SLt[4].ap, Act.Exp,
                                       scale=cst.ap[:, 3:4]),
                reads=[SLt[4], cst], writes=[SLt[2]])
            yield
            sop(lambda s: s.activation(SLt[7].ap, SLt[2].ap, Act.Sigmoid),
                reads=[SLt[2]], writes=[SLt[7]])
            yield
            sop(lambda s: s.activation(SLt[1].ap, SLt[3].ap, Act.Arctan),
                reads=[SLt[3]], writes=[SLt[1]])
            yield
            for c in range(2):
                sop(lambda s, c=c, aa=grp[2 + c].ap: s.activation(
                    bcx(13)[:, c * 512:(c + 1) * 512], aa, Act.Silu),
                    reads=[grp[2 + c]], writes=[SLt[13]])
                yield
            dma("qd", lambda q, ct=ct, h=h: q.dma_start(
                out=d_sg.ap[ct * 128:(ct + 1) * 128, h * FD:(h + 1) * FD],
                in_=bcx(13)[:, 0:FD]), reads=[SLt[13]], writes=[d_sg])
            yield
            pop(lambda g: g.tensor_scalar(SLt[2].ap, SLt[1].ap, 1.0, -HALF_PI,
                                          Alu.mult, Alu.add),
                reads=[SLt[1]], writes=[SLt[2]])
            yield
            pop(lambda g: g.tensor_scalar(SLt[3].ap, SLt[1].ap, -2.0, HALF_PI,
                                          Alu.mult, Alu.add),
                reads=[SLt[1]], writes=[SLt[3]])
            yield
            pop(lambda g: g.tensor_tensor(bcx(6)[:, 1024:2048],
                                          bcx(5)[:, 0:1024],
                                          bcx(5)[:, 1024:2048], Alu.mult),
                reads=[SLt[5]], writes=[SLt[6]])
            yield
            vop(lambda v: v.tensor_tensor(SLt[4].ap, bcx(6)[:, 0:1024],
                                          SLt[3].ap, Alu.mult),
                reads=[SLt[6], SLt[3]], writes=[SLt[4]])
            yield
            vop(lambda v: v.tensor_tensor(SLt[3].ap, SLt[4].ap, SLt[2].ap,
                                          Alu.add),
                reads=[SLt[4], SLt[2]], writes=[SLt[3]])
            yield
            vop(lambda v: v.tensor_tensor(SLt[2].ap, bcx(6)[:, 1024:2048],
                                          SLt[3].ap, Alu.mult),
                reads=[SLt[6], SLt[3]], writes=[SLt[2]])
            yield
            vop(lambda v: v.scalar_tensor_tensor(SLt[4].ap,
                                                 bcx(5)[:, 1024:2048],
                                                 HALF_PI, SLt[2].ap,
                                                 Alu.mult, Alu.add),
                reads=[SLt[5], SLt[2]], writes=[SLt[4]])
            yield

            if h == 0:
                vop(lambda v: v.tensor_tensor_scan(TH.ap, SLt[4].ap, SLt[4].ap,
                                                   0.0, Alu.add, Alu.bypass),
                    reads=[SLt[4]], writes=[TH])
                yield
                vop(lambda v: v.tensor_copy(car[0].ap, TH.ap[:, FD - 1:FD]),
                    reads=[TH], writes=[car[0]])
                yield
            else:
                vop(lambda v: v.tensor_tensor_scan(TH.ap, SLt[4].ap, SLt[4].ap,
                                                   car[0].ap[:, 0:1], Alu.add,
                                                   Alu.bypass),
                    reads=[SLt[4], car[0]], writes=[TH])
                yield

            MAGIC = 12582912.0
            pop(lambda g: g.tensor_scalar(SLt[3].ap, TH.ap, 1.0 / TWO_PI, MAGIC,
                                          Alu.mult, Alu.add),
                reads=[TH], writes=[SLt[3]])
            yield
            pop(lambda g: g.tensor_scalar(SLt[2].ap, TH.ap, 1.0 / TWO_PI, 0.25,
                                          Alu.mult, Alu.add),
                reads=[TH], writes=[SLt[2]])
            yield
            pop(lambda g: g.tensor_scalar(SLt[4].ap, SLt[2].ap, 1.0, MAGIC,
                                          Alu.mult, Alu.add),
                reads=[SLt[2]], writes=[SLt[4]])
            yield
            pop(lambda g: g.tensor_scalar(SLt[2].ap, SLt[3].ap, 1.0, -MAGIC,
                                          Alu.mult, Alu.add),
                reads=[SLt[3]], writes=[SLt[2]])
            yield
            pop(lambda g: g.tensor_scalar(SLt[3].ap, SLt[4].ap, 1.0, -MAGIC,
                                          Alu.mult, Alu.add),
                reads=[SLt[4]], writes=[SLt[3]])
            yield
            vop(lambda v: v.scalar_tensor_tensor(SLt[4].ap, SLt[2].ap, -TWO_PI,
                                                 TH.ap, Alu.mult, Alu.add),
                reads=[SLt[2], TH], writes=[SLt[4]])
            yield
            vop(lambda v: v.tensor_scalar(SLt[2].ap, SLt[4].ap, -PI, PI,
                                          Alu.max, Alu.min),
                reads=[SLt[4]], writes=[SLt[2]])
            yield
            vop(lambda v: v.scalar_tensor_tensor(SLt[4].ap, SLt[3].ap, -TWO_PI,
                                                 TH.ap, Alu.mult, Alu.add),
                reads=[SLt[3], TH], writes=[SLt[4]])
            yield
            vop(lambda v: v.tensor_scalar(SLt[3].ap, SLt[4].ap,
                                          -3.0 * HALF_PI, HALF_PI,
                                          Alu.max, Alu.min),
                reads=[SLt[4]], writes=[SLt[3]])
            yield
            sop(lambda s: s.activation(SLt[8].ap, SLt[2].ap, Act.Sin),
                reads=[SLt[2]], writes=[SLt[8]])
            yield
            sop(lambda s: s.activation(SLt[9].ap, SLt[3].ap, Act.Sin,
                                       bias=cst.ap[:, 4:5]),
                reads=[SLt[3], cst], writes=[SLt[9]])
            yield

            vop(lambda v, u_s=u_s: v.tensor_tensor(SLt[2].ap, u_s.ap,
                                                   SLt[9].ap, Alu.mult),
                reads=[u_s, SLt[9]], writes=[SLt[2]])
            yield
            vop(lambda v, u_s=u_s: v.scalar_tensor_tensor(
                SLt[3].ap, u_s.ap, -1.0, SLt[8].ap, Alu.mult, Alu.mult),
                reads=[u_s, SLt[8]], writes=[SLt[3]])
            yield
            if h == 0:
                vop(lambda v: v.tensor_tensor_scan(GR.ap, SLt[7].ap, SLt[2].ap,
                                                   0.0, Alu.mult, Alu.add),
                    reads=[SLt[7], SLt[2]], writes=[GR])
                yield
                vop(lambda v: v.tensor_tensor_scan(GI.ap, SLt[7].ap, SLt[3].ap,
                                                   0.0, Alu.mult, Alu.add),
                    reads=[SLt[7], SLt[3]], writes=[GI])
                yield
                vop(lambda v: v.tensor_copy(car[1].ap, GR.ap[:, FD - 1:FD]),
                    reads=[GR], writes=[car[1]])
                yield
                vop(lambda v: v.tensor_copy(car[2].ap, GI.ap[:, FD - 1:FD]),
                    reads=[GI], writes=[car[2]])
                yield
            else:
                vop(lambda v: v.tensor_tensor_scan(GR.ap, SLt[7].ap, SLt[2].ap,
                                                   car[1].ap[:, 0:1], Alu.mult,
                                                   Alu.add),
                    reads=[SLt[7], SLt[2], car[1]], writes=[GR])
                yield
                vop(lambda v: v.tensor_tensor_scan(GI.ap, SLt[7].ap, SLt[3].ap,
                                                   car[2].ap[:, 0:1], Alu.mult,
                                                   Alu.add),
                    reads=[SLt[7], SLt[3], car[2]], writes=[GI])
                yield

            for c in range(2):
                sop(lambda s, c=c, q_s=q_s, aa=grp[c].ap: s.copy(
                    q_s.ap[:, c * 512:(c + 1) * 512], aa),
                    reads=[grp[c]], writes=[q_s])
                yield

            ui = 2 * ct + h
            vop(lambda v, q_s=q_s: v.tensor_tensor(SLt[2].ap, q_s.ap,
                                                   SLt[9].ap, Alu.mult),
                reads=[q_s, SLt[9]], writes=[SLt[2]])
            yield
            vop(lambda v, q_s=q_s: v.scalar_tensor_tensor(
                SLt[3].ap, q_s.ap, -1.0, SLt[8].ap, Alu.mult, Alu.mult),
                reads=[q_s, SLt[8]], writes=[SLt[3]])
            yield
            vop(lambda v: v.tensor_tensor(SLt[4].ap, SLt[2].ap, GR.ap,
                                          Alu.mult),
                reads=[SLt[2], GR], writes=[SLt[4]])
            yield
            vop(lambda v: v.tensor_tensor(SLt[5].ap, SLt[3].ap, GI.ap,
                                          Alu.mult),
                reads=[SLt[3], GI], writes=[SLt[5]])
            yield
            vop(lambda v, ui=ui: v.scalar_tensor_tensor(
                SLt[0].ap, SLt[4].ap, 1.0, SLt[5].ap, Alu.mult, Alu.add,
                accum_out=st_yr.ap[:, ui:ui + 1]),
                reads=[SLt[4], SLt[5]], writes=[SLt[0], st_yr_c[ui]])
            yield
            dma("qd", lambda q, ct=ct, h=h, u_s=u_s: q.dma_start(
                out=d_yr.ap[ct * 128:(ct + 1) * 128, h * FD:(h + 1) * FD],
                in_=u_s.ap), reads=[u_s], writes=[d_yr])
            yield
            vop(lambda v: v.tensor_tensor(SLt[4].ap, SLt[2].ap, GI.ap,
                                          Alu.mult),
                reads=[SLt[2], GI], writes=[SLt[4]])
            yield
            vop(lambda v: v.tensor_tensor(SLt[5].ap, SLt[3].ap, GR.ap,
                                          Alu.mult),
                reads=[SLt[3], GR], writes=[SLt[5]])
            yield
            vop(lambda v, ui=ui: v.scalar_tensor_tensor(
                SLt[6].ap, SLt[5].ap, -1.0, SLt[4].ap, Alu.mult, Alu.add,
                accum_out=st_yi.ap[:, ui:ui + 1]),
                reads=[SLt[5], SLt[4]], writes=[SLt[6], st_yi_c[ui]])
            yield
            sop(lambda s, ui=ui, u_s=u_s: s.activation(
                SLt[2].ap, u_s.ap, Act.Square,
                accum_out=st_q2.ap[:, 2 * ui:2 * ui + 1]),
                reads=[u_s], writes=[SLt[2], st_q2_c[2 * ui]])
            yield
            sop(lambda s, ui=ui: s.activation(
                SLt[3].ap, SLt[6].ap, Act.Square,
                accum_out=st_q2.ap[:, 2 * ui + 1:2 * ui + 2]),
                reads=[SLt[6]], writes=[SLt[3], st_q2_c[2 * ui + 1]])
            yield

    SKEW = 27
    regsA = load_panels(0)
    regsB = load_panels(1)
    for p in range(4):
        gens = [b_chain(2 * p, regsA), b_chain(2 * p + 1, regsB)]
        alive = [True, True]
        for _ in range(SKEW):
            try:
                next(gens[0])
            except StopIteration:
                alive[0] = False
        while any(alive):
            for gi_ in range(2):
                if alive[gi_]:
                    try:
                        next(gens[gi_])
                    except StopIteration:
                        alive[gi_] = False
        if p + 1 < 4:
            regsA = load_panels(2 * p + 2)
            regsB = load_panels(2 * p + 3)

    if debug_outputs:
        for k in range(NCT):
            tb = sl[0]
            dma("qd", lambda q, k=k, tb=tb: q.dma_start(
                out=tb.ap, in_=d_yr.ap[k * 128:(k + 1) * 128, 0:FD]),
                reads=[d_yr], writes=[tb])
            dma("qd", lambda q, k=k, tb=tb: q.dma_start(
                out=dbg["dbg_yr"].ap[k * 128:(k + 1) * 128, 0:FD], in_=tb.ap),
                reads=[tb], writes=[dbg["dbg_yr"]])
            tb2 = sl[1]
            dma("qd", lambda q, k=k, tb2=tb2: q.dma_start(
                out=tb2.ap, in_=d_yr.ap[k * 128:(k + 1) * 128, FD:T]),
                reads=[d_yr], writes=[tb2])
            dma("qd", lambda q, k=k, tb2=tb2: q.dma_start(
                out=dbg["dbg_yr"].ap[k * 128:(k + 1) * 128, FD:T], in_=tb2.ap),
                reads=[tb2], writes=[dbg["dbg_yr"]])

    for k in range(8):
        for half in range(2):
            r = wreg[2 * k + half]
            dma("gd", lambda g, k=k, half=half, r=r: g.dma_start(
                out=r.ap, in_=d_WinT.ap[k * 128:(k + 1) * 128,
                                        half * 1024:(half + 1) * 1024]),
                reads=[d_WinT], writes=[r])
    woutb = [sl[i] for i in range(8)]
    for wt in range(16):
        s_i, off = wt // 2, (wt % 2) * 1024
        dma("gd", lambda g, wt=wt, s_i=s_i, off=off: g.dma_start(
            out=bc(s_i)[:, off:off + 1024],
            in_=d_WoutT.ap[wt * 128:(wt + 1) * 128, :]),
            reads=[d_WoutT], writes=[sl[s_i]])

    stA = st_g[0]
    for ct in range(NCT):
        vop(lambda v, ct=ct: v.tensor_tensor(
            stA.ap[:, ct:ct + 1], st_yr.ap[:, 2 * ct:2 * ct + 1],
            st_yr.ap[:, 2 * ct + 1:2 * ct + 2], Alu.add),
            reads=[st_yr_c[2 * ct], st_yr_c[2 * ct + 1]], writes=[stA])
        vop(lambda v, ct=ct: v.tensor_tensor(
            stA.ap[:, 8 + ct:9 + ct], st_yi.ap[:, 2 * ct:2 * ct + 1],
            st_yi.ap[:, 2 * ct + 1:2 * ct + 2], Alu.add),
            reads=[st_yi_c[2 * ct], st_yi_c[2 * ct + 1]], writes=[stA])
        vop(lambda v, ct=ct: v.tensor_tensor(
            st_g[1].ap[:, ct:ct + 1], st_q2.ap[:, 4 * ct:4 * ct + 1],
            st_q2.ap[:, 4 * ct + 1:4 * ct + 2], Alu.add),
            reads=[st_q2_c[4 * ct], st_q2_c[4 * ct + 1]], writes=[st_g[1]])
        vop(lambda v, ct=ct: v.tensor_tensor(
            st_g[2].ap[:, ct:ct + 1], st_q2.ap[:, 4 * ct + 2:4 * ct + 3],
            st_q2.ap[:, 4 * ct + 3:4 * ct + 4], Alu.add),
            reads=[st_q2_c[4 * ct + 2], st_q2_c[4 * ct + 3]], writes=[st_g[2]])
        vop(lambda v, ct=ct: v.tensor_tensor(
            stA.ap[:, 16 + ct:17 + ct], st_g[1].ap[:, ct:ct + 1],
            st_g[2].ap[:, ct:ct + 1], Alu.add), reads=[st_g[1], st_g[2]], writes=[stA])
    cur = stA
    nxt = 3
    for step in (16, 8, 4, 2, 1):
        mask = [(i + step) % 32 for i in range(32)]
        shb, sm = st_g[nxt], st_g[nxt + 1]
        vop(lambda v, cur=cur, shb=shb, mask=mask: v.stream_shuffle(shb.ap, cur.ap, mask),
            reads=[cur], writes=[shb])
        vop(lambda v, cur=cur, shb=shb, sm=sm: v.tensor_tensor(
            sm.ap, cur.ap, shb.ap, Alu.add), reads=[cur, shb], writes=[sm])
        cur = sm
        nxt += 2
    dma("qd", lambda q, cur=cur: q.dma_start(out=d_ccin.ap, in_=cur.ap),
        reads=[cur], writes=[d_ccin])
    E.op("cc", lambda g: g.collective_compute(
        "AllReduce", Alu.add, replica_groups=[[0, 1, 2, 3], [4, 5, 6, 7]],
        ins=[d_ccin.ap], outs=[d_ccout.ap]), reads=[d_ccin], writes=[d_ccout])

    def g1(tt):
        for c in range(4):
            chunks = []
            for k in range(8):
                wr_i, woff = 2 * k + c // 2, (c % 2) * 512
                chunks.append(lambda t, k=k, tt=tt, ba=pb[c].ap, wr_i=wr_i,
                              woff=woff: t.matmul(
                    ba, big32ap[:, k * T + tt * 128:k * T + (tt + 1) * 128],
                    wreg[wr_i].ap[:, woff:woff + 512], start=(k == 0), stop=False))
            chunks.append(lambda t, c=c, ba=pb[c].ap: t.matmul(
                ba, onesrow.ap, binbf.ap[0:1, c * 512:(c + 1) * 512],
                start=False, stop=True))
            mm_acc(chunks, reads=b32 + [onesrow, binbf] + wreg, writes=[pb[c]])
        for c in range(4):
            vop(lambda v, c=c, aa=pb[c].ap: v.tensor_scalar(
                bc(8)[:, c * 512:(c + 1) * 512], aa, 0.0, None, Alu.is_gt),
                reads=[pb[c]], writes=[sl[8]])
            sop(lambda s, c=c, aa=pb[c].ap: s.activation(
                bc(11)[:, c * 512:(c + 1) * 512], aa, Act.Silu),
                reads=[pb[c]], writes=[sl[11]])

    def g_masks(tt):
        vop(lambda v: v.memset(bc(10), 0.0), writes=[sl[10]])
        dmr = bc(10).rearrange("p (r n) -> p r n", r=PAR)
        der = bc(8).rearrange("p (r n) -> p r n", r=PAR)
        vop(lambda v, dmr=dmr: v.memset(dmr[:, :, 0:1], 1.0), writes=[sl[10]])
        for lv in range(7):
            bp, bchld, npar = 2 ** lv - 1, 2 ** (lv + 1) - 1, 2 ** lv
            vop(lambda v, dmr=dmr, der=der, bp=bp, bchld=bchld, npar=npar:
                v.tensor_mul(
                dmr[:, :, bchld + 1:bchld + 2 * npar:2], dmr[:, :, bp:bp + npar],
                der[:, :, bp:bp + npar]), reads=[sl[10], sl[8]], writes=[sl[10]])
            vop(lambda v, dmr=dmr, bp=bp, bchld=bchld, npar=npar:
                v.tensor_tensor(
                dmr[:, :, bchld:bchld + 2 * npar:2], dmr[:, :, bp:bp + npar],
                dmr[:, :, bchld + 1:bchld + 2 * npar:2], Alu.subtract),
                reads=[sl[10]], writes=[sl[10]])
        vop(lambda v: v.tensor_mul(bc(8), bc(11), bc(10)),
            reads=[sl[11], sl[10], sl[8]], writes=[sl[8]])
        for grp4 in range(4):
            tb_ = pb[6 + grp4 % 2]
            chunks = []
            for j4 in range(4):
                wt = grp4 * 4 + j4
                chunks.append(lambda t, j4=j4, wt=wt, tb_=tb_: t.matmul(
                    tb_.ap[:, j4 * 128:(j4 + 1) * 128],
                    bc(8)[:, wt * 128:(wt + 1) * 128], idn.ap,
                    start=(j4 == 0), stop=(j4 == 3)))
            mm_acc(chunks, reads=[sl[8], idn], writes=[tb_])
            for j4 in range(4):
                wt = grp4 * 4 + j4
                s_i = 12 + wt // 8
                off = (wt % 8) * 256 + (tt % 2) * 128
                sop(lambda s, j4=j4, s_i=s_i, off=off, tb_=tb_: s.copy(
                    bc(s_i)[:, off:off + 128],
                    tb_.ap[:, j4 * 128:(j4 + 1) * 128]),
                    reads=[tb_], writes=[sl[s_i]])

    def mtT(wt):
        s_i = 12 + wt // 8
        off = (wt % 8) * 256
        return bc(s_i)[:, off:off + 256]

    def g2(tt0):
        for sweep in range(2):
            for dt4 in range(4):
                dt = sweep * 4 + dt4
                bank = pb[4 + dt4 // 2]
                coff = (dt4 % 2) * 256
                chunks = []
                for wt in range(16):
                    s_i, woff = wt // 2, (wt % 2) * 1024
                    st_f = (wt == 0 and dt4 % 2 == 0)
                    sp_f = (wt == 15 and dt4 % 2 == 1)
                    chunks.append(lambda t, wt=wt, dt=dt, bank=bank, coff=coff,
                                  s_i=s_i, woff=woff, st_f=st_f, sp_f=sp_f:
                                  t.matmul(
                        bank.ap[:, coff:coff + 256],
                        bc(s_i)[:, woff + dt * 128:woff + (dt + 1) * 128],
                        mtT(wt), start=st_f, stop=sp_f))
                mm_acc(chunks, reads=[sl[12], sl[13]] + woutb, writes=[bank])
            for b_i in range(2):
                vop(lambda v, b_i=b_i: v.tensor_copy(
                    sl[9].ap[:, b_i * 512:(b_i + 1) * 512], pb[4 + b_i].ap),
                    reads=[pb[4 + b_i]], writes=[sl[9]])
            dma("qd", lambda q, tt0=tt0, sweep=sweep: q.dma_start(
                out=d_ffn.ap.rearrange("(a p) t -> p a t", p=128)[
                    :, sweep * 4:(sweep + 1) * 4, tt0 * 128:(tt0 + 2) * 128],
                in_=sl[9].ap.rearrange("p (a t) -> p a t", a=4)),
                reads=[sl[9]], writes=[d_ffn])

    g1(0)
    for tt in range(16):
        g_masks(tt)
        if tt + 1 < 16:
            g1(tt + 1)
        if tt % 2 == 1:
            g2(tt - 1)
    if debug_outputs:
        for k in range(NCT):
            for h in range(2):
                tb = sl[9]
                dma("qd", lambda q, k=k, h=h, tb=tb: q.dma_start(
                    out=tb.ap, in_=d_ffn.ap[k * 128:(k + 1) * 128, h * FD:(h + 1) * FD]),
                    reads=[d_ffn], writes=[tb])
                dma("qd", lambda q, k=k, h=h, tb=tb: q.dma_start(
                    out=dbg["dbg_ffn"].ap[k * 128:(k + 1) * 128, h * FD:(h + 1) * FD],
                    in_=tb.ap), reads=[tb], writes=[dbg["dbg_ffn"]])

    stG = st_g[1]
    dma("qd", lambda q: q.dma_start(out=stG.ap, in_=d_ccout.ap),
        reads=[d_ccout], writes=[stG])
    if debug_outputs:
        dma("qd", lambda q: q.dma_start(out=dbg["dbg_st"].ap, in_=stG.ap),
            reads=[stG], writes=[dbg["dbg_st"]])
    mrg = st_g[2]
    for a, b_ in ((0, 8), (8, 16), (16, 24)):
        vop(lambda v, a=a, b_=b_: v.tensor_scalar(mrg.ap[:, a:b_], stG.ap[:, a:b_],
                                                  1.0 / NG, None, Alu.mult),
            reads=[stG], writes=[mrg])
    sqm = st_g[3]
    vop(lambda v: v.tensor_tensor(sqm.ap[:, 0:8], mrg.ap[:, 0:8], mrg.ap[:, 0:8], Alu.mult),
        reads=[mrg], writes=[sqm])
    vop(lambda v: v.tensor_tensor(sqm.ap[:, 8:16], mrg.ap[:, 8:16], mrg.ap[:, 8:16],
                                  Alu.mult), reads=[mrg], writes=[sqm])
    vop(lambda v: v.tensor_tensor(st_g[4].ap[:, 0:8], mrg.ap[:, 16:24], sqm.ap[:, 0:8],
                                  Alu.subtract), reads=[mrg, sqm], writes=[st_g[4]])
    vop(lambda v: v.tensor_tensor(st_g[5].ap[:, 0:8], st_g[4].ap[:, 0:8], sqm.ap[:, 8:16],
                                  Alu.subtract), reads=[st_g[4], sqm], writes=[st_g[5]])
    sop(lambda s: s.activation(st_g[6].ap[:, 0:8], st_g[5].ap[:, 0:8], Act.Sqrt,
                               bias=cst.ap[:, 1:2]), reads=[st_g[5], cst], writes=[st_g[6]])
    vop(lambda v: v.reciprocal(st_g[7].ap[:, 0:8], st_g[6].ap[:, 0:8]),
        reads=[st_g[6]], writes=[st_g[7]])
    vop(lambda v: v.tensor_tensor(gnA.ap, st_g[7].ap[:, 0:8], v_gns.ap, Alu.mult),
        reads=[st_g[7], v_gns], writes=[gnA])
    vop(lambda v: v.tensor_tensor(st_g[8].ap[:, 0:8], mrg.ap[:, 0:8], gnA.ap, Alu.mult),
        reads=[mrg, gnA], writes=[st_g[8]])
    vop(lambda v: v.tensor_tensor(gnB.ap, v_gnb.ap, st_g[8].ap[:, 0:8], Alu.subtract),
        reads=[v_gnb, st_g[8]], writes=[gnB])

    for k in range(8):
        dma("gd", lambda g, k=k: g.dma_start(
            out=wreg[k].ap, in_=d_Wo.ap[k * 128:(k + 1) * 128, :]),
            reads=[d_Wo], writes=[wreg[k]])
    for ct in range(NCT):
        dma("qd", lambda q, ct=ct: q.dma_start(
            out=bc(13), in_=d_sg.ap[ct * 128:(ct + 1) * 128, :]),
            reads=[d_sg], writes=[sl[13]])
        for h in range(2):
            SS = sl if ct % 2 == 0 else slB
            yn = SS[0] if h == 0 else SS[1]
            yrl = SS[2] if h == 0 else SS[3]
            dma("qd", lambda q, ct=ct, h=h, yrl=yrl: q.dma_start(
                out=yrl.ap, in_=d_yr.ap[ct * 128:(ct + 1) * 128,
                                        h * FD:(h + 1) * FD]),
                reads=[d_yr], writes=[yrl])
            pop(lambda g, ct=ct, h=h, yn=yn, yrl=yrl: g.tensor_scalar(
                yn.ap, yrl.ap,
                gnA.ap[:, ct:ct + 1], gnB.ap[:, ct:ct + 1], Alu.mult, Alu.add),
                reads=[yrl, gnA, gnB], writes=[yn])
            pop(lambda g, ct=ct, h=h, yn=yn: g.tensor_tensor(
                b32[ct].ap[:, h * FD:(h + 1) * FD],
                yn.ap, bc(13)[:, h * FD:(h + 1) * FD], Alu.mult),
                reads=[yn, sl[13]], writes=[b32[ct]])
    if debug_outputs:
        for k in range(NCT):
            for h in range(2):
                tb = sl[0]
                vop(lambda v, k=k, h=h, tb=tb: v.tensor_copy(
                    tb.ap.bitcast(BF16)[:, 0:FD],
                    b32[k].ap[:, h * FD:(h + 1) * FD]),
                    reads=[b32[k]], writes=[tb])
                tb2 = sl[1]
                vop(lambda v, tb=tb, tb2=tb2: v.tensor_copy(
                    tb2.ap, tb.ap.bitcast(BF16)[:, 0:FD]),
                    reads=[tb], writes=[tb2])
                dma("qd", lambda q, k=k, h=h, tb2=tb2: q.dma_start(
                    out=dbg["dbg_z"].ap[k * 128:(k + 1) * 128, h * FD:(h + 1) * FD],
                    in_=tb2.ap), reads=[tb2], writes=[dbg["dbg_z"]])
    for dt in range(8):
        for tc in range(4):
            bank = pb[(dt * 4 + tc) % 8]
            chunks = []
            for k in range(8):
                chunks.append(lambda t, dt=dt, tc=tc, k=k, bank=bank: t.matmul(
                    bank.ap, wreg[k].ap[:, dt * 128:(dt + 1) * 128],
                    big32ap[:, k * T + tc * 512:k * T + (tc + 1) * 512],
                    start=(k == 0), stop=(k == 7)))
            mm_acc(chunks, reads=b32 + wreg[:8], writes=[bank])
            xnc = sl[2 + (dt * 4 + tc) % 2]
            dma("qd", lambda q, dt=dt, tc=tc, xnc=xnc: q.dma_start(
                out=xnc.ap[:, 0:512],
                in_=d_xn.ap[dt * 128:(dt + 1) * 128, tc * 512:(tc + 1) * 512]),
                reads=[d_xn], writes=[xnc])
            vop(lambda v, dt=dt, tc=tc, bank=bank, xnc=xnc: v.tensor_tensor(
                b64[dt].ap[:, tc * 512:(tc + 1) * 512],
                bank.ap, xnc.ap[:, 0:512], Alu.add),
                reads=[bank, xnc], writes=[b64[dt]])
    if debug_outputs:
        for k in range(NCT):
            dma("qd", lambda q, k=k: q.dma_start(
                out=dbg["dbg_mix"].ap[k * 128:(k + 1) * 128, :],
                in_=b64[k].ap), reads=[b64[k]], writes=[dbg["dbg_mix"]])
    mu_h, iv_h = ln_rows(b64, 1)
    for k in range(NCT):
        for h in range(2):
            t0, t1_, xa = sl[0], sl[1], sl[9]
            vop(lambda v, k=k, h=h, t0=t0: v.tensor_tensor(
                t0.ap, b64[k].ap[:, h * FD:(h + 1) * FD],
                mu_h[h].ap, Alu.subtract), reads=[b64[k], mu_h[h]], writes=[t0])
            vop(lambda v, h=h, t0=t0, t1_=t1_: v.tensor_tensor(
                t1_.ap, t0.ap, iv_h[h].ap, Alu.mult),
                reads=[t0, iv_h[h]], writes=[t1_])
            pop(lambda g, k=k, t1_=t1_, xa=xa: g.tensor_scalar(
                xa.ap, t1_.ap, v_l1s.ap[:, k:k + 1], v_l1b.ap[:, k:k + 1],
                Alu.mult, Alu.add), reads=[t1_, v_l1s, v_l1b], writes=[xa])
            sop(lambda s, k=k, h=h, xa=xa: s.copy(
                b32[k].ap[:, h * FD:(h + 1) * FD], xa.ap),
                reads=[xa], writes=[b32[k]])
            dma("qd", lambda q, k=k, h=h, xa=xa: q.dma_start(
                out=d_xat.ap[k * 128:(k + 1) * 128, h * FD:(h + 1) * FD], in_=xa.ap),
                reads=[xa], writes=[d_xat])
    if debug_outputs:
        for k in range(NCT):
            for h in range(2):
                tb = sl[0]
                dma("qd", lambda q, k=k, h=h, tb=tb: q.dma_start(
                    out=tb.ap, in_=d_xat.ap[k * 128:(k + 1) * 128, h * FD:(h + 1) * FD]),
                    reads=[d_xat], writes=[tb])
                dma("qd", lambda q, k=k, h=h, tb=tb: q.dma_start(
                    out=dbg["dbg_xat"].ap[k * 128:(k + 1) * 128, h * FD:(h + 1) * FD],
                    in_=tb.ap), reads=[tb], writes=[dbg["dbg_xat"]])

    def ew(fq, r):
        if fq % 2 == 0:
            s_i = 6 + r // 2
            return sl[s_i], bc(s_i)[:, (r % 2) * 1024:(r % 2) * 1024 + 1024]
        return wreg[r], wreg[r].ap

    hsb = [[Buf(bc(2 + st * 2 + j // 4)[:, (j % 4) * 512:(j % 4) * 512 + 512],
                f"hs{st}_{j}") for j in range(8)] for st in range(2)]

    def hsre(tc, j):
        b = hsb[tc % 2][j]
        return b, b.ap

    def eload(fq):
        for r in range(16):
            wb_, wap = ew(fq, r)
            ft = fq * 8 + (r % 8)
            if r < 8:
                dma("gd", lambda g, ft=ft, wap=wap: g.dma_start(
                    out=wap, in_=d_W1.ap[ft]), reads=[d_W1], writes=[wb_])
            else:
                dma("gd", lambda g, ft=ft, wap=wap: g.dma_start(
                    out=wap, in_=d_W2.ap[ft * 128:(ft + 1) * 128, :]),
                    reads=[d_W2], writes=[wb_])

    eload(0)
    eload(1)
    import os as _os
    EFQ = int(_os.environ.get('KEFQ', '4'))
    for fq in range(EFQ):
        for tc in range(4):
            def w1mm(j, fq=fq, tc=tc):
                hbank = pb[j % 2]
                wb_, wap = ew(fq, j)
                chunks = []
                for k in range(8):
                    chunks.append(lambda t, k=k, hbank=hbank, wap=wap, tc=tc:
                                  t.matmul(
                        hbank.ap, wap[:, k * 128:(k + 1) * 128],
                        big32ap[:, k * T + tc * 512:k * T + (tc + 1) * 512],
                        start=(k == 0), stop=(k == 7)))
                mm_acc(chunks, reads=[wb_] + b32, writes=[hbank])

            def gelu(j, fq=fq, tc=tc):
                hbank = pb[j % 2]
                hb_, hap = hsre(tc, j)
                ft = fq * 8 + j
                sop(lambda s, hap=hap, hbank=hbank, ft=ft: s.activation(
                    hap, hbank.ap, Act.Gelu_apprx_tanh,
                    bias=v_b1.ap[:, ft:ft + 1]),
                    reads=[hbank, v_b1], writes=[hb_])

            def w2mm(j, dts, fq=fq, tc=tc):
                wb_, wap = ew(fq, 8 + j)
                hb_, hap = hsre(tc, j)
                for dt in dts:
                    bank = pb[2 + (dt if dt < 6 else dt - 6)]
                    E.op("t", lambda t, wap=wap, hap=hap, dt=dt, bank=bank, j=j:
                         t.matmul(bank.ap, wap[:, dt * 128:(dt + 1) * 128],
                                  hap, start=(j == 0), stop=(j == 7)),
                         reads=[wb_, hb_],
                         writes=([bank] if (j == 0 or j == 7) else []),
                         wtok_update=(j == 7))

            def evac(dt, bank, fq=fq, tc=tc):
                reg_ap = b64[dt].ap[:, tc * 512:(tc + 1) * 512]
                if fq == 0:
                    sop(lambda s, bank=bank, reg_ap=reg_ap: s.copy(
                        reg_ap, bank.ap), reads=[bank], writes=[b64[dt]])
                else:
                    tslot = sl[dt % 2]
                    toff = ((dt // 2) % 2) * 512
                    tap = tslot.ap[:, toff:toff + 512]
                    vop(lambda v, bank=bank, tap=tap, reg_ap=reg_ap:
                        v.tensor_tensor(tap, bank.ap, reg_ap, Alu.add),
                        reads=[bank, b64[dt]], writes=[tslot])
                    sop(lambda s, tap=tap, reg_ap=reg_ap: s.copy(reg_ap, tap),
                        reads=[tslot], writes=[b64[dt]])

            w1mm(0)
            for j in range(8):
                gelu(j)
                if j + 1 < 8:
                    w1mm(j + 1)
                w2mm(j, range(6))
            evac(0, pb[2])
            evac(1, pb[3])
            for j in range(8):
                w2mm(j, (6, 7))
            for dt in (2, 3, 4, 5):
                evac(dt, pb[2 + dt])
            evac(6, pb[2])
            evac(7, pb[3])
        if fq + 2 < EFQ:
            eload(fq + 2)

    if debug_outputs:
        for k in range(NCT):
            dma("qd", lambda q, k=k: q.dma_start(
                out=dbg["dbg_o2"].ap[k * 128:(k + 1) * 128, :],
                in_=b64[k].ap), reads=[b64[k]], writes=[dbg["dbg_o2"]])

    for k in range(NCT):
        for h in range(2):
            xac, t0 = sl[1], sl[9]
            dma("qd", lambda q, k=k, h=h, xac=xac: q.dma_start(
                out=xac.ap, in_=d_xat.ap[k * 128:(k + 1) * 128, h * FD:(h + 1) * FD]),
                reads=[d_xat], writes=[xac])
            pop(lambda g, k=k, h=h, t0=t0: g.tensor_scalar(
                t0.ap, b64[k].ap[:, h * FD:(h + 1) * FD],
                1.0, v_b2.ap[:, k:k + 1], Alu.mult, Alu.add),
                reads=[b64[k], v_b2], writes=[t0])
            vop(lambda v, k=k, h=h, xac=xac, t0=t0: v.tensor_tensor(
                b64[k].ap[:, h * FD:(h + 1) * FD], t0.ap, xac.ap, Alu.add),
                reads=[t0, xac], writes=[b64[k]])
    mu_h, iv_h = ln_rows(b64, 1)
    out_toks = []
    for k in range(NCT):
        for h in range(2):
            t0, t1_, af = sl[0], sl[1], sl[9]
            vop(lambda v, k=k, h=h, t0=t0: v.tensor_tensor(
                t0.ap, b64[k].ap[:, h * FD:(h + 1) * FD],
                mu_h[h].ap, Alu.subtract), reads=[b64[k], mu_h[h]], writes=[t0])
            vop(lambda v, h=h, t0=t0, t1_=t1_: v.tensor_tensor(
                t1_.ap, t0.ap, iv_h[h].ap, Alu.mult),
                reads=[t0, iv_h[h]], writes=[t1_])
            pop(lambda g, k=k, t1_=t1_, af=af: g.tensor_scalar(
                af.ap, t1_.ap, v_l2s.ap[:, k:k + 1], v_l2b.ap[:, k:k + 1],
                Alu.mult, Alu.add), reads=[t1_, v_l2s, v_l2b], writes=[af])
            fc, xc2 = sl[3], sl[4]
            dma("qd", lambda q, k=k, h=h, fc=fc: q.dma_start(
                out=fc.ap, in_=d_ffn.ap[k * 128:(k + 1) * 128, h * FD:(h + 1) * FD]),
                reads=[d_ffn], writes=[fc])
            dma("qd", lambda q, k=k, h=h, xc2=xc2: q.dma_start(
                out=xc2.ap, in_=d_xT.ap[k * 128:(k + 1) * 128, h * FD:(h + 1) * FD]),
                reads=[d_xT], writes=[xc2])
            t2_ = sl[10]
            vop(lambda v, fc=fc, af=af, t2_=t2_: v.tensor_tensor(
                t2_.ap, af.ap, fc.ap, Alu.add), reads=[af, fc], writes=[t2_])
            ot = sl[11]
            vop(lambda v, xc2=xc2, t2_=t2_, ot=ot: v.tensor_tensor(
                ot.ap, t2_.ap, xc2.ap, Alu.add), reads=[t2_, xc2], writes=[ot])
            tk = dma("qd", lambda q, k=k, h=h, ot=ot: q.dma_start(
                out=d_y.ap[k * 128:(k + 1) * 128, h * FD:(h + 1) * FD], in_=ot.ap),
                reads=[ot], writes=[d_y])
            out_toks.append(tk)
    E.wait_on("q", out_toks)

    sems = {s: ctx.enter_context(nc.semaphore(f"sem_{s}")) for s in SEMS}
    runners = E.play(sems)
    with nc.Block() as blk:
        blk.tensor(runners["t"])
        blk.vector(runners["v"])
        blk.scalar(runners["s"])
        blk.gpsimd(runners["g"])
        blk.sync(runners["q"])
    ctx.close()
    return nc


def _host_inputs(inputs):
    x = np.asarray(inputs["L0_attn_x"], np.float32)
    Wq = np.asarray(inputs["L0_attn_Wq"], np.float32)
    Wk = np.asarray(inputs["L0_attn_Wk"], np.float32)
    Wv = np.asarray(inputs["L0_attn_Wv"], np.float32)
    Wa = np.asarray(inputs["L0_attn_Wa"], np.float32)
    Wg = np.asarray(inputs["L0_attn_Wg"], np.float32)
    Wo = np.asarray(inputs["L0_attn_Wo"], np.float32)
    W1 = np.asarray(inputs["L0_attn_W1"], np.float32)
    W2 = np.asarray(inputs["L0_attn_W2"], np.float32)
    W_in = np.asarray(inputs["L0_ffn_W_in"], np.float32)
    W_out = np.asarray(inputs["L0_ffn_W_out"], np.float32)
    b_in = np.asarray(inputs["L0_ffn_b_in"], np.float32)

    def pan(w):
        K, N = w.shape
        kt, npan = K // 128, N // 128
        return np.ascontiguousarray(
            w.reshape(kt, 128, npan, 128).transpose(2, 1, 0, 3).reshape(npan, 128, kt * 128))

    def vec_pp(v, cols):
        return np.ascontiguousarray(np.asarray(v, np.float32).reshape(cols, 128).T)

    WinT = np.zeros((D, WPAD), np.float32)
    WinT.reshape(D, PAR, 256)[:, :, :255] = W_in.T.reshape(D, PAR, 255)
    WoutT = np.zeros((WPAD, D), np.float32)
    WoutT.reshape(PAR, 256, D)[:, :255, :] = W_out.T.reshape(PAR, 255, D)
    binrow = np.zeros((1, WPAD), np.float32)
    binrow.reshape(PAR, 256)[:, :255] = np.asarray(b_in, np.float32).reshape(PAR, 255)

    shared = {
        "Wq_p": pan(Wq), "Wk_p": pan(Wk), "Wv_p": pan(Wv), "Wg_p": pan(Wg),
        "Wa_p": pan(Wa), "W1_p": pan(W1),
        "Wo_r": np.ascontiguousarray(Wo), "W2_r": np.ascontiguousarray(W2),
        "WinT_r": np.ascontiguousarray(WinT), "WoutT_r": np.ascontiguousarray(WoutT),
        "gn_s": vec_pp(inputs["L0_attn_gn_scale"], 8),
        "gn_b": vec_pp(inputs["L0_attn_gn_bias"], 8),
        "ln1_s": vec_pp(inputs["L0_attn_ln1_scale"], 8),
        "ln1_b": vec_pp(inputs["L0_attn_ln1_bias"], 8),
        "ln2_s": vec_pp(inputs["L0_attn_ln2_scale"], 8),
        "ln2_b": vec_pp(inputs["L0_attn_ln2_bias"], 8),
        "b1_pp": vec_pp(inputs["L0_attn_b1"], 32),
        "b2_pp": vec_pp(inputs["L0_attn_b2"], 8),
        "binrow": binrow,
    }
    in_maps = []
    for c in range(8):
        b, qd = c // 4, c % 4
        m = dict(shared)
        m["xT"] = np.ascontiguousarray(x[b, qd * T:(qd + 1) * T, :].T)
        in_maps.append(m)
    return in_maps


def kernel(**inputs):
    in_maps = _host_inputs(inputs)
    nc = build_program()
    res = run_bass_kernel_spmd(nc, in_maps, core_ids=list(range(8)))
    out = np.empty((2, 8192, D), np.float32)
    for c in range(8):
        b, qd = c // 4, c % 4
        out[b, qd * T:(qd + 1) * T, :] = np.asarray(res.results[c]["y"]).T
    return out
